# version 21
# baseline (speedup 1.0000x reference)
import numpy as np
import ml_dtypes
import concourse.bass as bass
import concourse.mybir as mybir
from concourse.bass_utils import run_bass_kernel_spmd
from contextlib import ExitStack

F32 = mybir.dt.float32
BF16 = mybir.dt.bfloat16
F8 = mybir.dt.float8e4
AF = mybir.ActivationFunctionType
ALU = mybir.AluOpType
AX = mybir.AxisListType.X

T, D, NH, DH, DLAT, DKV, DI = 4096, 1024, 16, 64, 256, 2048, 4096
NCORE = 8
TCH = 256
NCHUNK = T // TCH
TPC = TCH // 128
JB = 512
NJB = DI // JB
C1 = 12582912.0
C2 = 6755399441055744.0

ENG = ("tensor", "vector", "scalar", "gpsimd", "sync")


class B:

    def __init__(self):
        self.items = {e: [] for e in ENG}
        self.nop = {e: 0 for e in ENG}
        self.ndma = {}
        self.writer = {}
        self.readers = {}

    def _need(self, waits, eng, entry, is_dma):
        kind, e2, i = entry
        if (kind == "op" and e2 == eng and eng == "tensor" and not is_dma):
            return
        key = (kind, e2)
        v = (i + 1) if kind == "op" else 16 * (i + 1)
        if waits.get(key, 0) < v:
            waits[key] = v

    def _deps(self, eng, reads, writes, is_dma):
        waits = {}
        for b_ in reads:
            w = self.writer.get(b_)
            if w is not None:
                self._need(waits, eng, w, is_dma)
        for b_ in writes:
            w = self.writer.get(b_)
            if w is not None:
                self._need(waits, eng, w, is_dma)
            for r in self.readers.get(b_, ()):
                self._need(waits, eng, r, is_dma)
        return waits

    def _record(self, entry, reads, writes):
        for b_ in reads:
            self.readers.setdefault(b_, []).append(entry)
        for b_ in writes:
            self.writer[b_] = entry
            self.readers[b_] = []

    def op(self, eng, emit, reads=(), writes=()):
        waits = self._deps(eng, reads, writes, False)
        i = self.nop[eng]
        self.items[eng].append((waits, emit, "op"))
        self.nop[eng] = i + 1
        self._record(("op", eng, i), reads, writes)

    def dma(self, q, out_ap, in_ap, reads=(), writes=(), stream=None):
        stream = stream or (writes[0] if writes else "yout")
        waits = self._deps(q, reads, writes, True)
        i = self.ndma.get(stream, 0)
        self.items[q].append((waits, (out_ap, in_ap, stream), "dma"))
        self.ndma[stream] = i + 1
        self._record(("dma", stream, i), reads, writes)

    def final_wait(self, eng, stream):
        n = self.ndma.get(stream, 0)
        self.items[eng].append(({("dma", stream): 16 * n}, None, "wait_only"))

    def emit(self, block, ctx, nc, sems_op):
        sems_dma = {}
        for s in self.ndma:
            sems_dma[s] = ctx.enter_context(nc.semaphore("sdm_%s" % s))
        handles = {"vector": block.vector, "scalar": block.scalar,
                   "tensor": block.tensor, "gpsimd": block.gpsimd,
                   "sync": block.sync}
        for e in ENG:
            items = self.items[e]
            if not items:
                continue

            def make(eng_name, eng_items):
                def _(eng):
                    hw = {}
                    for waits, payload, kind in eng_items:
                        for key in sorted(waits):
                            v = waits[key]
                            if hw.get(key, 0) >= v:
                                continue
                            k2, e2 = key
                            sem = sems_op[e2] if k2 == "op" else sems_dma[e2]
                            eng.wait_ge(sem, v)
                            hw[key] = v
                        if kind == "op":
                            payload(eng).then_inc(sems_op[eng_name], 1)
                        elif kind == "dma":
                            out_ap, in_ap, stream = payload
                            eng.dma_start(out=out_ap, in_=in_ap).then_inc(
                                sems_dma[stream], 16)
                return _
            handles[e](make(e, items))


def build_nc(n_chunks=NCHUNK, phase=3):
    nc = bass.Bass("TRN2", target_bir_lowering=False, debug=False)

    d_xT = nc.dram_tensor("xT", [D, T], F32, kind="ExternalInput").ap()
    d_xtm = nc.dram_tensor("xtm", [T, D], F32, kind="ExternalInput").ap()
    d_Wq = nc.dram_tensor("Wq", [D, D], F32, kind="ExternalInput").ap()
    d_Wdn = nc.dram_tensor("Wdn", [D, DLAT], F32, kind="ExternalInput").ap()
    d_Wup = nc.dram_tensor("Wup", [DLAT, DKV], F32, kind="ExternalInput").ap()
    d_Wout = nc.dram_tensor("Wout", [D, D], F32, kind="ExternalInput").ap()
    d_bdn = nc.dram_tensor("bdn_cols", [128, 2], F32, kind="ExternalInput").ap()
    d_bout = nc.dram_tensor("bout_row", [1, D], BF16, kind="ExternalInput").ap()
    d_bq_row = nc.dram_tensor("bq_row", [1, D], BF16, kind="ExternalInput").ap()
    d_bup_row = nc.dram_tensor("bup_row", [1, DKV], BF16,
                               kind="ExternalInput").ap()
    d_wgT = nc.dram_tensor("wgT", [D, DI], F32, kind="ExternalInput").ap()
    d_wuT = nc.dram_tensor("wuT", [D, DI], F32, kind="ExternalInput").ap()
    d_wdT = nc.dram_tensor("wdT", [DI, D], F32, kind="ExternalInput").ap()
    d_consts = nc.dram_tensor("consts", [128, 8], F32, kind="ExternalInput").ap()
    d_mU = nc.dram_tensor("mU", [128, 128], BF16, kind="ExternalInput").ap()
    d_mV = nc.dram_tensor("mV", [128, 512], BF16, kind="ExternalInput").ap()
    d_id128 = nc.dram_tensor("id128", [128, 128], BF16, kind="ExternalInput").ap()
    d_id2 = nc.dram_tensor("id2", [128, 64], BF16, kind="ExternalInput").ap()
    d_ones = nc.dram_tensor("ones_row", [1, 256], BF16,
                            kind="ExternalInput").ap()
    d_y = nc.dram_tensor("y", [T, D], F32, kind="ExternalOutput").ap()
    s_wgu = nc.dram_tensor("wgu_s", [128, 65536], F8, kind="Internal").ap()
    s_wdq = nc.dram_tensor("wdq_s", [128, 32768], F8, kind="Internal").ap()

    with ExitStack() as ctx:
        def sb(name, shape, dtype=F32):
            return ctx.enter_context(nc.sbuf_tensor(name, shape, dtype))

        def psum(name, shape, dtype=F32):
            return ctx.enter_context(nc.psum_tensor(name, shape, dtype))

        consts = sb("consts_sb", [128, 8])
        id128 = sb("id128t", [128, 128], BF16)
        id2 = sb("id2t", [128, 64], BF16)
        mU = sb("mUt", [128, 128], BF16)
        mV = sb("mVt", [128, 512], BF16)
        ones_row = sb("ones_rowt", [1, 256], BF16)
        ones1 = sb("ones1t", [1, 128])
        bout_row = sb("bout_rowt", [1, D], BF16)
        bq_row = sb("bq_rowt", [1, D], BF16)
        bup_row = sb("bup_rowt", [1, DKV], BF16)
        bdn_cols = sb("bdn_colst", [128, 2])
        wq_sb = sb("wq_sb", [128, 8192], BF16)
        wdn_sb = sb("wdn_sb", [128, 2048], BF16)
        wup_sb = sb("wup_sb", [128, 4096], BF16)
        wout_sb = sb("wout_sb", [128, 8192], BF16)
        cols = sb("colst", [128, 48])
        rdt = sb("rdt", [128, 16])
        xstg = sb("xstgt", [128, 2048])
        xtb = sb("xtb", [128, 2048], BF16)
        kvlT = sb("kvlT", [128, 512], BF16)
        qT = sb("qTt", [128, TCH * 8], BF16)
        kvT = sb("kvTt", [128, TCH * 16], BF16)
        kdupT = sb("kdupTt", [128, TCH * 16], BF16)
        eM = sb("eMt", [128, 2048], BF16)
        vaug = sb("vaugt", [128, 1024], BF16)
        onecol = sb("onecolt", [128, 1], BF16)
        aoP = eM[:, 1024:2048]
        aoTs = eM[:, 0:1024]
        xq = eM[:, 0:1024]
        xtmt = sb("xtmtt", [128, 1024])
        xtmt2 = sb("xtmt2t", [128, 1024])
        rA = [sb("r1a", [128, 1024]), sb("r1b", [128, 1024])]
        tmp1 = sb("tmp1t", [128, 1024])
        x1ch = sb("x1cht", [128, 2048])
        xqTs = [sb("xqTs0", [128, 1024], BF16), sb("xqTs1", [128, 1024], BF16)]
        sgate = sb("sgatet", [128, JB])
        hch = sb("hcht", [128, 2 * DI])
        hqb2 = [sb("hqb0", [128, 1024], BF16), sb("hqb1", [128, 1024], BF16)]
        hqTs = [sb("hqTst0", [128, DI], BF16), sb("hqTst1", [128, DI], BF16)]
        wgu = [sb("wgu%d" % i, [128, 4096], F8) for i in range(4)]
        wdt = [sb("wdt%d" % i, [128, 2048], F8) for i in range(5)]
        bns = sb("bnst", [128, 12])
        bnag = sb("bnagt", [128, 2])

        cC2 = consts[:, 0:1]
        cC1 = consts[:, 1:2]
        cONE = consts[:, 2:3]
        cEPS = consts[:, 3:4]
        AG, AU, AD = 5, 6, 7

        def col(i):
            return cols[:, i:i + 1]
        (IND, INDC, WSC, AG127, ACOMB, MU, VAR_, LCOL, RSTD, NEGMUR, TMPC,
         MX, MXC, SX, S127, SXI, MXR, MX1C0, MX1C1, SG0, SG1,
         MH, MHC, SH127, SHI, YSC0, YSC1, SCR) = range(28)
        HP0 = 32

        psc0 = psum("psc0", [128, 512])
        psc1 = psum("psc1", [128, 512])
        pbt = psum("pbt", [128, 1024], BF16)
        avp = psum("avp", [128, 1024])
        avd = psum("avd", [128, 512])
        pp0 = psum("pp0", [128, 512])
        pp1 = psum("pp1", [128, 512])

        sems_op = {e: ctx.enter_context(nc.semaphore("so_" + e)) for e in ENG}
        block = ctx.enter_context(nc.Block())
        b = B()

        def emit_proj(ch, skip_load=False, load_q="sync"):
            t0 = ch * TCH
            if not skip_load:
                b.dma(load_q, xstg[:].rearrange("p (ic t) -> p ic t", t=TCH),
                      d_xT.rearrange("(ic p) t -> p ic t", p=128)[:, :,
                                                                 t0:t0 + TCH],
                      writes=["xstg"])
            b.op("vector", lambda v: v.tensor_copy(xtb[:], xstg[:]),
                 reads=["xstg"], writes=["xtb"])

            for lc in range(2):
                ps, pn = ([pp0, "pp0"], [pp1, "pp1"])[lc]
                for ic in range(8):
                    b.op("tensor",
                         lambda t, ps=ps, ic=ic, lc=lc: t.matmul(
                             ps[:, 0:TCH],
                             wdn_sb[:, ic * 256 + lc * 128:
                                    ic * 256 + lc * 128 + 128],
                             xtb[:, ic * TCH:(ic + 1) * TCH],
                             start=(ic == 0), stop=(ic == 7)),
                         reads=["wdn_sb", "xtb"], writes=[pn])
                b.op("scalar",
                     lambda s, ps=ps, lc=lc: s.activation(
                         kvlT[:, lc * TCH:(lc + 1) * TCH], ps[:, 0:TCH],
                         AF.Identity, bias=bdn_cols[:, lc:lc + 1], scale=1.0),
                     reads=[pn, "bdn"], writes=["kvlT"])

            for quad in range(2):
                for oc4 in range(4):
                    oc = quad * 4 + oc4
                    for ic in range(8):
                        b.op("tensor",
                             lambda t, ic=ic, oc=oc, oc4=oc4: t.matmul(
                                 avp[:, oc4 * 256: oc4 * 256 + TCH],
                                 wq_sb[:, ic * 1024 + oc * 128:
                                       ic * 1024 + oc * 128 + 128],
                                 xtb[:, ic * TCH:(ic + 1) * TCH],
                                 start=(ic == 0), stop=False),
                             reads=["wq_sb", "xtb"], writes=["avp"])
                    b.op("tensor",
                         lambda t, oc=oc, oc4=oc4: t.matmul(
                             avp[:, oc4 * 256: oc4 * 256 + TCH],
                             bq_row[0:1, oc * 128:(oc + 1) * 128],
                             ones_row[0:1, 0:TCH],
                             start=False, stop=True),
                         reads=["bq_row", "ones_row"], writes=["avp"])
                qdst = qT[:].rearrange("p (t c) -> p t c", c=8)[
                    :, :, quad * 4: quad * 4 + 4]
                qsrc = avp[:].rearrange("p (c t) -> p c t", t=256) \
                    .transpose([0, 2, 1])
                b.op("scalar",
                     lambda s, qdst=qdst, qsrc=qsrc: s.activation(
                         qdst, qsrc, AF.Identity, bias=0.0, scale=1.0),
                     reads=["avp"], writes=["qT"])

            for hq4 in range(4):
                for hh in range(4):
                    h = hq4 * 4 + hh
                    psA = [pp0, pp1][hh // 2]
                    pnA = ["pp0", "pp1"][hh // 2]
                    psB = [psc0, psc1][hh // 2]
                    pnB = ["psc0", "psc1"][hh // 2]
                    co = (hh % 2) * 256
                    for lc in range(2):
                        b.op("tensor",
                             lambda t, h=h, lc=lc, psA=psA, co=co: t.matmul(
                                 psA[:, co:co + TCH],
                                 wup_sb[:, lc * 2048 + h * 128:
                                        lc * 2048 + h * 128 + 128],
                                 kvlT[:, lc * TCH:(lc + 1) * TCH],
                                 start=(lc == 0), stop=False),
                             reads=["wup_sb", "kvlT"], writes=[pnA])
                    b.op("tensor",
                         lambda t, h=h, psA=psA, co=co: t.matmul(
                             psA[:, co:co + TCH],
                             bup_row[0:1, h * 128:(h + 1) * 128],
                             ones_row[0:1, 0:TCH],
                             start=False, stop=True),
                         reads=["bup_row", "ones_row"], writes=[pnA])
                    for lc in range(2):
                        b.op("tensor",
                             lambda t, h=h, lc=lc, psB=psB, co=co: t.matmul(
                                 psB[64:128, co:co + TCH],
                                 wup_sb[:, lc * 2048 + h * 128:
                                        lc * 2048 + h * 128 + 64],
                                 kvlT[:, lc * TCH:(lc + 1) * TCH],
                                 start=(lc == 0), stop=False),
                             reads=["wup_sb", "kvlT"], writes=[pnB])
                    b.op("tensor",
                         lambda t, h=h, psB=psB, co=co: t.matmul(
                             psB[64:128, co:co + TCH],
                             bup_row[0:1, h * 128: h * 128 + 64],
                             ones_row[0:1, 0:TCH],
                             start=False, stop=True),
                         reads=["bup_row", "ones_row"], writes=[pnB])
                for bk in range(2):
                    psA = [pp0, pp1][bk]
                    pnA = ["pp0", "pp1"][bk]
                    psB = [psc0, psc1][bk]
                    pnB = ["psc0", "psc1"][bk]
                    g0 = hq4 * 4 + bk * 2
                    kvdst = kvT[:].rearrange("p (t g) -> p t g", g=16)[
                        :, :, g0:g0 + 2]
                    src_a = psA[:].rearrange("p (g t) -> p g t", t=256) \
                        .transpose([0, 2, 1])
                    b.op("scalar",
                         lambda s, kvdst=kvdst, src_a=src_a: s.activation(
                             kvdst, src_a, AF.Identity, bias=0.0, scale=1.0),
                         reads=[pnA], writes=["kvT"])
                    kddst = kdupT[64:128, :].rearrange(
                        "p (t g) -> p t g", g=16)[:, :, g0:g0 + 2]
                    src_b = psB[64:128, :].rearrange(
                        "p (g t) -> p g t", t=256).transpose([0, 2, 1])
                    b.op("scalar",
                         lambda s, kddst=kddst, src_b=src_b: s.activation(
                             kddst, src_b, AF.Identity, bias=0.0, scale=1.0),
                         reads=[pnB], writes=["kdupT"])

        def emit_attnA_qk(ch, w, part=None):
            tok0 = w * 128
            for b8 in range(2):
                if part is not None and part != b8:
                    continue
                for i8 in range(8):
                    s_ = b8 * 8 + i8
                    S16 = (tok0 + 8 * s_) * 16
                    S8 = (tok0 + 8 * s_) * 8
                    b.op("tensor",
                         lambda t, i8=i8, S16=S16, S8=S8: t.matmul(
                             psc0[:, i8 * 64:(i8 + 1) * 64],
                             kvT[0:64, S16:S16 + 128],
                             qT[0:64, S8:S8 + 64],
                             start=(i8 == 0), stop=False),
                         reads=["kvT", "qT"], writes=["psc0"])
                    b.op("tensor",
                         lambda t, i8=i8, S16=S16, S8=S8: t.matmul(
                             psc1[:, i8 * 64:(i8 + 1) * 64],
                             kdupT[64:128, S16:S16 + 128],
                             qT[64:128, S8:S8 + 64],
                             start=(i8 == 0), stop=False),
                         reads=["kdupT", "qT"], writes=["psc1"])
                b.op("tensor",
                     lambda t: t.matmul(psc0[:], mU[0:64, :], mV[0:64, :],
                                        start=False, stop=True),
                     reads=["mU", "mV"], writes=["psc0"])
                b.op("tensor",
                     lambda t: t.matmul(psc1[:], mU[64:128, :],
                                        mV[64:128, :],
                                        start=False, stop=True),
                     reads=["mU", "mV"], writes=["psc1"])
                eMb = eM[:, b8 * 1024:(b8 + 1) * 1024].rearrange(
                    "p (s c) -> p s c", c=128)
                b.op("scalar",
                     lambda s, eMb=eMb: s.activation(
                         eMb[:, :, 0:64], psc0[:], AF.Exp, bias=0.0,
                         scale=0.125),
                     reads=["psc0"], writes=["eM"])
                b.op("scalar",
                     lambda s, eMb=eMb: s.activation(
                         eMb[:, :, 64:128], psc1[:], AF.Exp, bias=0.0,
                         scale=0.125),
                     reads=["psc1"], writes=["eM"])

        def emit_attnA(ch, w, skip_xtmt=False, skip_qk=False, xtmt_q="sync"):
            tglob = ch * TPC + w
            tok0 = w * 128
            xtb_, xtn_ = [(xtmt, "xtmt"), (xtmt2, "xtmt2")][tglob % 2]
            if not skip_xtmt:
                b.dma(xtmt_q, xtb_[:],
                      d_xtm[tglob * 128:(tglob + 1) * 128, :],
                      writes=[xtn_])
            if not skip_qk:
                emit_attnA_qk(ch, w)

            for s_ in range(16):
                S16 = (tok0 + 8 * s_) * 16
                b.op("tensor",
                     lambda t, s_=s_, S16=S16: t.transpose(
                         pbt[:, s_ * 64:(s_ + 1) * 64],
                         kvT[64:128, S16:S16 + 128],
                         id2[64:128, 0:64]),
                     reads=["kvT", "id2"], writes=["pbt"])
            b.op("scalar", lambda sc: sc.activation(
                vaug[:], pbt[:], AF.Identity, bias=0.0, scale=1.0),
                 reads=["pbt"], writes=["vaug"])

            for s_ in range(16):
                b.op("tensor",
                     lambda t, s_=s_: t.matmul(
                         avp[:, s_ * 64:(s_ + 1) * 64],
                         eM[:, s_ * 128:(s_ + 1) * 128],
                         vaug[:, s_ * 64:(s_ + 1) * 64],
                         start=True, stop=True),
                     reads=["eM", "vaug"], writes=["avp"])
                b.op("tensor",
                     lambda t, s_=s_: t.matmul(
                         avd[:, s_:s_ + 1],
                         eM[:, s_ * 128:(s_ + 1) * 128],
                         onecol[:],
                         start=True, stop=True),
                     reads=["eM", "onecol"], writes=["avd"])
            b.op("vector", lambda v: v.reciprocal(rdt[:], avd[:, 0:16]),
                 reads=["avd"], writes=["rdt"])
            b.op("vector", lambda v: v.tensor_mul(
                aoP.rearrange("p (s j) -> p s j", j=64),
                avp[:].rearrange("p (s j) -> p s j", j=64),
                rdt[:].unsqueeze(2).broadcast_to([128, 16, 64])),
                 reads=["avp", "rdt", "eM"], writes=["eM"])

            for p_ in (1, 0):
                for s_ in range(16):
                    b.op("tensor",
                         lambda t, p_=p_, s_=s_: t.transpose(
                             pbt[p_ * 64:(p_ + 1) * 64,
                                 s_ * 64:(s_ + 1) * 64],
                             aoP[p_ * 64:(p_ + 1) * 64,
                                 s_ * 64:(s_ + 1) * 64],
                             id2[p_ * 64:(p_ + 1) * 64, 0:64]),
                         reads=["eM", "id2"], writes=["pbt"])
                b.op("scalar",
                     lambda sc, p_=p_: sc.activation(
                         aoTs[p_ * 64:(p_ + 1) * 64, :],
                         pbt[p_ * 64:(p_ + 1) * 64, :],
                         AF.Identity, bias=0.0, scale=1.0),
                     reads=["pbt"], writes=["eM"])

            aoTr = aoTs.rearrange("p (t c) -> p t c", c=8)
            for cs in range(8):
                for half in range(2):
                    ps, pn = ([pp0, "pp0"], [pp1, "pp1"])[half]
                    b.op("tensor",
                         lambda t, ps=ps, cs=cs, half=half: t.matmul(
                             ps[:],
                             aoTr[:, :, cs],
                             wout_sb[:, cs * 1024 + half * 512:
                                     cs * 1024 + half * 512 + 512],
                             start=(cs == 0), stop=False),
                         reads=["eM", "wout_sb"], writes=[pn])
            for half in range(2):
                ps, pn = ([pp0, "pp0"], [pp1, "pp1"])[half]
                b.op("tensor",
                     lambda t, ps=ps, half=half: t.matmul(
                         ps[:], ones_row[0:1, 0:128],
                         bout_row[0:1, half * 512:(half + 1) * 512],
                         start=False, stop=True),
                     reads=["ones_row", "bout_row"], writes=[pn])
            for half in range(2):
                ps, pn = ([pp0, "pp0"], [pp1, "pp1"])[half]
                b.op("vector",
                     lambda v, ps=ps, half=half, w=w: v.tensor_add(
                         rA[w][:, half * 512:(half + 1) * 512], ps[:],
                         xtb_[:, half * 512:(half + 1) * 512]),
                     reads=[pn, xtn_], writes=["rA%d" % w])

        def emit_tileB_xpose(w):
            for ic in range(8):
                b.op("tensor",
                     lambda t, ic=ic: t.transpose(
                         pbt[:, ic * 128:(ic + 1) * 128],
                         xq[:, ic * 128:(ic + 1) * 128], id128[:]),
                     reads=["eM", "id128"], writes=["pbt"])
            b.op("scalar",
                 lambda sc, w=w: sc.activation(
                     xqTs[w][:], pbt[:], AF.Identity, bias=0.0, scale=1.0),
                 reads=["pbt"], writes=["xqT%d" % w])

        def emit_tileB(ch, w):
            rw = rA[w]
            rn = "rA%d" % w
            b.op("vector", lambda v: v.bn_stats(bns[:, 0:6], rw[:, 0:512]),
                 reads=[rn], writes=["bns"])
            b.op("vector", lambda v: v.bn_stats(bns[:, 6:12], rw[:, 512:1024]),
                 reads=[rn], writes=["bns"])
            b.op("vector", lambda v: v.bn_aggr(bnag[:], bns[:]),
                 reads=["bns"], writes=["bnag"])
            b.op("scalar", lambda s: s.activation(col(LCOL), bnag[:, 1:2],
                                                  AF.Ln, bias=cEPS, scale=1.0),
                 reads=["bnag", "consts"], writes=["cols"])
            b.op("scalar", lambda s: s.activation(col(RSTD), col(LCOL),
                                                  AF.Exp, bias=0.0, scale=-0.5),
                 reads=["cols"], writes=["cols"])
            b.op("vector", lambda v: v.tensor_mul(col(TMPC), bnag[:, 0:1],
                                                  col(RSTD)),
                 reads=["bnag", "cols"], writes=["cols"])
            b.op("vector", lambda v: v.tensor_scalar(col(NEGMUR), col(TMPC),
                                                     -1.0, None, ALU.mult),
                 reads=["cols"], writes=["cols"])
            b.op("scalar",
                 lambda s, w=w: s.activation(
                     x1ch[:, w * 1024:(w + 1) * 1024], rw[:], AF.Identity,
                     bias=col(NEGMUR), scale=col(RSTD)),
                 reads=[rn, "cols"], writes=["x1ch"])
            b.op("vector", lambda v, rw=rw: v.tensor_scalar(
                rw[:], rw[:], bnag[:, 0:1], None, ALU.subtract),
                 reads=[rn, "bnag"], writes=[rn])
            b.op("vector", lambda v, rw=rw: v.tensor_reduce(
                col(MX), rw[:], AX, ALU.max, apply_absolute_value=True),
                 reads=[rn], writes=["cols"])
            b.op("vector", lambda v: v.tensor_scalar(col(MXC), col(MX), 1e-5,
                                                     None, ALU.max),
                 reads=["cols"], writes=["cols"])
            b.op("vector", lambda v: v.reciprocal(col(SX), col(MXC)),
                 reads=["cols"], writes=["cols"])
            b.op("vector", lambda v: v.tensor_scalar(col(S127), col(SX), 127.0,
                                                     None, ALU.mult),
                 reads=["cols"], writes=["cols"])
            b.op("vector", lambda v: v.tensor_mul(col(SXI), col(S127),
                                                  col(INDC)),
                 reads=["cols"], writes=["cols"])
            b.op("scalar", lambda s, rw=rw: s.activation(
                tmp1[:], rw[:], AF.Identity, bias=cC2, scale=col(S127)),
                 reads=[rn, "cols", "consts"], writes=["tmp1"])
            b.op("scalar", lambda s, rw=rw: s.activation(
                xtmt[:], rw[:], AF.Identity, bias=cC1, scale=col(SXI)),
                 reads=[rn, "cols", "consts"], writes=["xtmt"])
            b.op("vector", lambda v: v.tensor_scalar(
                xtmt[:], xtmt[:], -C1, None, ALU.add),
                 reads=["xtmt"], writes=["xtmt"])
            b.op("vector", lambda v: v.scalar_tensor_tensor(
                xq, tmp1[:], -C2, xtmt[:], ALU.add, ALU.add),
                 reads=["tmp1", "xtmt"], writes=["eM"])
            b.op("vector", lambda v: v.tensor_mul(col(MXR), col(MX),
                                                  col(RSTD)),
                 reads=["cols"], writes=["cols"])
            b.op("vector",
                 lambda v, w=w: v.tensor_scalar(
                     col([MX1C0, MX1C1][w]), col(MXR), 1e-5, None, ALU.max),
                 reads=["cols"], writes=["cols"])
            b.op("vector",
                 lambda v, w=w: v.tensor_mul(
                     col([SG0, SG1][w]), col([MX1C0, MX1C1][w]), col(AG127)),
                 reads=["cols"], writes=["cols"])

        gbanks = [(pp0[:], "pp0"), (avp[:, 0:512], "avp"), (psc0[:], "psc0")]
        ubanks = [(pp1[:], "pp1"), (avp[:, 512:1024], "avp"), (psc1[:], "psc1")]

        def emit_ffn(ch, after_first_w0=None, qk_next=None):
            for jb in range(NJB):
                for icH in range(2):
                    u4 = (jb * 2 + icH) % 4
                    b.dma(["sync", "scalar"][icH], wgu[u4][:],
                          s_wgu[:, jb * 8192 + icH * 4096:
                                jb * 8192 + (icH + 1) * 4096],
                          reads=["wgu_sA" if jb < 4 else "wgu_sB"],
                          writes=["wgu%d" % u4])
                for w in range(TPC):
                    k = jb * 2 + w
                    gb, gn = gbanks[k % 3]
                    ub, un = ubanks[k % 3]
                    for ic in range(8):
                        icH, icl = ic // 4, ic % 4
                        u4 = (jb * 2 + icH) % 4
                        b.op("tensor",
                             lambda t, w=w, ic=ic, u4=u4, icl=icl,
                             gb=gb: t.matmul(
                                 gb,
                                 xqTs[w][:, ic * 128:(ic + 1) * 128],
                                 wgu[u4][:, icl * 1024:icl * 1024 + 512],
                                 start=(ic == 0), stop=(ic == 7)),
                             reads=["xqT%d" % w, "wgu%d" % u4],
                             writes=[gn])
                        b.op("tensor",
                             lambda t, w=w, ic=ic, u4=u4, icl=icl,
                             ub=ub: t.matmul(
                                 ub,
                                 xqTs[w][:, ic * 128:(ic + 1) * 128],
                                 wgu[u4][:, icl * 1024 + 512:
                                          (icl + 1) * 1024],
                                 start=(ic == 0), stop=(ic == 7)),
                             reads=["xqT%d" % w, "wgu%d" % u4],
                             writes=[un])
                    b.op("scalar",
                         lambda s, w=w, gb=gb: s.activation(
                             sgate[:], gb, AF.Silu, bias=0.0,
                             scale=col([SG0, SG1][w])),
                         reads=[gn, "cols"], writes=["sgate"])
                    hreg = "hch%d" % ((w * DI + jb * JB) // 2048)
                    b.op("vector",
                         lambda v, w=w, jb=jb, ub=ub: v.tensor_mul(
                             hch[:, w * DI + jb * JB: w * DI + (jb + 1) * JB],
                             sgate[:], ub),
                         reads=["sgate", un], writes=[hreg])
                    b.op("vector",
                         lambda v, w=w, jb=jb: v.tensor_reduce(
                             col(HP0 + w * 8 + jb),
                             hch[:, w * DI + jb * JB: w * DI + (jb + 1) * JB],
                             AX, ALU.max, apply_absolute_value=True),
                         reads=[hreg], writes=["cols"])
                    if jb == 0 and w == 0 and after_first_w0 is not None:
                        after_first_w0()

            def hquant_cols(w):
                b.op("vector",
                     lambda v, w=w: v.tensor_reduce(
                         col(MH), cols[:, HP0 + w * 8:HP0 + w * 8 + 8], AX,
                         ALU.max),
                     reads=["cols"], writes=["cols"])
                b.op("vector", lambda v: v.tensor_scalar(col(MHC), col(MH),
                                                         1e-5, None, ALU.max),
                     reads=["cols"], writes=["cols"])
                b.op("vector", lambda v: v.reciprocal(col(SCR), col(MHC)),
                     reads=["cols"], writes=["cols"])
                b.op("vector", lambda v: v.tensor_scalar(col(SH127), col(SCR),
                                                         127.0, None,
                                                         ALU.mult),
                     reads=["cols"], writes=["cols"])
                b.op("vector", lambda v: v.tensor_mul(col(SHI), col(SH127),
                                                      col(INDC)),
                     reads=["cols"], writes=["cols"])

            def hquant_qb(w, qb):
                hr = ["hch%d" % (w * 2), "hch%d" % (w * 2 + 1)]
                hsl = hch[:, w * DI + qb * 1024: w * DI + (qb + 1) * 1024]
                hb, hn = hqb2[qb % 2], "hqb%d" % (qb % 2)
                tt, tn = [(tmp1, "tmp1"), (xtmt, "xtmt")][qb % 2]
                tm2, tn2 = [(rA[0], "rA0"), (rA[1], "rA1")][qb % 2]
                b.op("scalar",
                     lambda s, hsl=hsl, tt=tt: s.activation(
                         tt[:], hsl, AF.Identity, bias=cC2,
                         scale=col(SH127)),
                     reads=hr + ["cols", "consts"], writes=[tn])
                b.op("scalar",
                     lambda s, hsl=hsl, tm2=tm2: s.activation(
                         tm2[:], hsl, AF.Identity, bias=cC1,
                         scale=col(SHI)),
                     reads=hr + ["cols", "consts"], writes=[tn2])
                b.op("vector", lambda v, tm2=tm2: v.tensor_scalar(
                    tm2[:], tm2[:], -C1, None, ALU.add),
                     reads=[tn2], writes=[tn2])
                b.op("vector",
                     lambda v, hb=hb, tt=tt,
                     tm2=tm2: v.scalar_tensor_tensor(
                         hb[:], tt[:], -C2,
                         tm2[:], ALU.add, ALU.add),
                     reads=[tn, tn2], writes=[hn])
                for jj in range(8):
                    b.op("tensor",
                         lambda t, hb=hb, jj=jj: t.transpose(
                             pbt[:, jj * 128:(jj + 1) * 128],
                             hb[:, jj * 128:(jj + 1) * 128], id128[:]),
                         reads=[hn, "id128"], writes=["pbt"])
                b.op("scalar",
                     lambda sc, qb=qb, w=w: sc.activation(
                         hqTs[w][:, qb * 1024:(qb + 1) * 1024],
                         pbt[:], AF.Identity, bias=0.0, scale=1.0),
                     reads=["pbt"], writes=["hqT%d" % w])

            def hquant_ysc(w):
                b.op("vector",
                     lambda v, w=w: v.tensor_mul(
                         col([YSC0, YSC1][w]), col(MHC),
                         col([MX1C0, MX1C1][w])),
                     reads=["cols"], writes=["cols"])
                b.op("vector",
                     lambda v, w=w: v.tensor_mul(
                         col([YSC0, YSC1][w]), col([YSC0, YSC1][w]),
                         col(ACOMB)),
                     reads=["cols"], writes=["cols"])

            for w in range(TPC):
                hquant_cols(w)
                for qb in range(4):
                    hquant_qb(w, qb)
                hquant_ysc(w)

            y2s = [[avp[:, 0:512], avp[:, 512:1024]], [pp0[:], pp1[:]]]
            y2ns = [["avp", "avp"], ["pp0", "pp1"]]
            for jb2 in range(16):
                wd_t, wd_n = wdt[jb2 % 5], "wdt%d" % (jb2 % 5)
                b.dma("gpsimd", wd_t[:],
                      s_wdq[:, jb2 * 2048:(jb2 + 1) * 2048],
                      reads=["wdq_s"], writes=[wd_n])
                for w in range(TPC):
                    for jc in range(2):
                        jg = jb2 * 2 + jc
                        for half in range(2):
                            b.op("tensor",
                                 lambda t, w=w, jg=jg, jc=jc, half=half,
                                 wd_t=wd_t: t.matmul(
                                     y2s[w][half],
                                     hqTs[w][:, jg * 128:(jg + 1) * 128],
                                     wd_t[:, jc * 1024 + half * 512:
                                          jc * 1024 + half * 512 + 512],
                                     start=(jg == 0), stop=(jg == 31)),
                                 reads=["hqT%d" % w, wd_n],
                                 writes=[y2ns[w][half]])
            for w in range(TPC):
                rw = rA[w]
                rn = "rA%d" % w
                for half in range(2):
                    b.op("vector",
                         lambda v, w=w, half=half, rw=rw: v.scalar_tensor_tensor(
                             rw[:, half * 512:(half + 1) * 512], y2s[w][half],
                             col([YSC0, YSC1][w]),
                             x1ch[:, w * 1024 + half * 512:
                                  w * 1024 + half * 512 + 512],
                             ALU.mult, ALU.add),
                         reads=[y2ns[w][half], "x1ch", "cols"], writes=[rn])
                b.op("vector", lambda v, rw=rw: v.bn_stats(bns[:, 0:6],
                                                           rw[:, 0:512]),
                     reads=[rn], writes=["bns"])
                b.op("vector", lambda v, rw=rw: v.bn_stats(bns[:, 6:12],
                                                           rw[:, 512:1024]),
                     reads=[rn], writes=["bns"])
                b.op("vector", lambda v: v.bn_aggr(bnag[:], bns[:]),
                     reads=["bns"], writes=["bnag"])
                b.op("scalar", lambda s: s.activation(col(LCOL), bnag[:, 1:2],
                                                      AF.Ln, bias=cEPS,
                                                      scale=1.0),
                     reads=["bnag", "consts"], writes=["cols"])
                b.op("scalar", lambda s: s.activation(col(RSTD), col(LCOL),
                                                      AF.Exp, bias=0.0,
                                                      scale=-0.5),
                     reads=["cols"], writes=["cols"])
                b.op("vector", lambda v: v.tensor_mul(col(TMPC), bnag[:, 0:1],
                                                      col(RSTD)),
                     reads=["bnag", "cols"], writes=["cols"])
                b.op("vector", lambda v: v.tensor_scalar(col(NEGMUR),
                                                         col(TMPC),
                                                         -1.0, None, ALU.mult),
                     reads=["cols"], writes=["cols"])
                b.op("scalar",
                     lambda s, rw=rw: s.activation(
                         tmp1[:], rw[:], AF.Identity,
                         bias=col(NEGMUR), scale=col(RSTD)),
                     reads=[rn, "cols"], writes=["tmp1"])
                tglob = ch * TPC + w
                b.dma("sync", d_y[tglob * 128:(tglob + 1) * 128, :], tmp1[:],
                      reads=["tmp1"])

        b.dma("sync", consts[:], d_consts, writes=["consts"])
        b.dma("sync", id128[:], d_id128, writes=["id128"])
        b.dma("sync", id2[:], d_id2, writes=["id2"])
        b.dma("sync", mU[:], d_mU, writes=["mU"])
        b.dma("sync", mV[:], d_mV, writes=["mV"])
        b.dma("sync", ones_row[:], d_ones, writes=["ones_row"])
        b.dma("sync", bout_row[:], d_bout, writes=["bout_row"])
        b.dma("sync", bq_row[:], d_bq_row, writes=["bq_row"])
        b.dma("sync", bup_row[:], d_bup_row, writes=["bup_row"])
        b.dma("sync", bdn_cols[:], d_bdn, writes=["bdn"])

        b.op("vector", lambda v: v.memset(col(SCR), 1.0), writes=["cols"])
        b.op("vector", lambda v: v.tensor_scalar(col(IND), col(SCR), C2, -C2,
                                                 ALU.add, ALU.add),
             reads=["cols"], writes=["cols"])
        b.op("vector", lambda v: v.tensor_scalar(col(INDC), col(IND), -1.0, 1.0,
                                                 ALU.mult, ALU.add),
             reads=["cols"], writes=["cols"])
        b.op("vector", lambda v: v.memset(onecol[:], 1.0), writes=["onecol"])
        b.op("vector", lambda v: v.memset(ones1[:], 1.0), writes=["ones1"])

        b.dma("sync", xstg[:].rearrange("p (ic t) -> p ic t", t=TCH),
              d_xT.rearrange("(ic p) t -> p ic t", p=128)[:, :, 0:TCH],
              writes=["xstg"])
        b.dma("sync", xtmt[:], d_xtm[0:128, :], writes=["xtmt"])
        b.dma("sync", xtmt2[:], d_xtm[128:256, :],
              writes=["xtmt2"])

        def load_cast(qq, dst_sb, dst_name, dram, n_iter, width, stgs):
            rr = dram.rearrange("(i p) o -> p i o", p=128)
            for i in range(n_iter):
                stg, sn = stgs[i % len(stgs)]
                b.dma(qq, stg[:, 0:width], rr[:, i, :], writes=[sn])
                b.op("vector",
                     lambda v, stg=stg, i=i: v.tensor_copy(
                         dst_sb[:, i * width:(i + 1) * width],
                         stg[:, 0:width]),
                     reads=[sn], writes=[dst_name])

        stg_h01 = [(hch[:, 0:2048], "hch0"), (hch[:, 2048:4096], "hch1")]
        stg_x1h = [(x1ch[:, 0:1024], "x1ch"), (x1ch[:, 1024:2048], "x1ch")]
        stg_rA = [(rA[0][:], "rA0"), (rA[1][:], "rA1")]
        load_cast("sync", wq_sb, "wq_sb", d_Wq, 8, 1024, stg_h01)
        load_cast("sync", wdn_sb, "wdn_sb", d_Wdn, 8, 256, stg_h01)
        load_cast("scalar", wout_sb, "wout_sb", d_Wout, 8, 1024, stg_rA)
        rru2 = d_Wup.rearrange("(i p) o -> p i o", p=128)
        for i2 in range(2):
            for hh in range(2):
                stg, sn = stg_rA[hh]
                b.dma("scalar", stg[:, 0:1024],
                      rru2[:, i2, hh * 1024:(hh + 1) * 1024], writes=[sn])
                b.op("vector",
                     lambda v, stg=stg, i2=i2, hh=hh: v.tensor_copy(
                         wup_sb[:, i2 * 2048 + hh * 1024:
                                i2 * 2048 + (hh + 1) * 1024], stg[:, 0:1024]),
                     reads=[sn], writes=["wup_sb"])

        for ac in (MX, SCR, MXR):
            b.op("vector", lambda v, ac=ac: v.memset(col(ac), 0.0),
                 writes=["cols"])

        def p1_unit(qq, src, stg, sns, tc, ac):
            b.dma(qq, stg, src, writes=sns)
            b.op("vector",
                 lambda v: v.tensor_reduce(col(tc), stg, AX, ALU.add,
                                           apply_absolute_value=True),
                 reads=sns, writes=["cols"])
            b.op("vector",
                 lambda v: v.tensor_add(col(ac), col(ac), col(tc)),
                 reads=["cols"], writes=["cols"])

        NEL = float(D * DI)
        rrg1 = d_wgT.rearrange("(l p g) c -> p l (g c)", p=128, g=2)
        rrg = d_wgT.rearrange("(rb p) c -> p rb c", p=128)
        rru = d_wuT.rearrange("(rb p) c -> p rb c", p=128)
        rrd = d_wdT.rearrange("(rb p) c -> p rb c", p=128)
        wg1 = [(rrg1[:, i // 2, (i % 2) * 4096:(i % 2) * 4096 + 4096],
                hch[:, 0:4096] if i % 2 == 0 else hch[:, 4096:8192],
                ["hch0", "hch1"] if i % 2 == 0 else ["hch2", "hch3"])
               for i in range(8)]
        wd1 = [(rrd[:, i, :],
                x1ch[:, 0:1024] if i % 2 == 0 else x1ch[:, 1024:2048],
                ["x1ch"]) for i in range(32)]
        wu1 = [(rru[:, i // 4, (i % 4) * 1024:(i % 4) * 1024 + 1024],
                tmp1[:], ["tmp1"]) for i in range(32)]
        for i in range(32):
            p1_unit("sync", *wd1[i], MXC, MX)
            p1_unit("sync", *wu1[i], SXI, MXR)
            if i < 8:
                p1_unit("scalar", *wg1[i], TMPC, SCR)
            if n_chunks > 0 and phase >= 1:
                if i == 7:
                    emit_proj(0, skip_load=True)
                elif i == 15 and phase >= 2:
                    emit_attnA(0, 0, skip_xtmt=True)
                elif i == 23 and phase >= 2:
                    emit_attnA(0, 1, skip_xtmt=True)
                elif i == 27 and n_chunks > 1:
                    emit_proj(1, load_q="scalar")

        def qw_bcast(ac, acol, wc):
            b.op("tensor", lambda t: t.matmul(avd[0:1, 0:1], cONE, col(ac),
                                              start=True, stop=True),
                 reads=["consts", "cols"], writes=["avd"])
            b.op("vector",
                 lambda v: v.tensor_scalar(col(ac)[0:1, :], avd[0:1, 0:1],
                                           1.0 / NEL, 1e-5, ALU.mult,
                                           ALU.max),
                 reads=["avd"], writes=["cols"])
            b.op("tensor", lambda t: t.matmul(avd[:, 1:2], ones1[0:1, :],
                                              col(ac)[0:1, :], start=True,
                                              stop=True),
                 reads=["ones1", "cols"], writes=["avd"])
            b.op("vector",
                 lambda v: v.tensor_copy(consts[:, acol:acol + 1],
                                         avd[:, 1:2]),
                 reads=["avd"], writes=["consts"])
            b.op("vector",
                 lambda v: v.reciprocal(col(wc), consts[:, acol:acol + 1]),
                 reads=["consts"], writes=["cols"])

        qw_bcast(SCR, AG, WSC)
        qw_bcast(MXR, AU, SH127)
        qw_bcast(MX, AD, S127)

        rr5g = s_wgu.rearrange("p (jb ic k) -> p jb ic k", ic=8, k=1024)
        rr5d = s_wdq.rearrange("p (jb jc o) -> p jb jc o", jc=2, o=1024)

        def p2_unit(qq, src, stg, sns, ub, un, bnb, bnn, tern, ternn,
                    wc, oap, tap, sname):
            b.dma(qq, stg, src, writes=sns)
            b.op("scalar",
                 lambda s: s.activation(ub, stg, AF.Identity, bias=0.0,
                                        scale=col(wc)),
                 reads=sns + ["cols"], writes=[un])
            b.op("vector",
                 lambda v: v.tensor_scalar(bnb, ub, -0.5, None, ALU.is_le),
                 reads=[un], writes=[bnn])
            b.op("vector",
                 lambda v: v.scalar_tensor_tensor(tern, ub, 0.5, bnb,
                                                  ALU.is_ge, ALU.subtract),
                 reads=[un, bnn], writes=[ternn])
            b.dma(qq, oap, tap, reads=[ternn], writes=[sname])

        gu2 = []
        for cb_ in range(2):
            for rb_ in range(8):
                for rrw, koff, wc in ((rrg, 0, WSC), (rru, 512, SH127)):
                    gu2.append((rrw[:, rb_, cb_ * 2048:(cb_ + 1) * 2048],
                                rb_, cb_, koff, wc))
        for i in range(32):
            stgd, snd = stg_x1h[i % 2]
            td8 = hqTs[0][:, 0:512].bitcast(F8)
            p2_unit("sync", rrd[:, i, :], stgd, [snd],
                    hch[:, 4096:5120], "hch2", hch[:, 6144:7168], "hch3",
                    td8, "hqT0", S127,
                    rr5d[:, i // 2, i % 2, :], td8, "wdq_s")
            src, rb_, cb_, koff, wc = gu2[i]
            stgg, sng = stg_h01[i % 2]
            tern = hqTs[1][:, 0:1024].bitcast(F8)
            p2_unit("scalar", src, stgg, [sng],
                    hch[:, 4096:6144], "hch2", hch[:, 6144:8192], "hch3",
                    tern, "hqT1", wc,
                    rr5g[:, cb_ * 4:(cb_ + 1) * 4, rb_, koff:koff + 512],
                    tern.rearrange("p (jb j) -> p jb j", j=512),
                    "wgu_sA" if cb_ == 0 else "wgu_sB")
        b.op("vector", lambda v: v.tensor_scalar(col(AG127),
                                                 consts[:, AG:AG + 1],
                                                 1.0 / 127.0, None, ALU.mult),
             reads=["consts"], writes=["cols"])
        b.op("vector", lambda v: v.tensor_mul(col(ACOMB), consts[:, AU:AU + 1],
                                              consts[:, AD:AD + 1]),
             reads=["consts"], writes=["cols"])
        b.op("vector", lambda v: v.tensor_scalar(col(ACOMB), col(ACOMB),
                                                 1.0 / (127.0 * 127.0), None,
                                                 ALU.mult),
             reads=["cols"], writes=["cols"])

        if n_chunks > 0 and phase >= 2:
            emit_tileB(0, 0)
            emit_tileB_xpose(0)
            emit_tileB(0, 1)
            if phase < 3:
                emit_tileB_xpose(1)

        for ch in range(n_chunks):
            if ch > 0:
                if phase >= 2:
                    emit_attnA(ch, 0)
                    emit_attnA(ch, 1)
            if ch + 1 < n_chunks and phase >= 1 and ch > 0:
                emit_proj(ch + 1)
            if ch > 0 and phase >= 2:
                emit_tileB(ch, 0)
                emit_tileB_xpose(0)
                emit_tileB(ch, 1)
            if phase >= 3:
                emit_ffn(ch, after_first_w0=lambda: emit_tileB_xpose(1))
            elif phase >= 2 and ch > 0:
                emit_tileB_xpose(1)

        b.final_wait("sync", "yout")
        b.emit(block, ctx, nc, sems_op)
    return nc


_CACHE = {}


def _host_prep(inputs):
    x = np.asarray(inputs["L0_attn_x"], dtype=np.float32)
    Wq = np.ascontiguousarray(np.asarray(inputs["L0_attn_Wq"], np.float32))
    bq = np.asarray(inputs["L0_attn_bq"], np.float32)
    Wdn = np.ascontiguousarray(np.asarray(inputs["L0_attn_Wdown"], np.float32))
    bdn = np.asarray(inputs["L0_attn_bdown"], np.float32)
    Wup = np.ascontiguousarray(np.asarray(inputs["L0_attn_Wup"], np.float32))
    bup = np.asarray(inputs["L0_attn_bup"], np.float32)
    Wout = np.ascontiguousarray(np.asarray(inputs["L0_attn_Wout"], np.float32))
    bout = np.asarray(inputs["L0_attn_bout"], np.float32)
    wgT = np.ascontiguousarray(np.asarray(inputs["L0_ffn_w_gate"], np.float32).T)
    wuT = np.ascontiguousarray(np.asarray(inputs["L0_ffn_w_up"], np.float32).T)
    wdT = np.ascontiguousarray(np.asarray(inputs["L0_ffn_w_down"], np.float32).T)

    bdn_cols = np.ascontiguousarray(bdn.reshape(2, 128).T)
    bout_row = np.ascontiguousarray(bout.reshape(1, D)
                                    .astype(ml_dtypes.bfloat16))
    bq_row = np.ascontiguousarray(bq.reshape(1, D).astype(ml_dtypes.bfloat16))
    bup_row = np.ascontiguousarray(bup.reshape(1, DKV)
                                   .astype(ml_dtypes.bfloat16))

    consts = np.zeros((128, 8), np.float32)
    consts[:, 0] = C2
    consts[:, 1] = C1
    consts[:, 2] = 1.0
    consts[:, 3] = 1e-5

    m_idx = np.arange(128)
    rm = m_idx // 16
    rn64 = (np.arange(64)) // 8
    NEG = float(np.float32(ml_dtypes.bfloat16(-30000.0)))
    mU = np.zeros((128, 128), np.float32)
    mV = np.zeros((128, 512), np.float32)
    for base in (0, 64):
        mU[base, :] = 1.0
        mV[base, :] = NEG
        for r in range(8):
            mU[base + 1 + r, :] = (rm == r).astype(np.float32)
            mV[base + 1 + r, :] = np.tile((rn64 == r) * (-NEG), 8)
    mU = mU.astype(ml_dtypes.bfloat16)
    mV = mV.astype(ml_dtypes.bfloat16)
    id128 = np.eye(128, dtype=ml_dtypes.bfloat16)
    id2 = np.zeros((128, 64), dtype=ml_dtypes.bfloat16)
    id2[0:64, :] = np.eye(64)
    id2[64:128, :] = np.eye(64)
    ones_row = np.ones((1, 256), ml_dtypes.bfloat16)

    shared = {
        "Wq": Wq, "Wdn": Wdn, "Wup": Wup, "Wout": Wout,
        "bdn_cols": bdn_cols, "bout_row": bout_row,
        "bq_row": bq_row, "bup_row": bup_row,
        "wgT": wgT, "wuT": wuT, "wdT": wdT,
        "consts": consts, "mU": mU, "mV": mV,
        "id128": id128, "id2": id2,
        "ones_row": ones_row,
    }
    in_maps = []
    for bidx in range(NCORE):
        m = dict(shared)
        m["xT"] = np.ascontiguousarray(x[bidx].T)
        m["xtm"] = np.ascontiguousarray(x[bidx])
        in_maps.append(m)
    return in_maps


def kernel(**inputs):
    if "nc" not in _CACHE:
        _CACHE["nc"] = build_nc()
    nc = _CACHE["nc"]
    in_maps = _host_prep(inputs)
    res = run_bass_kernel_spmd(nc, in_maps, core_ids=list(range(NCORE)))
    out = np.stack([np.asarray(r["y"], dtype=np.float32) for r in res.results],
                   axis=0)
    return out


# version 22
# speedup vs baseline: 1.0385x; 1.0385x over previous
import numpy as np
import ml_dtypes
import concourse.bass as bass
import concourse.mybir as mybir
from concourse.bass_utils import run_bass_kernel_spmd
from contextlib import ExitStack

F32 = mybir.dt.float32
BF16 = mybir.dt.bfloat16
F8 = mybir.dt.float8e4
AF = mybir.ActivationFunctionType
ALU = mybir.AluOpType
AX = mybir.AxisListType.X

T, D, NH, DH, DLAT, DKV, DI = 4096, 1024, 16, 64, 256, 2048, 4096
NCORE = 8
TCH = 256
NCHUNK = T // TCH
TPC = TCH // 128
JB = 512
NJB = DI // JB
C1 = 12582912.0
C2 = 6755399441055744.0

ENG = ("tensor", "vector", "scalar", "gpsimd", "sync")


class B:

    def __init__(self):
        self.items = {e: [] for e in ENG}
        self.nop = {e: 0 for e in ENG}
        self.ndma = {}
        self.writer = {}
        self.readers = {}

    def _need(self, waits, eng, entry, is_dma):
        kind, e2, i = entry
        if (kind == "op" and e2 == eng and eng == "tensor" and not is_dma):
            return
        key = (kind, e2)
        v = (i + 1) if kind == "op" else 16 * (i + 1)
        if waits.get(key, 0) < v:
            waits[key] = v

    def _deps(self, eng, reads, writes, is_dma):
        waits = {}
        for b_ in reads:
            w = self.writer.get(b_)
            if w is not None:
                self._need(waits, eng, w, is_dma)
        for b_ in writes:
            w = self.writer.get(b_)
            if w is not None:
                self._need(waits, eng, w, is_dma)
            for r in self.readers.get(b_, ()):
                self._need(waits, eng, r, is_dma)
        return waits

    def _record(self, entry, reads, writes):
        for b_ in reads:
            self.readers.setdefault(b_, []).append(entry)
        for b_ in writes:
            self.writer[b_] = entry
            self.readers[b_] = []

    def op(self, eng, emit, reads=(), writes=()):
        waits = self._deps(eng, reads, writes, False)
        i = self.nop[eng]
        self.items[eng].append((waits, emit, "op"))
        self.nop[eng] = i + 1
        self._record(("op", eng, i), reads, writes)

    def dma(self, q, out_ap, in_ap, reads=(), writes=(), stream=None):
        stream = stream or (writes[0] if writes else "yout")
        waits = self._deps(q, reads, writes, True)
        i = self.ndma.get(stream, 0)
        self.items[q].append((waits, (out_ap, in_ap, stream), "dma"))
        self.ndma[stream] = i + 1
        self._record(("dma", stream, i), reads, writes)

    def final_wait(self, eng, stream):
        n = self.ndma.get(stream, 0)
        self.items[eng].append(({("dma", stream): 16 * n}, None, "wait_only"))

    def emit(self, block, ctx, nc, sems_op):
        sems_dma = {}
        for s in self.ndma:
            sems_dma[s] = ctx.enter_context(nc.semaphore("sdm_%s" % s))
        handles = {"vector": block.vector, "scalar": block.scalar,
                   "tensor": block.tensor, "gpsimd": block.gpsimd,
                   "sync": block.sync}
        for e in ENG:
            items = self.items[e]
            if not items:
                continue

            def make(eng_name, eng_items):
                def _(eng):
                    hw = {}
                    for waits, payload, kind in eng_items:
                        for key in sorted(waits):
                            v = waits[key]
                            if hw.get(key, 0) >= v:
                                continue
                            k2, e2 = key
                            sem = sems_op[e2] if k2 == "op" else sems_dma[e2]
                            eng.wait_ge(sem, v)
                            hw[key] = v
                        if kind == "op":
                            payload(eng).then_inc(sems_op[eng_name], 1)
                        elif kind == "dma":
                            out_ap, in_ap, stream = payload
                            eng.dma_start(out=out_ap, in_=in_ap).then_inc(
                                sems_dma[stream], 16)
                return _
            handles[e](make(e, items))


def build_nc(n_chunks=NCHUNK, phase=3):
    nc = bass.Bass("TRN2", target_bir_lowering=False, debug=False)

    d_xT = nc.dram_tensor("xT", [D, T], F32, kind="ExternalInput").ap()
    d_xtm = nc.dram_tensor("xtm", [T, D], F32, kind="ExternalInput").ap()
    d_Wq = nc.dram_tensor("Wq", [D, D], F32, kind="ExternalInput").ap()
    d_Wdn = nc.dram_tensor("Wdn", [D, DLAT], F32, kind="ExternalInput").ap()
    d_Wup = nc.dram_tensor("Wup", [DLAT, DKV], F32, kind="ExternalInput").ap()
    d_Wout = nc.dram_tensor("Wout", [D, D], F32, kind="ExternalInput").ap()
    d_bdn = nc.dram_tensor("bdn_cols", [128, 2], F32, kind="ExternalInput").ap()
    d_bout = nc.dram_tensor("bout_row", [1, D], BF16, kind="ExternalInput").ap()
    d_bq_row = nc.dram_tensor("bq_row", [1, D], BF16, kind="ExternalInput").ap()
    d_bup_row = nc.dram_tensor("bup_row", [1, DKV], BF16,
                               kind="ExternalInput").ap()
    d_wgT = nc.dram_tensor("wgT", [D, DI], F32, kind="ExternalInput").ap()
    d_wuT = nc.dram_tensor("wuT", [D, DI], F32, kind="ExternalInput").ap()
    d_wdT = nc.dram_tensor("wdT", [DI, D], F32, kind="ExternalInput").ap()
    d_consts = nc.dram_tensor("consts", [128, 8], F32, kind="ExternalInput").ap()
    d_mU = nc.dram_tensor("mU", [128, 128], BF16, kind="ExternalInput").ap()
    d_mV = nc.dram_tensor("mV", [128, 512], BF16, kind="ExternalInput").ap()
    d_id128 = nc.dram_tensor("id128", [128, 128], BF16, kind="ExternalInput").ap()
    d_id2 = nc.dram_tensor("id2", [128, 64], BF16, kind="ExternalInput").ap()
    d_ones = nc.dram_tensor("ones_row", [1, 256], BF16,
                            kind="ExternalInput").ap()
    d_y = nc.dram_tensor("y", [T, D], F32, kind="ExternalOutput").ap()
    s_wgu = nc.dram_tensor("wgu_s", [128, 65536], F8, kind="Internal").ap()
    s_wdq = nc.dram_tensor("wdq_s", [128, 32768], F8, kind="Internal").ap()

    with ExitStack() as ctx:
        def sb(name, shape, dtype=F32):
            return ctx.enter_context(nc.sbuf_tensor(name, shape, dtype))

        def psum(name, shape, dtype=F32):
            return ctx.enter_context(nc.psum_tensor(name, shape, dtype))

        consts = sb("consts_sb", [128, 8])
        id128 = sb("id128t", [128, 128], BF16)
        id2 = sb("id2t", [128, 64], BF16)
        mU = sb("mUt", [128, 128], BF16)
        mV = sb("mVt", [128, 512], BF16)
        ones_row = sb("ones_rowt", [1, 256], BF16)
        ones1 = sb("ones1t", [1, 128])
        bout_row = sb("bout_rowt", [1, D], BF16)
        bq_row = sb("bq_rowt", [1, D], BF16)
        bup_row = sb("bup_rowt", [1, DKV], BF16)
        bdn_cols = sb("bdn_colst", [128, 2])
        wq_sb = sb("wq_sb", [128, 8192], BF16)
        wdn_sb = sb("wdn_sb", [128, 2048], BF16)
        wup_sb = sb("wup_sb", [128, 4096], BF16)
        wout_sb = sb("wout_sb", [128, 8192], BF16)
        cols = sb("colst", [128, 48])
        rdt = sb("rdt", [128, 16])
        xstg = sb("xstgt", [128, 2048])
        xtb = sb("xtb", [128, 2048], BF16)
        kvlT = sb("kvlT", [128, 512], BF16)
        qT = sb("qTt", [128, TCH * 8], BF16)
        kvT = sb("kvTt", [128, TCH * 16], BF16)
        kdupT = sb("kdupTt", [128, TCH * 16], BF16)
        eM = sb("eMt", [128, 2048], BF16)
        vaug = sb("vaugt", [128, 1024], BF16)
        onecol = sb("onecolt", [128, 1], BF16)
        aoP = eM[:, 1024:2048]
        aoTs = eM[:, 0:1024]
        xq = eM[:, 0:1024]
        xtmt = sb("xtmtt", [128, 1024])
        xtmt2 = sb("xtmt2t", [128, 1024])
        rA = [sb("r1a", [128, 1024]), sb("r1b", [128, 1024])]
        tmp1 = sb("tmp1t", [128, 1024])
        x1ch = sb("x1cht", [128, 2048])
        xqTs = [sb("xqTs0", [128, 1024], BF16), sb("xqTs1", [128, 1024], BF16)]
        sgate = sb("sgatet", [128, JB])
        hch = sb("hcht", [128, 2 * DI])
        hqb2 = [sb("hqb0", [128, 1024], BF16), sb("hqb1", [128, 1024], BF16)]
        hqTs = [sb("hqTst0", [128, DI], BF16), sb("hqTst1", [128, DI], BF16)]
        wgu = [sb("wgu%d" % i, [128, 4096], F8) for i in range(4)]
        wdt = [sb("wdt%d" % i, [128, 2048], F8) for i in range(5)]
        bns = sb("bnst", [128, 12])
        bnag = sb("bnagt", [128, 2])

        cC2 = consts[:, 0:1]
        cC1 = consts[:, 1:2]
        cONE = consts[:, 2:3]
        cEPS = consts[:, 3:4]
        AG, AU, AD = 5, 6, 7

        def col(i):
            return cols[:, i:i + 1]
        (IND, INDC, WSC, AG127, ACOMB, MU, VAR_, LCOL, RSTD, NEGMUR, TMPC,
         MX, MXC, SX, S127, SXI, MXR, MX1C0, MX1C1, SG0, SG1,
         MH, MHC, SH127, SHI, YSC0, YSC1, SCR) = range(28)
        HP0 = 32

        psc0 = psum("psc0", [128, 512])
        psc1 = psum("psc1", [128, 512])
        pbt = psum("pbt", [128, 1024], BF16)
        avp = psum("avp", [128, 1024])
        avd = psum("avd", [128, 512])
        pp0 = psum("pp0", [128, 512])
        pp1 = psum("pp1", [128, 512])

        sems_op = {e: ctx.enter_context(nc.semaphore("so_" + e)) for e in ENG}
        block = ctx.enter_context(nc.Block())
        b = B()

        def emit_proj(ch, skip_load=False, load_q="sync"):
            t0 = ch * TCH
            if not skip_load:
                b.dma(load_q, xstg[:].rearrange("p (ic t) -> p ic t", t=TCH),
                      d_xT.rearrange("(ic p) t -> p ic t", p=128)[:, :,
                                                                 t0:t0 + TCH],
                      writes=["xstg"])
            b.op("vector", lambda v: v.tensor_copy(xtb[:], xstg[:]),
                 reads=["xstg"], writes=["xtb"])

            for lc in range(2):
                ps, pn = ([pp0, "pp0"], [pp1, "pp1"])[lc]
                for ic in range(8):
                    b.op("tensor",
                         lambda t, ps=ps, ic=ic, lc=lc: t.matmul(
                             ps[:, 0:TCH],
                             wdn_sb[:, ic * 256 + lc * 128:
                                    ic * 256 + lc * 128 + 128],
                             xtb[:, ic * TCH:(ic + 1) * TCH],
                             start=(ic == 0), stop=(ic == 7)),
                         reads=["wdn_sb", "xtb"], writes=[pn])
                b.op("scalar",
                     lambda s, ps=ps, lc=lc: s.activation(
                         kvlT[:, lc * TCH:(lc + 1) * TCH], ps[:, 0:TCH],
                         AF.Identity, bias=bdn_cols[:, lc:lc + 1], scale=1.0),
                     reads=[pn, "bdn"], writes=["kvlT"])

            for quad in range(2):
                for oc4 in range(4):
                    oc = quad * 4 + oc4
                    for ic in range(8):
                        b.op("tensor",
                             lambda t, ic=ic, oc=oc, oc4=oc4: t.matmul(
                                 avp[:, oc4 * 256: oc4 * 256 + TCH],
                                 wq_sb[:, ic * 1024 + oc * 128:
                                       ic * 1024 + oc * 128 + 128],
                                 xtb[:, ic * TCH:(ic + 1) * TCH],
                                 start=(ic == 0), stop=False),
                             reads=["wq_sb", "xtb"], writes=["avp"])
                    b.op("tensor",
                         lambda t, oc=oc, oc4=oc4: t.matmul(
                             avp[:, oc4 * 256: oc4 * 256 + TCH],
                             bq_row[0:1, oc * 128:(oc + 1) * 128],
                             ones_row[0:1, 0:TCH],
                             start=False, stop=True),
                         reads=["bq_row", "ones_row"], writes=["avp"])
                qdst = qT[:].rearrange("p (t c) -> p t c", c=8)[
                    :, :, quad * 4: quad * 4 + 4]
                qsrc = avp[:].rearrange("p (c t) -> p c t", t=256) \
                    .transpose([0, 2, 1])
                b.op("scalar",
                     lambda s, qdst=qdst, qsrc=qsrc: s.activation(
                         qdst, qsrc, AF.Identity, bias=0.0, scale=1.0),
                     reads=["avp"], writes=["qT"])

            for hq4 in range(4):
                for hh in range(4):
                    h = hq4 * 4 + hh
                    psA = [pp0, pp1][hh // 2]
                    pnA = ["pp0", "pp1"][hh // 2]
                    psB = [psc0, psc1][hh // 2]
                    pnB = ["psc0", "psc1"][hh // 2]
                    co = (hh % 2) * 256
                    for lc in range(2):
                        b.op("tensor",
                             lambda t, h=h, lc=lc, psA=psA, co=co: t.matmul(
                                 psA[:, co:co + TCH],
                                 wup_sb[:, lc * 2048 + h * 128:
                                        lc * 2048 + h * 128 + 128],
                                 kvlT[:, lc * TCH:(lc + 1) * TCH],
                                 start=(lc == 0), stop=False),
                             reads=["wup_sb", "kvlT"], writes=[pnA])
                    b.op("tensor",
                         lambda t, h=h, psA=psA, co=co: t.matmul(
                             psA[:, co:co + TCH],
                             bup_row[0:1, h * 128:(h + 1) * 128],
                             ones_row[0:1, 0:TCH],
                             start=False, stop=True),
                         reads=["bup_row", "ones_row"], writes=[pnA])
                    for lc in range(2):
                        b.op("tensor",
                             lambda t, h=h, lc=lc, psB=psB, co=co: t.matmul(
                                 psB[64:128, co:co + TCH],
                                 wup_sb[:, lc * 2048 + h * 128:
                                        lc * 2048 + h * 128 + 64],
                                 kvlT[:, lc * TCH:(lc + 1) * TCH],
                                 start=(lc == 0), stop=False),
                             reads=["wup_sb", "kvlT"], writes=[pnB])
                    b.op("tensor",
                         lambda t, h=h, psB=psB, co=co: t.matmul(
                             psB[64:128, co:co + TCH],
                             bup_row[0:1, h * 128: h * 128 + 64],
                             ones_row[0:1, 0:TCH],
                             start=False, stop=True),
                         reads=["bup_row", "ones_row"], writes=[pnB])
                for bk in range(2):
                    psA = [pp0, pp1][bk]
                    pnA = ["pp0", "pp1"][bk]
                    psB = [psc0, psc1][bk]
                    pnB = ["psc0", "psc1"][bk]
                    g0 = hq4 * 4 + bk * 2
                    kvdst = kvT[:].rearrange("p (t g) -> p t g", g=16)[
                        :, :, g0:g0 + 2]
                    src_a = psA[:].rearrange("p (g t) -> p g t", t=256) \
                        .transpose([0, 2, 1])
                    b.op("scalar",
                         lambda s, kvdst=kvdst, src_a=src_a: s.activation(
                             kvdst, src_a, AF.Identity, bias=0.0, scale=1.0),
                         reads=[pnA], writes=["kvT"])
                    kddst = kdupT[64:128, :].rearrange(
                        "p (t g) -> p t g", g=16)[:, :, g0:g0 + 2]
                    src_b = psB[64:128, :].rearrange(
                        "p (g t) -> p g t", t=256).transpose([0, 2, 1])
                    b.op("scalar",
                         lambda s, kddst=kddst, src_b=src_b: s.activation(
                             kddst, src_b, AF.Identity, bias=0.0, scale=1.0),
                         reads=[pnB], writes=["kdupT"])

        def emit_attnA_qk(ch, w, part=None):
            tok0 = w * 128
            for b8 in range(2):
                if part is not None and part != b8:
                    continue
                for i8 in range(8):
                    s_ = b8 * 8 + i8
                    S16 = (tok0 + 8 * s_) * 16
                    S8 = (tok0 + 8 * s_) * 8
                    b.op("tensor",
                         lambda t, i8=i8, S16=S16, S8=S8: t.matmul(
                             psc0[:, i8 * 64:(i8 + 1) * 64],
                             kvT[0:64, S16:S16 + 128],
                             qT[0:64, S8:S8 + 64],
                             start=(i8 == 0), stop=False),
                         reads=["kvT", "qT"], writes=["psc0"])
                    b.op("tensor",
                         lambda t, i8=i8, S16=S16, S8=S8: t.matmul(
                             psc1[:, i8 * 64:(i8 + 1) * 64],
                             kdupT[64:128, S16:S16 + 128],
                             qT[64:128, S8:S8 + 64],
                             start=(i8 == 0), stop=False),
                         reads=["kdupT", "qT"], writes=["psc1"])
                b.op("tensor",
                     lambda t: t.matmul(psc0[:], mU[0:64, :], mV[0:64, :],
                                        start=False, stop=True),
                     reads=["mU", "mV"], writes=["psc0"])
                b.op("tensor",
                     lambda t: t.matmul(psc1[:], mU[64:128, :],
                                        mV[64:128, :],
                                        start=False, stop=True),
                     reads=["mU", "mV"], writes=["psc1"])
                eMb = eM[:, b8 * 1024:(b8 + 1) * 1024].rearrange(
                    "p (s c) -> p s c", c=128)
                b.op("scalar",
                     lambda s, eMb=eMb: s.activation(
                         eMb[:, :, 0:64], psc0[:], AF.Exp, bias=0.0,
                         scale=0.125),
                     reads=["psc0"], writes=["eM"])
                b.op("scalar",
                     lambda s, eMb=eMb: s.activation(
                         eMb[:, :, 64:128], psc1[:], AF.Exp, bias=0.0,
                         scale=0.125),
                     reads=["psc1"], writes=["eM"])

        def emit_attnA(ch, w, skip_xtmt=False, skip_qk=False, xtmt_q="sync"):
            tglob = ch * TPC + w
            tok0 = w * 128
            xtb_, xtn_ = [(xtmt, "xtmt"), (xtmt2, "xtmt2")][tglob % 2]
            if not skip_xtmt:
                b.dma(xtmt_q, xtb_[:],
                      d_xtm[tglob * 128:(tglob + 1) * 128, :],
                      writes=[xtn_])
            if not skip_qk:
                emit_attnA_qk(ch, w)

            for s_ in range(16):
                S16 = (tok0 + 8 * s_) * 16
                b.op("tensor",
                     lambda t, s_=s_, S16=S16: t.transpose(
                         pbt[:, s_ * 64:(s_ + 1) * 64],
                         kvT[64:128, S16:S16 + 128],
                         id2[64:128, 0:64]),
                     reads=["kvT", "id2"], writes=["pbt"])
            b.op("vector", lambda v: v.tensor_copy(vaug[:], pbt[:]),
                 reads=["pbt"], writes=["vaug"])

            for s_ in range(16):
                b.op("tensor",
                     lambda t, s_=s_: t.matmul(
                         avp[:, s_ * 64:(s_ + 1) * 64],
                         eM[:, s_ * 128:(s_ + 1) * 128],
                         vaug[:, s_ * 64:(s_ + 1) * 64],
                         start=True, stop=True),
                     reads=["eM", "vaug"], writes=["avp"])
                b.op("tensor",
                     lambda t, s_=s_: t.matmul(
                         avd[:, s_:s_ + 1],
                         eM[:, s_ * 128:(s_ + 1) * 128],
                         onecol[:],
                         start=True, stop=True),
                     reads=["eM", "onecol"], writes=["avd"])
            b.op("vector", lambda v: v.reciprocal(rdt[:], avd[:, 0:16]),
                 reads=["avd"], writes=["rdt"])
            b.op("vector", lambda v: v.tensor_mul(
                aoP.rearrange("p (s j) -> p s j", j=64),
                avp[:].rearrange("p (s j) -> p s j", j=64),
                rdt[:].unsqueeze(2).broadcast_to([128, 16, 64])),
                 reads=["avp", "rdt", "eM"], writes=["eM"])

            for p_ in (1, 0):
                for s_ in range(16):
                    b.op("tensor",
                         lambda t, p_=p_, s_=s_: t.transpose(
                             pbt[p_ * 64:(p_ + 1) * 64,
                                 s_ * 64:(s_ + 1) * 64],
                             aoP[p_ * 64:(p_ + 1) * 64,
                                 s_ * 64:(s_ + 1) * 64],
                             id2[p_ * 64:(p_ + 1) * 64, 0:64]),
                         reads=["eM", "id2"], writes=["pbt"])
                b.op("vector",
                     lambda v, p_=p_: v.tensor_copy(
                         aoTs[p_ * 64:(p_ + 1) * 64, :],
                         pbt[p_ * 64:(p_ + 1) * 64, :]),
                     reads=["pbt"], writes=["eM"])

            aoTr = aoTs.rearrange("p (t c) -> p t c", c=8)
            for cs in range(8):
                for half in range(2):
                    ps, pn = ([pp0, "pp0"], [pp1, "pp1"])[half]
                    b.op("tensor",
                         lambda t, ps=ps, cs=cs, half=half: t.matmul(
                             ps[:],
                             aoTr[:, :, cs],
                             wout_sb[:, cs * 1024 + half * 512:
                                     cs * 1024 + half * 512 + 512],
                             start=(cs == 0), stop=False),
                         reads=["eM", "wout_sb"], writes=[pn])
            for half in range(2):
                ps, pn = ([pp0, "pp0"], [pp1, "pp1"])[half]
                b.op("tensor",
                     lambda t, ps=ps, half=half: t.matmul(
                         ps[:], ones_row[0:1, 0:128],
                         bout_row[0:1, half * 512:(half + 1) * 512],
                         start=False, stop=True),
                     reads=["ones_row", "bout_row"], writes=[pn])
            for half in range(2):
                ps, pn = ([pp0, "pp0"], [pp1, "pp1"])[half]
                b.op("vector",
                     lambda v, ps=ps, half=half, w=w: v.tensor_add(
                         rA[w][:, half * 512:(half + 1) * 512], ps[:],
                         xtb_[:, half * 512:(half + 1) * 512]),
                     reads=[pn, xtn_], writes=["rA%d" % w])

        def emit_tileB_xpose(w):
            for ic in range(8):
                b.op("tensor",
                     lambda t, ic=ic: t.transpose(
                         pbt[:, ic * 128:(ic + 1) * 128],
                         xq[:, ic * 128:(ic + 1) * 128], id128[:]),
                     reads=["eM", "id128"], writes=["pbt"])
            b.op("vector",
                 lambda v, w=w: v.tensor_copy(xqTs[w][:], pbt[:]),
                 reads=["pbt"], writes=["xqT%d" % w])

        def emit_tileB(ch, w):
            rw = rA[w]
            rn = "rA%d" % w
            b.op("vector", lambda v: v.bn_stats(bns[:, 0:6], rw[:, 0:512]),
                 reads=[rn], writes=["bns"])
            b.op("vector", lambda v: v.bn_stats(bns[:, 6:12], rw[:, 512:1024]),
                 reads=[rn], writes=["bns"])
            b.op("vector", lambda v: v.bn_aggr(bnag[:], bns[:]),
                 reads=["bns"], writes=["bnag"])
            b.op("scalar", lambda s: s.activation(col(LCOL), bnag[:, 1:2],
                                                  AF.Ln, bias=cEPS, scale=1.0),
                 reads=["bnag", "consts"], writes=["cols"])
            b.op("scalar", lambda s: s.activation(col(RSTD), col(LCOL),
                                                  AF.Exp, bias=0.0, scale=-0.5),
                 reads=["cols"], writes=["cols"])
            b.op("vector", lambda v: v.tensor_mul(col(TMPC), bnag[:, 0:1],
                                                  col(RSTD)),
                 reads=["bnag", "cols"], writes=["cols"])
            b.op("vector", lambda v: v.tensor_scalar(col(NEGMUR), col(TMPC),
                                                     -1.0, None, ALU.mult),
                 reads=["cols"], writes=["cols"])
            b.op("scalar",
                 lambda s, w=w: s.activation(
                     x1ch[:, w * 1024:(w + 1) * 1024], rw[:], AF.Identity,
                     bias=col(NEGMUR), scale=col(RSTD)),
                 reads=[rn, "cols"], writes=["x1ch"])
            b.op("vector", lambda v, rw=rw: v.tensor_scalar(
                rw[:], rw[:], bnag[:, 0:1], None, ALU.subtract),
                 reads=[rn, "bnag"], writes=[rn])
            b.op("vector", lambda v, rw=rw: v.tensor_reduce(
                col(MX), rw[:], AX, ALU.max, apply_absolute_value=True),
                 reads=[rn], writes=["cols"])
            b.op("vector", lambda v: v.tensor_scalar(col(MXC), col(MX), 1e-5,
                                                     None, ALU.max),
                 reads=["cols"], writes=["cols"])
            b.op("vector", lambda v: v.reciprocal(col(SX), col(MXC)),
                 reads=["cols"], writes=["cols"])
            b.op("vector", lambda v: v.tensor_scalar(col(S127), col(SX), 127.0,
                                                     None, ALU.mult),
                 reads=["cols"], writes=["cols"])
            b.op("vector", lambda v: v.tensor_mul(col(SXI), col(S127),
                                                  col(INDC)),
                 reads=["cols"], writes=["cols"])
            b.op("scalar", lambda s, rw=rw: s.activation(
                tmp1[:], rw[:], AF.Identity, bias=cC2, scale=col(S127)),
                 reads=[rn, "cols", "consts"], writes=["tmp1"])
            b.op("scalar", lambda s, rw=rw: s.activation(
                xtmt[:], rw[:], AF.Identity, bias=cC1, scale=col(SXI)),
                 reads=[rn, "cols", "consts"], writes=["xtmt"])
            b.op("vector", lambda v: v.tensor_scalar(
                xtmt[:], xtmt[:], -C1, None, ALU.add),
                 reads=["xtmt"], writes=["xtmt"])
            b.op("vector", lambda v: v.scalar_tensor_tensor(
                xq, tmp1[:], -C2, xtmt[:], ALU.add, ALU.add),
                 reads=["tmp1", "xtmt"], writes=["eM"])
            b.op("vector", lambda v: v.tensor_mul(col(MXR), col(MX),
                                                  col(RSTD)),
                 reads=["cols"], writes=["cols"])
            b.op("vector",
                 lambda v, w=w: v.tensor_scalar(
                     col([MX1C0, MX1C1][w]), col(MXR), 1e-5, None, ALU.max),
                 reads=["cols"], writes=["cols"])
            b.op("vector",
                 lambda v, w=w: v.tensor_mul(
                     col([SG0, SG1][w]), col([MX1C0, MX1C1][w]), col(AG127)),
                 reads=["cols"], writes=["cols"])

        gbanks = [(pp0[:], "pp0"), (avp[:, 0:512], "avp"), (psc0[:], "psc0")]
        ubanks = [(pp1[:], "pp1"), (avp[:, 512:1024], "avp"), (psc1[:], "psc1")]

        def emit_ffn(ch, after_first_w0=None, qk_next=None):
            for jb in range(NJB):
                for icH in range(2):
                    u4 = (jb * 2 + icH) % 4
                    b.dma(["sync", "scalar"][icH], wgu[u4][:],
                          s_wgu[:, jb * 8192 + icH * 4096:
                                jb * 8192 + (icH + 1) * 4096],
                          reads=["wgu_sA" if jb < 4 else "wgu_sB"],
                          writes=["wgu%d" % u4])
                for w in range(TPC):
                    k = jb * 2 + w
                    gb, gn = gbanks[k % 3]
                    ub, un = ubanks[k % 3]
                    for ic in range(8):
                        icH, icl = ic // 4, ic % 4
                        u4 = (jb * 2 + icH) % 4
                        b.op("tensor",
                             lambda t, w=w, ic=ic, u4=u4, icl=icl,
                             gb=gb: t.matmul(
                                 gb,
                                 xqTs[w][:, ic * 128:(ic + 1) * 128],
                                 wgu[u4][:, icl * 1024:icl * 1024 + 512],
                                 start=(ic == 0), stop=(ic == 7)),
                             reads=["xqT%d" % w, "wgu%d" % u4],
                             writes=[gn])
                        b.op("tensor",
                             lambda t, w=w, ic=ic, u4=u4, icl=icl,
                             ub=ub: t.matmul(
                                 ub,
                                 xqTs[w][:, ic * 128:(ic + 1) * 128],
                                 wgu[u4][:, icl * 1024 + 512:
                                          (icl + 1) * 1024],
                                 start=(ic == 0), stop=(ic == 7)),
                             reads=["xqT%d" % w, "wgu%d" % u4],
                             writes=[un])
                    b.op("scalar",
                         lambda s, w=w, gb=gb: s.activation(
                             sgate[:], gb, AF.Silu, bias=0.0,
                             scale=col([SG0, SG1][w])),
                         reads=[gn, "cols"], writes=["sgate"])
                    hreg = "hch%d" % ((w * DI + jb * JB) // 2048)
                    b.op("vector",
                         lambda v, w=w, jb=jb, ub=ub: v.tensor_mul(
                             hch[:, w * DI + jb * JB: w * DI + (jb + 1) * JB],
                             sgate[:], ub),
                         reads=["sgate", un], writes=[hreg])
                    b.op("vector",
                         lambda v, w=w, jb=jb: v.tensor_reduce(
                             col(HP0 + w * 8 + jb),
                             hch[:, w * DI + jb * JB: w * DI + (jb + 1) * JB],
                             AX, ALU.max, apply_absolute_value=True),
                         reads=[hreg], writes=["cols"])
                    if jb == 0 and w == 0 and after_first_w0 is not None:
                        after_first_w0()

            def hquant_cols(w):
                b.op("vector",
                     lambda v, w=w: v.tensor_reduce(
                         col(MH), cols[:, HP0 + w * 8:HP0 + w * 8 + 8], AX,
                         ALU.max),
                     reads=["cols"], writes=["cols"])
                b.op("vector", lambda v: v.tensor_scalar(col(MHC), col(MH),
                                                         1e-5, None, ALU.max),
                     reads=["cols"], writes=["cols"])
                b.op("vector", lambda v: v.reciprocal(col(SCR), col(MHC)),
                     reads=["cols"], writes=["cols"])
                b.op("vector", lambda v: v.tensor_scalar(col(SH127), col(SCR),
                                                         127.0, None,
                                                         ALU.mult),
                     reads=["cols"], writes=["cols"])
                b.op("vector", lambda v: v.tensor_mul(col(SHI), col(SH127),
                                                      col(INDC)),
                     reads=["cols"], writes=["cols"])

            def hquant_qb(w, qb):
                hr = ["hch%d" % (w * 2), "hch%d" % (w * 2 + 1)]
                hsl = hch[:, w * DI + qb * 1024: w * DI + (qb + 1) * 1024]
                hb, hn = hqb2[qb % 2], "hqb%d" % (qb % 2)
                tt, tn = [(tmp1, "tmp1"), (xtmt, "xtmt")][qb % 2]
                tm2, tn2 = [(rA[0], "rA0"), (rA[1], "rA1")][qb % 2]
                b.op("scalar",
                     lambda s, hsl=hsl, tt=tt: s.activation(
                         tt[:], hsl, AF.Identity, bias=cC2,
                         scale=col(SH127)),
                     reads=hr + ["cols", "consts"], writes=[tn])
                b.op("scalar",
                     lambda s, hsl=hsl, tm2=tm2: s.activation(
                         tm2[:], hsl, AF.Identity, bias=cC1,
                         scale=col(SHI)),
                     reads=hr + ["cols", "consts"], writes=[tn2])
                b.op("vector", lambda v, tm2=tm2: v.tensor_scalar(
                    tm2[:], tm2[:], -C1, None, ALU.add),
                     reads=[tn2], writes=[tn2])
                b.op("vector",
                     lambda v, hb=hb, tt=tt,
                     tm2=tm2: v.scalar_tensor_tensor(
                         hb[:], tt[:], -C2,
                         tm2[:], ALU.add, ALU.add),
                     reads=[tn, tn2], writes=[hn])
                for jj in range(8):
                    b.op("tensor",
                         lambda t, hb=hb, jj=jj: t.transpose(
                             pbt[:, jj * 128:(jj + 1) * 128],
                             hb[:, jj * 128:(jj + 1) * 128], id128[:]),
                         reads=[hn, "id128"], writes=["pbt"])
                b.op("vector",
                     lambda v, qb=qb, w=w: v.tensor_copy(
                         hqTs[w][:, qb * 1024:(qb + 1) * 1024],
                         pbt[:]),
                     reads=["pbt"], writes=["hqT%d" % w])

            def hquant_ysc(w):
                b.op("vector",
                     lambda v, w=w: v.tensor_mul(
                         col([YSC0, YSC1][w]), col(MHC),
                         col([MX1C0, MX1C1][w])),
                     reads=["cols"], writes=["cols"])
                b.op("vector",
                     lambda v, w=w: v.tensor_mul(
                         col([YSC0, YSC1][w]), col([YSC0, YSC1][w]),
                         col(ACOMB)),
                     reads=["cols"], writes=["cols"])

            for w in range(TPC):
                hquant_cols(w)
                for qb in range(4):
                    hquant_qb(w, qb)
                hquant_ysc(w)

            y2s = [[avp[:, 0:512], avp[:, 512:1024]], [pp0[:], pp1[:]]]
            y2ns = [["avp", "avp"], ["pp0", "pp1"]]
            for jb2 in range(16):
                wd_t, wd_n = wdt[jb2 % 5], "wdt%d" % (jb2 % 5)
                b.dma("gpsimd", wd_t[:],
                      s_wdq[:, jb2 * 2048:(jb2 + 1) * 2048],
                      reads=["wdq_s"], writes=[wd_n])
                for w in range(TPC):
                    for jc in range(2):
                        jg = jb2 * 2 + jc
                        for half in range(2):
                            b.op("tensor",
                                 lambda t, w=w, jg=jg, jc=jc, half=half,
                                 wd_t=wd_t: t.matmul(
                                     y2s[w][half],
                                     hqTs[w][:, jg * 128:(jg + 1) * 128],
                                     wd_t[:, jc * 1024 + half * 512:
                                          jc * 1024 + half * 512 + 512],
                                     start=(jg == 0), stop=(jg == 31)),
                                 reads=["hqT%d" % w, wd_n],
                                 writes=[y2ns[w][half]])
            for w in range(TPC):
                rw = rA[w]
                rn = "rA%d" % w
                for half in range(2):
                    b.op("vector",
                         lambda v, w=w, half=half, rw=rw: v.scalar_tensor_tensor(
                             rw[:, half * 512:(half + 1) * 512], y2s[w][half],
                             col([YSC0, YSC1][w]),
                             x1ch[:, w * 1024 + half * 512:
                                  w * 1024 + half * 512 + 512],
                             ALU.mult, ALU.add),
                         reads=[y2ns[w][half], "x1ch", "cols"], writes=[rn])
                b.op("vector", lambda v, rw=rw: v.bn_stats(bns[:, 0:6],
                                                           rw[:, 0:512]),
                     reads=[rn], writes=["bns"])
                b.op("vector", lambda v, rw=rw: v.bn_stats(bns[:, 6:12],
                                                           rw[:, 512:1024]),
                     reads=[rn], writes=["bns"])
                b.op("vector", lambda v: v.bn_aggr(bnag[:], bns[:]),
                     reads=["bns"], writes=["bnag"])
                b.op("scalar", lambda s: s.activation(col(LCOL), bnag[:, 1:2],
                                                      AF.Ln, bias=cEPS,
                                                      scale=1.0),
                     reads=["bnag", "consts"], writes=["cols"])
                b.op("scalar", lambda s: s.activation(col(RSTD), col(LCOL),
                                                      AF.Exp, bias=0.0,
                                                      scale=-0.5),
                     reads=["cols"], writes=["cols"])
                b.op("vector", lambda v: v.tensor_mul(col(TMPC), bnag[:, 0:1],
                                                      col(RSTD)),
                     reads=["bnag", "cols"], writes=["cols"])
                b.op("vector", lambda v: v.tensor_scalar(col(NEGMUR),
                                                         col(TMPC),
                                                         -1.0, None, ALU.mult),
                     reads=["cols"], writes=["cols"])
                b.op("scalar",
                     lambda s, rw=rw: s.activation(
                         tmp1[:], rw[:], AF.Identity,
                         bias=col(NEGMUR), scale=col(RSTD)),
                     reads=[rn, "cols"], writes=["tmp1"])
                tglob = ch * TPC + w
                b.dma("sync", d_y[tglob * 128:(tglob + 1) * 128, :], tmp1[:],
                      reads=["tmp1"])

        b.dma("sync", consts[:], d_consts, writes=["consts"])
        b.dma("sync", id128[:], d_id128, writes=["id128"])
        b.dma("sync", id2[:], d_id2, writes=["id2"])
        b.dma("sync", mU[:], d_mU, writes=["mU"])
        b.dma("sync", mV[:], d_mV, writes=["mV"])
        b.dma("sync", ones_row[:], d_ones, writes=["ones_row"])
        b.dma("sync", bout_row[:], d_bout, writes=["bout_row"])
        b.dma("sync", bq_row[:], d_bq_row, writes=["bq_row"])
        b.dma("sync", bup_row[:], d_bup_row, writes=["bup_row"])
        b.dma("sync", bdn_cols[:], d_bdn, writes=["bdn"])

        b.op("vector", lambda v: v.memset(col(SCR), 1.0), writes=["cols"])
        b.op("vector", lambda v: v.tensor_scalar(col(IND), col(SCR), C2, -C2,
                                                 ALU.add, ALU.add),
             reads=["cols"], writes=["cols"])
        b.op("vector", lambda v: v.tensor_scalar(col(INDC), col(IND), -1.0, 1.0,
                                                 ALU.mult, ALU.add),
             reads=["cols"], writes=["cols"])
        b.op("vector", lambda v: v.memset(onecol[:], 1.0), writes=["onecol"])
        b.op("vector", lambda v: v.memset(ones1[:], 1.0), writes=["ones1"])

        b.dma("sync", xstg[:].rearrange("p (ic t) -> p ic t", t=TCH),
              d_xT.rearrange("(ic p) t -> p ic t", p=128)[:, :, 0:TCH],
              writes=["xstg"])
        b.dma("sync", xtmt[:], d_xtm[0:128, :], writes=["xtmt"])
        b.dma("sync", xtmt2[:], d_xtm[128:256, :],
              writes=["xtmt2"])

        def load_cast(qq, dst_sb, dst_name, dram, n_iter, width, stgs):
            rr = dram.rearrange("(i p) o -> p i o", p=128)
            for i in range(n_iter):
                stg, sn = stgs[i % len(stgs)]
                b.dma(qq, stg[:, 0:width], rr[:, i, :], writes=[sn])
                b.op("vector",
                     lambda v, stg=stg, i=i: v.tensor_copy(
                         dst_sb[:, i * width:(i + 1) * width],
                         stg[:, 0:width]),
                     reads=[sn], writes=[dst_name])

        stg_h01 = [(hch[:, 0:2048], "hch0"), (hch[:, 2048:4096], "hch1")]
        stg_x1h = [(x1ch[:, 0:1024], "x1ch"), (x1ch[:, 1024:2048], "x1ch")]
        stg_rA = [(rA[0][:], "rA0"), (rA[1][:], "rA1")]
        load_cast("sync", wq_sb, "wq_sb", d_Wq, 8, 1024, stg_h01)
        load_cast("sync", wdn_sb, "wdn_sb", d_Wdn, 8, 256, stg_h01)
        load_cast("scalar", wout_sb, "wout_sb", d_Wout, 8, 1024, stg_rA)
        rru2 = d_Wup.rearrange("(i p) o -> p i o", p=128)
        for i2 in range(2):
            for hh in range(2):
                stg, sn = stg_rA[hh]
                b.dma("scalar", stg[:, 0:1024],
                      rru2[:, i2, hh * 1024:(hh + 1) * 1024], writes=[sn])
                b.op("vector",
                     lambda v, stg=stg, i2=i2, hh=hh: v.tensor_copy(
                         wup_sb[:, i2 * 2048 + hh * 1024:
                                i2 * 2048 + (hh + 1) * 1024], stg[:, 0:1024]),
                     reads=[sn], writes=["wup_sb"])

        for ac in (MX, SCR, MXR):
            b.op("vector", lambda v, ac=ac: v.memset(col(ac), 0.0),
                 writes=["cols"])

        def p1_unit(qq, src, stg, sns, tc, ac):
            b.dma(qq, stg, src, writes=sns)
            b.op("vector",
                 lambda v: v.tensor_reduce(col(tc), stg, AX, ALU.add,
                                           apply_absolute_value=True),
                 reads=sns, writes=["cols"])
            b.op("vector",
                 lambda v: v.tensor_add(col(ac), col(ac), col(tc)),
                 reads=["cols"], writes=["cols"])

        NEL = float(D * DI)
        rrg1 = d_wgT.rearrange("(l p g) c -> p l (g c)", p=128, g=2)
        rrg = d_wgT.rearrange("(rb p) c -> p rb c", p=128)
        rru = d_wuT.rearrange("(rb p) c -> p rb c", p=128)
        rrd = d_wdT.rearrange("(rb p) c -> p rb c", p=128)
        wg1 = [(rrg1[:, i // 2, (i % 2) * 4096:(i % 2) * 4096 + 4096],
                hch[:, 0:4096] if i % 2 == 0 else hch[:, 4096:8192],
                ["hch0", "hch1"] if i % 2 == 0 else ["hch2", "hch3"])
               for i in range(8)]
        wd1 = [(rrd[:, i, :],
                x1ch[:, 0:1024] if i % 2 == 0 else x1ch[:, 1024:2048],
                ["x1ch"]) for i in range(32)]
        wu1 = [(rru[:, i // 4, (i % 4) * 1024:(i % 4) * 1024 + 1024],
                tmp1[:], ["tmp1"]) for i in range(32)]
        for i in range(32):
            p1_unit("gpsimd", *wd1[i], MXC, MX)
            p1_unit("sync", *wu1[i], SXI, MXR)
            if i < 8:
                p1_unit("scalar", *wg1[i], TMPC, SCR)
            if n_chunks > 0 and phase >= 1:
                if i == 7:
                    emit_proj(0, skip_load=True)
                elif i == 15 and phase >= 2:
                    emit_attnA(0, 0, skip_xtmt=True)
                elif i == 23 and phase >= 2:
                    emit_attnA(0, 1, skip_xtmt=True)
                elif i == 27 and n_chunks > 1:
                    emit_proj(1, load_q="scalar")

        def qw_bcast(ac, acol, wc):
            b.op("tensor", lambda t: t.matmul(avd[0:1, 0:1], cONE, col(ac),
                                              start=True, stop=True),
                 reads=["consts", "cols"], writes=["avd"])
            b.op("vector",
                 lambda v: v.tensor_scalar(col(ac)[0:1, :], avd[0:1, 0:1],
                                           1.0 / NEL, 1e-5, ALU.mult,
                                           ALU.max),
                 reads=["avd"], writes=["cols"])
            b.op("tensor", lambda t: t.matmul(avd[:, 1:2], ones1[0:1, :],
                                              col(ac)[0:1, :], start=True,
                                              stop=True),
                 reads=["ones1", "cols"], writes=["avd"])
            b.op("vector",
                 lambda v: v.tensor_copy(consts[:, acol:acol + 1],
                                         avd[:, 1:2]),
                 reads=["avd"], writes=["consts"])
            b.op("vector",
                 lambda v: v.reciprocal(col(wc), consts[:, acol:acol + 1]),
                 reads=["consts"], writes=["cols"])

        qw_bcast(SCR, AG, WSC)
        qw_bcast(MXR, AU, SH127)
        qw_bcast(MX, AD, S127)

        rr5g = s_wgu.rearrange("p (jb ic k) -> p jb ic k", ic=8, k=1024)
        rr5d = s_wdq.rearrange("p (jb jc o) -> p jb jc o", jc=2, o=1024)

        def p2_unit(qq, src, stg, sns, ub, un, bnb, bnn, tern, ternn,
                    wc, oap, tap, sname):
            b.dma(qq, stg, src, writes=sns)
            b.op("scalar",
                 lambda s: s.activation(ub, stg, AF.Identity, bias=0.0,
                                        scale=col(wc)),
                 reads=sns + ["cols"], writes=[un])
            b.op("vector",
                 lambda v: v.tensor_scalar(bnb, ub, -0.5, None, ALU.is_le),
                 reads=[un], writes=[bnn])
            b.op("vector",
                 lambda v: v.scalar_tensor_tensor(tern, ub, 0.5, bnb,
                                                  ALU.is_ge, ALU.subtract),
                 reads=[un, bnn], writes=[ternn])
            b.dma(qq, oap, tap, reads=[ternn], writes=[sname])

        gu2 = []
        for cb_ in range(2):
            for rb_ in range(8):
                for rrw, koff, wc in ((rrg, 0, WSC), (rru, 512, SH127)):
                    gu2.append((rrw[:, rb_, cb_ * 2048:(cb_ + 1) * 2048],
                                rb_, cb_, koff, wc))
        for i in range(32):
            stgd, snd = stg_x1h[i % 2]
            td8 = hqTs[0][:, 0:512].bitcast(F8)
            p2_unit("gpsimd", rrd[:, i, :], stgd, [snd],
                    hch[:, 4096:5120], "hch2", hch[:, 6144:7168], "hch3",
                    td8, "hqT0", S127,
                    rr5d[:, i // 2, i % 2, :], td8, "wdq_s")
            src, rb_, cb_, koff, wc = gu2[i]
            stgg, sng = stg_h01[i % 2]
            tern = hqTs[1][:, 0:1024].bitcast(F8)
            p2_unit(["scalar", "sync"][i % 2], src, stgg, [sng],
                    hch[:, 4096:6144], "hch2", hch[:, 6144:8192], "hch3",
                    tern, "hqT1", wc,
                    rr5g[:, cb_ * 4:(cb_ + 1) * 4, rb_, koff:koff + 512],
                    tern.rearrange("p (jb j) -> p jb j", j=512),
                    "wgu_sA" if cb_ == 0 else "wgu_sB")
        b.op("vector", lambda v: v.tensor_scalar(col(AG127),
                                                 consts[:, AG:AG + 1],
                                                 1.0 / 127.0, None, ALU.mult),
             reads=["consts"], writes=["cols"])
        b.op("vector", lambda v: v.tensor_mul(col(ACOMB), consts[:, AU:AU + 1],
                                              consts[:, AD:AD + 1]),
             reads=["consts"], writes=["cols"])
        b.op("vector", lambda v: v.tensor_scalar(col(ACOMB), col(ACOMB),
                                                 1.0 / (127.0 * 127.0), None,
                                                 ALU.mult),
             reads=["cols"], writes=["cols"])

        if n_chunks > 0 and phase >= 2:
            emit_tileB(0, 0)
            emit_tileB_xpose(0)
            emit_tileB(0, 1)
            if phase < 3:
                emit_tileB_xpose(1)

        for ch in range(n_chunks):
            if ch > 0:
                if phase >= 2:
                    emit_attnA(ch, 0)
                    emit_attnA(ch, 1)
            if ch + 1 < n_chunks and phase >= 1 and ch > 0:
                emit_proj(ch + 1)
            if ch > 0 and phase >= 2:
                emit_tileB(ch, 0)
                emit_tileB_xpose(0)
                emit_tileB(ch, 1)
            if phase >= 3:
                emit_ffn(ch, after_first_w0=lambda: emit_tileB_xpose(1))
            elif phase >= 2 and ch > 0:
                emit_tileB_xpose(1)

        b.final_wait("sync", "yout")
        b.emit(block, ctx, nc, sems_op)
    return nc


_CACHE = {}


def _host_prep(inputs):
    x = np.asarray(inputs["L0_attn_x"], dtype=np.float32)
    Wq = np.ascontiguousarray(np.asarray(inputs["L0_attn_Wq"], np.float32))
    bq = np.asarray(inputs["L0_attn_bq"], np.float32)
    Wdn = np.ascontiguousarray(np.asarray(inputs["L0_attn_Wdown"], np.float32))
    bdn = np.asarray(inputs["L0_attn_bdown"], np.float32)
    Wup = np.ascontiguousarray(np.asarray(inputs["L0_attn_Wup"], np.float32))
    bup = np.asarray(inputs["L0_attn_bup"], np.float32)
    Wout = np.ascontiguousarray(np.asarray(inputs["L0_attn_Wout"], np.float32))
    bout = np.asarray(inputs["L0_attn_bout"], np.float32)
    wgT = np.ascontiguousarray(np.asarray(inputs["L0_ffn_w_gate"], np.float32).T)
    wuT = np.ascontiguousarray(np.asarray(inputs["L0_ffn_w_up"], np.float32).T)
    wdT = np.ascontiguousarray(np.asarray(inputs["L0_ffn_w_down"], np.float32).T)

    bdn_cols = np.ascontiguousarray(bdn.reshape(2, 128).T)
    bout_row = np.ascontiguousarray(bout.reshape(1, D)
                                    .astype(ml_dtypes.bfloat16))
    bq_row = np.ascontiguousarray(bq.reshape(1, D).astype(ml_dtypes.bfloat16))
    bup_row = np.ascontiguousarray(bup.reshape(1, DKV)
                                   .astype(ml_dtypes.bfloat16))

    consts = np.zeros((128, 8), np.float32)
    consts[:, 0] = C2
    consts[:, 1] = C1
    consts[:, 2] = 1.0
    consts[:, 3] = 1e-5

    m_idx = np.arange(128)
    rm = m_idx // 16
    rn64 = (np.arange(64)) // 8
    NEG = float(np.float32(ml_dtypes.bfloat16(-30000.0)))
    mU = np.zeros((128, 128), np.float32)
    mV = np.zeros((128, 512), np.float32)
    for base in (0, 64):
        mU[base, :] = 1.0
        mV[base, :] = NEG
        for r in range(8):
            mU[base + 1 + r, :] = (rm == r).astype(np.float32)
            mV[base + 1 + r, :] = np.tile((rn64 == r) * (-NEG), 8)
    mU = mU.astype(ml_dtypes.bfloat16)
    mV = mV.astype(ml_dtypes.bfloat16)
    id128 = np.eye(128, dtype=ml_dtypes.bfloat16)
    id2 = np.zeros((128, 64), dtype=ml_dtypes.bfloat16)
    id2[0:64, :] = np.eye(64)
    id2[64:128, :] = np.eye(64)
    ones_row = np.ones((1, 256), ml_dtypes.bfloat16)

    shared = {
        "Wq": Wq, "Wdn": Wdn, "Wup": Wup, "Wout": Wout,
        "bdn_cols": bdn_cols, "bout_row": bout_row,
        "bq_row": bq_row, "bup_row": bup_row,
        "wgT": wgT, "wuT": wuT, "wdT": wdT,
        "consts": consts, "mU": mU, "mV": mV,
        "id128": id128, "id2": id2,
        "ones_row": ones_row,
    }
    in_maps = []
    for bidx in range(NCORE):
        m = dict(shared)
        m["xT"] = np.ascontiguousarray(x[bidx].T)
        m["xtm"] = np.ascontiguousarray(x[bidx])
        in_maps.append(m)
    return in_maps


def kernel(**inputs):
    if "nc" not in _CACHE:
        _CACHE["nc"] = build_nc()
    nc = _CACHE["nc"]
    in_maps = _host_prep(inputs)
    res = run_bass_kernel_spmd(nc, in_maps, core_ids=list(range(NCORE)))
    out = np.stack([np.asarray(r["y"], dtype=np.float32) for r in res.results],
                   axis=0)
    return out


# version 23
# speedup vs baseline: 1.0429x; 1.0042x over previous
import numpy as np
import ml_dtypes
import concourse.bass as bass
import concourse.mybir as mybir
from concourse.bass_utils import run_bass_kernel_spmd
from contextlib import ExitStack

F32 = mybir.dt.float32
BF16 = mybir.dt.bfloat16
F8 = mybir.dt.float8e4
AF = mybir.ActivationFunctionType
ALU = mybir.AluOpType
AX = mybir.AxisListType.X

T, D, NH, DH, DLAT, DKV, DI = 4096, 1024, 16, 64, 256, 2048, 4096
NCORE = 8
TCH = 256
NCHUNK = T // TCH
TPC = TCH // 128
JB = 512
NJB = DI // JB
C1 = 12582912.0
C2 = 6755399441055744.0

ENG = ("tensor", "vector", "scalar", "gpsimd", "sync")


class B:

    def __init__(self):
        self.items = {e: [] for e in ENG}
        self.nop = {e: 0 for e in ENG}
        self.ndma = {}
        self.writer = {}
        self.readers = {}

    def _need(self, waits, eng, entry, is_dma):
        kind, e2, i = entry
        if (kind == "op" and e2 == eng and eng == "tensor" and not is_dma):
            return
        key = (kind, e2)
        v = (i + 1) if kind == "op" else 16 * (i + 1)
        if waits.get(key, 0) < v:
            waits[key] = v

    def _deps(self, eng, reads, writes, is_dma):
        waits = {}
        for b_ in reads:
            w = self.writer.get(b_)
            if w is not None:
                self._need(waits, eng, w, is_dma)
        for b_ in writes:
            w = self.writer.get(b_)
            if w is not None:
                self._need(waits, eng, w, is_dma)
            for r in self.readers.get(b_, ()):
                self._need(waits, eng, r, is_dma)
        return waits

    def _record(self, entry, reads, writes):
        for b_ in reads:
            self.readers.setdefault(b_, []).append(entry)
        for b_ in writes:
            self.writer[b_] = entry
            self.readers[b_] = []

    def op(self, eng, emit, reads=(), writes=()):
        waits = self._deps(eng, reads, writes, False)
        i = self.nop[eng]
        self.items[eng].append((waits, emit, "op"))
        self.nop[eng] = i + 1
        self._record(("op", eng, i), reads, writes)

    def dma(self, q, out_ap, in_ap, reads=(), writes=(), stream=None):
        stream = stream or (writes[0] if writes else "yout")
        waits = self._deps(q, reads, writes, True)
        i = self.ndma.get(stream, 0)
        self.items[q].append((waits, (out_ap, in_ap, stream), "dma"))
        self.ndma[stream] = i + 1
        self._record(("dma", stream, i), reads, writes)

    def final_wait(self, eng, stream):
        n = self.ndma.get(stream, 0)
        self.items[eng].append(({("dma", stream): 16 * n}, None, "wait_only"))

    def emit(self, block, ctx, nc, sems_op):
        sems_dma = {}
        for s in self.ndma:
            sems_dma[s] = ctx.enter_context(nc.semaphore("sdm_%s" % s))
        handles = {"vector": block.vector, "scalar": block.scalar,
                   "tensor": block.tensor, "gpsimd": block.gpsimd,
                   "sync": block.sync}
        for e in ENG:
            items = self.items[e]
            if not items:
                continue

            def make(eng_name, eng_items):
                def _(eng):
                    hw = {}
                    for waits, payload, kind in eng_items:
                        for key in sorted(waits):
                            v = waits[key]
                            if hw.get(key, 0) >= v:
                                continue
                            k2, e2 = key
                            sem = sems_op[e2] if k2 == "op" else sems_dma[e2]
                            eng.wait_ge(sem, v)
                            hw[key] = v
                        if kind == "op":
                            payload(eng).then_inc(sems_op[eng_name], 1)
                        elif kind == "dma":
                            out_ap, in_ap, stream = payload
                            eng.dma_start(out=out_ap, in_=in_ap).then_inc(
                                sems_dma[stream], 16)
                return _
            handles[e](make(e, items))


def build_nc(n_chunks=NCHUNK, phase=3):
    nc = bass.Bass("TRN2", target_bir_lowering=False, debug=False)

    d_xT = nc.dram_tensor("xT", [D, T], F32, kind="ExternalInput").ap()
    d_xtm = nc.dram_tensor("xtm", [T, D], F32, kind="ExternalInput").ap()
    d_Wq = nc.dram_tensor("Wq", [D, D], F32, kind="ExternalInput").ap()
    d_Wdn = nc.dram_tensor("Wdn", [D, DLAT], F32, kind="ExternalInput").ap()
    d_Wup = nc.dram_tensor("Wup", [DLAT, DKV], F32, kind="ExternalInput").ap()
    d_Wout = nc.dram_tensor("Wout", [D, D], F32, kind="ExternalInput").ap()
    d_bdn = nc.dram_tensor("bdn_cols", [128, 2], F32, kind="ExternalInput").ap()
    d_bout = nc.dram_tensor("bout_row", [1, D], BF16, kind="ExternalInput").ap()
    d_bq_row = nc.dram_tensor("bq_row", [1, D], BF16, kind="ExternalInput").ap()
    d_bup_row = nc.dram_tensor("bup_row", [1, DKV], BF16,
                               kind="ExternalInput").ap()
    d_wgT = nc.dram_tensor("wgT", [D, DI], F32, kind="ExternalInput").ap()
    d_wuT = nc.dram_tensor("wuT", [D, DI], F32, kind="ExternalInput").ap()
    d_wdT = nc.dram_tensor("wdT", [DI, D], F32, kind="ExternalInput").ap()
    d_consts = nc.dram_tensor("consts", [128, 8], F32, kind="ExternalInput").ap()
    d_mU = nc.dram_tensor("mU", [128, 128], BF16, kind="ExternalInput").ap()
    d_mV = nc.dram_tensor("mV", [128, 512], BF16, kind="ExternalInput").ap()
    d_id128 = nc.dram_tensor("id128", [128, 128], BF16, kind="ExternalInput").ap()
    d_id2 = nc.dram_tensor("id2", [128, 64], BF16, kind="ExternalInput").ap()
    d_ones = nc.dram_tensor("ones_row", [1, 256], BF16,
                            kind="ExternalInput").ap()
    d_y = nc.dram_tensor("y", [T, D], F32, kind="ExternalOutput").ap()
    s_wgu = nc.dram_tensor("wgu_s", [128, 65536], F8, kind="Internal").ap()
    s_wdq = nc.dram_tensor("wdq_s", [128, 32768], F8, kind="Internal").ap()

    with ExitStack() as ctx:
        def sb(name, shape, dtype=F32):
            return ctx.enter_context(nc.sbuf_tensor(name, shape, dtype))

        def psum(name, shape, dtype=F32):
            return ctx.enter_context(nc.psum_tensor(name, shape, dtype))

        consts = sb("consts_sb", [128, 8])
        id128 = sb("id128t", [128, 128], BF16)
        id2 = sb("id2t", [128, 64], BF16)
        mU = sb("mUt", [128, 128], BF16)
        mV = sb("mVt", [128, 512], BF16)
        ones_row = sb("ones_rowt", [1, 256], BF16)
        ones1 = sb("ones1t", [1, 128])
        bout_row = sb("bout_rowt", [1, D], BF16)
        bq_row = sb("bq_rowt", [1, D], BF16)
        bup_row = sb("bup_rowt", [1, DKV], BF16)
        bdn_cols = sb("bdn_colst", [128, 2])
        wq_sb = sb("wq_sb", [128, 8192], BF16)
        wdn_sb = sb("wdn_sb", [128, 2048], BF16)
        wup_sb = sb("wup_sb", [128, 4096], BF16)
        wout_sb = sb("wout_sb", [128, 8192], BF16)
        cols = sb("colst", [128, 48])
        rdt = sb("rdt", [128, 16])
        xstg = sb("xstgt", [128, 2048])
        xtb = sb("xtb", [128, 2048], BF16)
        kvlT = sb("kvlT", [128, 512], BF16)
        qT = sb("qTt", [128, TCH * 8], BF16)
        kvT = sb("kvTt", [128, TCH * 16], BF16)
        kdupT = sb("kdupTt", [128, TCH * 16], BF16)
        eM = sb("eMt", [128, 2048], BF16)
        vaug = sb("vaugt", [128, 1024], BF16)
        onecol = sb("onecolt", [128, 1], BF16)
        aoP = eM[:, 1024:2048]
        aoTs = eM[:, 0:1024]
        xq = eM[:, 0:1024]
        xtmt = sb("xtmtt", [128, 1024])
        xtmt2 = sb("xtmt2t", [128, 1024])
        rA = [sb("r1a", [128, 1024]), sb("r1b", [128, 1024])]
        tmp1 = sb("tmp1t", [128, 1024])
        x1ch = sb("x1cht", [128, 2048])
        xqTs = [sb("xqTs0", [128, 1024], BF16), sb("xqTs1", [128, 1024], BF16)]
        sgate = sb("sgatet", [128, JB])
        hch = sb("hcht", [128, 2 * DI])
        hqb2 = [sb("hqb0", [128, 1024], BF16), sb("hqb1", [128, 1024], BF16)]
        hqTs = [sb("hqTst0", [128, DI], BF16), sb("hqTst1", [128, DI], BF16)]
        wgu = [sb("wgu%d" % i, [128, 4096], F8) for i in range(4)]
        wdt = [sb("wdt%d" % i, [128, 2048], F8) for i in range(5)]
        bns = sb("bnst", [128, 12])
        bnag = sb("bnagt", [128, 2])

        cC2 = consts[:, 0:1]
        cC1 = consts[:, 1:2]
        cONE = consts[:, 2:3]
        cEPS = consts[:, 3:4]
        AG, AU, AD = 5, 6, 7

        def col(i):
            return cols[:, i:i + 1]
        (IND, INDC, WSC, AG127, ACOMB, MU, VAR_, LCOL, RSTD, NEGMUR, TMPC,
         MX, MXC, SX, S127, SXI, MXR, MX1C0, MX1C1, SG0, SG1,
         MH, MHC, SH127, SHI, YSC0, YSC1, SCR) = range(28)
        HP0 = 32

        psc0 = psum("psc0", [128, 512])
        psc1 = psum("psc1", [128, 512])
        pbt = psum("pbt", [128, 1024], BF16)
        avp = psum("avp", [128, 1024])
        avd = psum("avd", [128, 512])
        pp0 = psum("pp0", [128, 512])
        pp1 = psum("pp1", [128, 512])

        sems_op = {e: ctx.enter_context(nc.semaphore("so_" + e)) for e in ENG}
        block = ctx.enter_context(nc.Block())
        b = B()

        def emit_proj(ch, skip_load=False, load_q="sync"):
            t0 = ch * TCH
            if not skip_load:
                b.dma(load_q, xstg[:].rearrange("p (ic t) -> p ic t", t=TCH),
                      d_xT.rearrange("(ic p) t -> p ic t", p=128)[:, :,
                                                                 t0:t0 + TCH],
                      writes=["xstg"])
            b.op("vector", lambda v: v.tensor_copy(xtb[:], xstg[:]),
                 reads=["xstg"], writes=["xtb"])

            for lc in range(2):
                ps, pn = ([pp0, "pp0"], [pp1, "pp1"])[lc]
                for ic in range(8):
                    b.op("tensor",
                         lambda t, ps=ps, ic=ic, lc=lc: t.matmul(
                             ps[:, 0:TCH],
                             wdn_sb[:, ic * 256 + lc * 128:
                                    ic * 256 + lc * 128 + 128],
                             xtb[:, ic * TCH:(ic + 1) * TCH],
                             start=(ic == 0), stop=(ic == 7)),
                         reads=["wdn_sb", "xtb"], writes=[pn])
                b.op("scalar",
                     lambda s, ps=ps, lc=lc: s.activation(
                         kvlT[:, lc * TCH:(lc + 1) * TCH], ps[:, 0:TCH],
                         AF.Identity, bias=bdn_cols[:, lc:lc + 1], scale=1.0),
                     reads=[pn, "bdn"], writes=["kvlT"])

            for quad in range(2):
                for oc4 in range(4):
                    oc = quad * 4 + oc4
                    for ic in range(8):
                        b.op("tensor",
                             lambda t, ic=ic, oc=oc, oc4=oc4: t.matmul(
                                 avp[:, oc4 * 256: oc4 * 256 + TCH],
                                 wq_sb[:, ic * 1024 + oc * 128:
                                       ic * 1024 + oc * 128 + 128],
                                 xtb[:, ic * TCH:(ic + 1) * TCH],
                                 start=(ic == 0), stop=False),
                             reads=["wq_sb", "xtb"], writes=["avp"])
                    b.op("tensor",
                         lambda t, oc=oc, oc4=oc4: t.matmul(
                             avp[:, oc4 * 256: oc4 * 256 + TCH],
                             bq_row[0:1, oc * 128:(oc + 1) * 128],
                             ones_row[0:1, 0:TCH],
                             start=False, stop=True),
                         reads=["bq_row", "ones_row"], writes=["avp"])
                qdst = qT[:].rearrange("p (t c) -> p t c", c=8)[
                    :, :, quad * 4: quad * 4 + 4]
                qsrc = avp[:].rearrange("p (c t) -> p c t", t=256) \
                    .transpose([0, 2, 1])
                b.op("scalar",
                     lambda s, qdst=qdst, qsrc=qsrc: s.activation(
                         qdst, qsrc, AF.Identity, bias=0.0, scale=1.0),
                     reads=["avp"], writes=["qT"])

            for hq4 in range(4):
                for hh in range(4):
                    h = hq4 * 4 + hh
                    psA = [pp0, pp1][hh // 2]
                    pnA = ["pp0", "pp1"][hh // 2]
                    psB = [psc0, psc1][hh // 2]
                    pnB = ["psc0", "psc1"][hh // 2]
                    co = (hh % 2) * 256
                    for lc in range(2):
                        b.op("tensor",
                             lambda t, h=h, lc=lc, psA=psA, co=co: t.matmul(
                                 psA[:, co:co + TCH],
                                 wup_sb[:, lc * 2048 + h * 128:
                                        lc * 2048 + h * 128 + 128],
                                 kvlT[:, lc * TCH:(lc + 1) * TCH],
                                 start=(lc == 0), stop=False),
                             reads=["wup_sb", "kvlT"], writes=[pnA])
                    b.op("tensor",
                         lambda t, h=h, psA=psA, co=co: t.matmul(
                             psA[:, co:co + TCH],
                             bup_row[0:1, h * 128:(h + 1) * 128],
                             ones_row[0:1, 0:TCH],
                             start=False, stop=True),
                         reads=["bup_row", "ones_row"], writes=[pnA])
                    for lc in range(2):
                        b.op("tensor",
                             lambda t, h=h, lc=lc, psB=psB, co=co: t.matmul(
                                 psB[64:128, co:co + TCH],
                                 wup_sb[:, lc * 2048 + h * 128:
                                        lc * 2048 + h * 128 + 64],
                                 kvlT[:, lc * TCH:(lc + 1) * TCH],
                                 start=(lc == 0), stop=False),
                             reads=["wup_sb", "kvlT"], writes=[pnB])
                    b.op("tensor",
                         lambda t, h=h, psB=psB, co=co: t.matmul(
                             psB[64:128, co:co + TCH],
                             bup_row[0:1, h * 128: h * 128 + 64],
                             ones_row[0:1, 0:TCH],
                             start=False, stop=True),
                         reads=["bup_row", "ones_row"], writes=[pnB])
                for bk in range(2):
                    psA = [pp0, pp1][bk]
                    pnA = ["pp0", "pp1"][bk]
                    psB = [psc0, psc1][bk]
                    pnB = ["psc0", "psc1"][bk]
                    g0 = hq4 * 4 + bk * 2
                    kvdst = kvT[:].rearrange("p (t g) -> p t g", g=16)[
                        :, :, g0:g0 + 2]
                    src_a = psA[:].rearrange("p (g t) -> p g t", t=256) \
                        .transpose([0, 2, 1])
                    b.op("scalar",
                         lambda s, kvdst=kvdst, src_a=src_a: s.activation(
                             kvdst, src_a, AF.Identity, bias=0.0, scale=1.0),
                         reads=[pnA], writes=["kvT"])
                    kddst = kdupT[64:128, :].rearrange(
                        "p (t g) -> p t g", g=16)[:, :, g0:g0 + 2]
                    src_b = psB[64:128, :].rearrange(
                        "p (g t) -> p g t", t=256).transpose([0, 2, 1])
                    b.op("scalar",
                         lambda s, kddst=kddst, src_b=src_b: s.activation(
                             kddst, src_b, AF.Identity, bias=0.0, scale=1.0),
                         reads=[pnB], writes=["kdupT"])

        def emit_attnA_qk(ch, w, part=None):
            tok0 = w * 128
            for b8 in range(2):
                if part is not None and part != b8:
                    continue
                for i8 in range(8):
                    s_ = b8 * 8 + i8
                    S16 = (tok0 + 8 * s_) * 16
                    S8 = (tok0 + 8 * s_) * 8
                    b.op("tensor",
                         lambda t, i8=i8, S16=S16, S8=S8: t.matmul(
                             psc0[:, i8 * 64:(i8 + 1) * 64],
                             kvT[0:64, S16:S16 + 128],
                             qT[0:64, S8:S8 + 64],
                             start=(i8 == 0), stop=False),
                         reads=["kvT", "qT"], writes=["psc0"])
                    b.op("tensor",
                         lambda t, i8=i8, S16=S16, S8=S8: t.matmul(
                             psc1[:, i8 * 64:(i8 + 1) * 64],
                             kdupT[64:128, S16:S16 + 128],
                             qT[64:128, S8:S8 + 64],
                             start=(i8 == 0), stop=False),
                         reads=["kdupT", "qT"], writes=["psc1"])
                b.op("tensor",
                     lambda t: t.matmul(psc0[:], mU[0:64, :], mV[0:64, :],
                                        start=False, stop=True),
                     reads=["mU", "mV"], writes=["psc0"])
                b.op("tensor",
                     lambda t: t.matmul(psc1[:], mU[64:128, :],
                                        mV[64:128, :],
                                        start=False, stop=True),
                     reads=["mU", "mV"], writes=["psc1"])
                eMb = eM[:, b8 * 1024:(b8 + 1) * 1024].rearrange(
                    "p (s c) -> p s c", c=128)
                b.op("scalar",
                     lambda s, eMb=eMb: s.activation(
                         eMb[:, :, 0:64], psc0[:], AF.Exp, bias=0.0,
                         scale=0.125),
                     reads=["psc0"], writes=["eM"])
                b.op("scalar",
                     lambda s, eMb=eMb: s.activation(
                         eMb[:, :, 64:128], psc1[:], AF.Exp, bias=0.0,
                         scale=0.125),
                     reads=["psc1"], writes=["eM"])

        def emit_attnA(ch, w, skip_xtmt=False, skip_qk=False, xtmt_q="sync"):
            tglob = ch * TPC + w
            tok0 = w * 128
            xtb_, xtn_ = [(xtmt, "xtmt"), (xtmt2, "xtmt2")][tglob % 2]
            if not skip_xtmt:
                b.dma(xtmt_q, xtb_[:],
                      d_xtm[tglob * 128:(tglob + 1) * 128, :],
                      writes=[xtn_])
            if not skip_qk:
                emit_attnA_qk(ch, w)

            for s_ in range(16):
                S16 = (tok0 + 8 * s_) * 16
                b.op("tensor",
                     lambda t, s_=s_, S16=S16: t.transpose(
                         pbt[:, s_ * 64:(s_ + 1) * 64],
                         kvT[64:128, S16:S16 + 128],
                         id2[64:128, 0:64]),
                     reads=["kvT", "id2"], writes=["pbt"])
            b.op("vector", lambda v: v.tensor_copy(vaug[:], pbt[:]),
                 reads=["pbt"], writes=["vaug"])

            for s_ in range(16):
                b.op("tensor",
                     lambda t, s_=s_: t.matmul(
                         avp[:, s_ * 64:(s_ + 1) * 64],
                         eM[:, s_ * 128:(s_ + 1) * 128],
                         vaug[:, s_ * 64:(s_ + 1) * 64],
                         start=True, stop=True),
                     reads=["eM", "vaug"], writes=["avp"])
                b.op("tensor",
                     lambda t, s_=s_: t.matmul(
                         avd[:, s_:s_ + 1],
                         eM[:, s_ * 128:(s_ + 1) * 128],
                         onecol[:],
                         start=True, stop=True),
                     reads=["eM", "onecol"], writes=["avd"])
            b.op("vector", lambda v: v.reciprocal(rdt[:], avd[:, 0:16]),
                 reads=["avd"], writes=["rdt"])
            b.op("vector", lambda v: v.tensor_mul(
                aoP.rearrange("p (s j) -> p s j", j=64),
                avp[:].rearrange("p (s j) -> p s j", j=64),
                rdt[:].unsqueeze(2).broadcast_to([128, 16, 64])),
                 reads=["avp", "rdt", "eM"], writes=["eM"])

            for p_ in (1, 0):
                for s_ in range(16):
                    b.op("tensor",
                         lambda t, p_=p_, s_=s_: t.transpose(
                             pbt[p_ * 64:(p_ + 1) * 64,
                                 s_ * 64:(s_ + 1) * 64],
                             aoP[p_ * 64:(p_ + 1) * 64,
                                 s_ * 64:(s_ + 1) * 64],
                             id2[p_ * 64:(p_ + 1) * 64, 0:64]),
                         reads=["eM", "id2"], writes=["pbt"])
                b.op("vector",
                     lambda v, p_=p_: v.tensor_copy(
                         aoTs[p_ * 64:(p_ + 1) * 64, :],
                         pbt[p_ * 64:(p_ + 1) * 64, :]),
                     reads=["pbt"], writes=["eM"])

            aoTr = aoTs.rearrange("p (t c) -> p t c", c=8)
            for cs in range(8):
                for half in range(2):
                    ps, pn = ([pp0, "pp0"], [pp1, "pp1"])[half]
                    b.op("tensor",
                         lambda t, ps=ps, cs=cs, half=half: t.matmul(
                             ps[:],
                             aoTr[:, :, cs],
                             wout_sb[:, cs * 1024 + half * 512:
                                     cs * 1024 + half * 512 + 512],
                             start=(cs == 0), stop=False),
                         reads=["eM", "wout_sb"], writes=[pn])
            for half in range(2):
                ps, pn = ([pp0, "pp0"], [pp1, "pp1"])[half]
                b.op("tensor",
                     lambda t, ps=ps, half=half: t.matmul(
                         ps[:], ones_row[0:1, 0:128],
                         bout_row[0:1, half * 512:(half + 1) * 512],
                         start=False, stop=True),
                     reads=["ones_row", "bout_row"], writes=[pn])
            for half in range(2):
                ps, pn = ([pp0, "pp0"], [pp1, "pp1"])[half]
                b.op("vector",
                     lambda v, ps=ps, half=half, w=w: v.tensor_add(
                         rA[w][:, half * 512:(half + 1) * 512], ps[:],
                         xtb_[:, half * 512:(half + 1) * 512]),
                     reads=[pn, xtn_], writes=["rA%d" % w])

        def emit_tileB_xpose(w):
            for ic in range(8):
                b.op("tensor",
                     lambda t, ic=ic: t.transpose(
                         pbt[:, ic * 128:(ic + 1) * 128],
                         xq[:, ic * 128:(ic + 1) * 128], id128[:]),
                     reads=["eM", "id128"], writes=["pbt"])
            b.op("vector",
                 lambda v, w=w: v.tensor_copy(xqTs[w][:], pbt[:]),
                 reads=["pbt"], writes=["xqT%d" % w])

        def emit_tileB(ch, w):
            rw = rA[w]
            rn = "rA%d" % w
            b.op("vector", lambda v: v.bn_stats(bns[:, 0:6], rw[:, 0:512]),
                 reads=[rn], writes=["bns"])
            b.op("vector", lambda v: v.bn_stats(bns[:, 6:12], rw[:, 512:1024]),
                 reads=[rn], writes=["bns"])
            b.op("vector", lambda v: v.bn_aggr(bnag[:], bns[:]),
                 reads=["bns"], writes=["bnag"])
            b.op("scalar", lambda s: s.activation(col(LCOL), bnag[:, 1:2],
                                                  AF.Ln, bias=cEPS, scale=1.0),
                 reads=["bnag", "consts"], writes=["cols"])
            b.op("scalar", lambda s: s.activation(col(RSTD), col(LCOL),
                                                  AF.Exp, bias=0.0, scale=-0.5),
                 reads=["cols"], writes=["cols"])
            b.op("vector", lambda v: v.tensor_mul(col(TMPC), bnag[:, 0:1],
                                                  col(RSTD)),
                 reads=["bnag", "cols"], writes=["cols"])
            b.op("vector", lambda v: v.tensor_scalar(col(NEGMUR), col(TMPC),
                                                     -1.0, None, ALU.mult),
                 reads=["cols"], writes=["cols"])
            b.op("scalar",
                 lambda s, w=w: s.activation(
                     x1ch[:, w * 1024:(w + 1) * 1024], rw[:], AF.Identity,
                     bias=col(NEGMUR), scale=col(RSTD)),
                 reads=[rn, "cols"], writes=["x1ch"])
            b.op("vector", lambda v, rw=rw: v.tensor_scalar(
                rw[:], rw[:], bnag[:, 0:1], None, ALU.subtract),
                 reads=[rn, "bnag"], writes=[rn])
            b.op("vector", lambda v, rw=rw: v.tensor_reduce(
                col(MX), rw[:], AX, ALU.max, apply_absolute_value=True),
                 reads=[rn], writes=["cols"])
            b.op("vector", lambda v: v.tensor_scalar(col(MXC), col(MX), 1e-5,
                                                     None, ALU.max),
                 reads=["cols"], writes=["cols"])
            b.op("vector", lambda v: v.reciprocal(col(SX), col(MXC)),
                 reads=["cols"], writes=["cols"])
            b.op("vector", lambda v: v.tensor_scalar(col(S127), col(SX), 127.0,
                                                     None, ALU.mult),
                 reads=["cols"], writes=["cols"])
            b.op("vector", lambda v: v.tensor_mul(col(SXI), col(S127),
                                                  col(INDC)),
                 reads=["cols"], writes=["cols"])
            b.op("scalar", lambda s, rw=rw: s.activation(
                tmp1[:], rw[:], AF.Identity, bias=cC2, scale=col(S127)),
                 reads=[rn, "cols", "consts"], writes=["tmp1"])
            b.op("scalar", lambda s, rw=rw: s.activation(
                xtmt[:], rw[:], AF.Identity, bias=cC1, scale=col(SXI)),
                 reads=[rn, "cols", "consts"], writes=["xtmt"])
            b.op("vector", lambda v: v.tensor_scalar(
                xtmt[:], xtmt[:], -C1, None, ALU.add),
                 reads=["xtmt"], writes=["xtmt"])
            b.op("vector", lambda v: v.scalar_tensor_tensor(
                xq, tmp1[:], -C2, xtmt[:], ALU.add, ALU.add),
                 reads=["tmp1", "xtmt"], writes=["eM"])
            b.op("vector", lambda v: v.tensor_mul(col(MXR), col(MX),
                                                  col(RSTD)),
                 reads=["cols"], writes=["cols"])
            b.op("vector",
                 lambda v, w=w: v.tensor_scalar(
                     col([MX1C0, MX1C1][w]), col(MXR), 1e-5, None, ALU.max),
                 reads=["cols"], writes=["cols"])
            b.op("vector",
                 lambda v, w=w: v.tensor_mul(
                     col([SG0, SG1][w]), col([MX1C0, MX1C1][w]), col(AG127)),
                 reads=["cols"], writes=["cols"])

        gbanks = [(pp0[:], "pp0"), (avp[:, 0:512], "avp"), (psc0[:], "psc0")]
        ubanks = [(pp1[:], "pp1"), (avp[:, 512:1024], "avp"), (psc1[:], "psc1")]

        def emit_ffn(ch, after_first_w0=None, qk_next=None):
            for jb in range(NJB):
                for icH in range(2):
                    u4 = (jb * 2 + icH) % 4
                    b.dma(["sync", "scalar"][icH], wgu[u4][:],
                          s_wgu[:, jb * 8192 + icH * 4096:
                                jb * 8192 + (icH + 1) * 4096],
                          reads=["wgu_sA" if jb < 4 else "wgu_sB"],
                          writes=["wgu%d" % u4])
                for w in range(TPC):
                    k = jb * 2 + w
                    gb, gn = gbanks[k % 3]
                    ub, un = ubanks[k % 3]
                    for ic in range(8):
                        icH, icl = ic // 4, ic % 4
                        u4 = (jb * 2 + icH) % 4
                        b.op("tensor",
                             lambda t, w=w, ic=ic, u4=u4, icl=icl,
                             gb=gb: t.matmul(
                                 gb,
                                 xqTs[w][:, ic * 128:(ic + 1) * 128],
                                 wgu[u4][:, icl * 1024:icl * 1024 + 512],
                                 start=(ic == 0), stop=(ic == 7)),
                             reads=["xqT%d" % w, "wgu%d" % u4],
                             writes=[gn])
                        b.op("tensor",
                             lambda t, w=w, ic=ic, u4=u4, icl=icl,
                             ub=ub: t.matmul(
                                 ub,
                                 xqTs[w][:, ic * 128:(ic + 1) * 128],
                                 wgu[u4][:, icl * 1024 + 512:
                                          (icl + 1) * 1024],
                                 start=(ic == 0), stop=(ic == 7)),
                             reads=["xqT%d" % w, "wgu%d" % u4],
                             writes=[un])
                    b.op("scalar",
                         lambda s, w=w, gb=gb: s.activation(
                             sgate[:], gb, AF.Silu, bias=0.0,
                             scale=col([SG0, SG1][w])),
                         reads=[gn, "cols"], writes=["sgate"])
                    hreg = "hch%d" % ((w * DI + jb * JB) // 2048)
                    b.op("vector",
                         lambda v, w=w, jb=jb, ub=ub: v.tensor_mul(
                             hch[:, w * DI + jb * JB: w * DI + (jb + 1) * JB],
                             sgate[:], ub),
                         reads=["sgate", un], writes=[hreg])
                    b.op("vector",
                         lambda v, w=w, jb=jb: v.tensor_reduce(
                             col(HP0 + w * 8 + jb),
                             hch[:, w * DI + jb * JB: w * DI + (jb + 1) * JB],
                             AX, ALU.max, apply_absolute_value=True),
                         reads=[hreg], writes=["cols"])
                    if jb == 0 and w == 0 and after_first_w0 is not None:
                        after_first_w0()

            def hquant_cols(w):
                b.op("vector",
                     lambda v, w=w: v.tensor_reduce(
                         col(MH), cols[:, HP0 + w * 8:HP0 + w * 8 + 8], AX,
                         ALU.max),
                     reads=["cols"], writes=["cols"])
                b.op("vector", lambda v: v.tensor_scalar(col(MHC), col(MH),
                                                         1e-5, None, ALU.max),
                     reads=["cols"], writes=["cols"])
                b.op("vector", lambda v: v.reciprocal(col(SCR), col(MHC)),
                     reads=["cols"], writes=["cols"])
                b.op("vector", lambda v: v.tensor_scalar(col(SH127), col(SCR),
                                                         127.0, None,
                                                         ALU.mult),
                     reads=["cols"], writes=["cols"])
                b.op("vector", lambda v: v.tensor_mul(col(SHI), col(SH127),
                                                      col(INDC)),
                     reads=["cols"], writes=["cols"])

            def hquant_qb(w, qb):
                hr = ["hch%d" % (w * 2), "hch%d" % (w * 2 + 1)]
                hsl = hch[:, w * DI + qb * 1024: w * DI + (qb + 1) * 1024]
                hb, hn = hqb2[qb % 2], "hqb%d" % (qb % 2)
                tt, tn = [(tmp1, "tmp1"), (xtmt, "xtmt")][qb % 2]
                tm2, tn2 = [(rA[0], "rA0"), (rA[1], "rA1")][qb % 2]
                b.op("scalar",
                     lambda s, hsl=hsl, tt=tt: s.activation(
                         tt[:], hsl, AF.Identity, bias=cC2,
                         scale=col(SH127)),
                     reads=hr + ["cols", "consts"], writes=[tn])
                b.op("scalar",
                     lambda s, hsl=hsl, tm2=tm2: s.activation(
                         tm2[:], hsl, AF.Identity, bias=cC1,
                         scale=col(SHI)),
                     reads=hr + ["cols", "consts"], writes=[tn2])
                b.op("vector", lambda v, tm2=tm2: v.tensor_scalar(
                    tm2[:], tm2[:], -C1, None, ALU.add),
                     reads=[tn2], writes=[tn2])
                b.op("vector",
                     lambda v, hb=hb, tt=tt,
                     tm2=tm2: v.scalar_tensor_tensor(
                         hb[:], tt[:], -C2,
                         tm2[:], ALU.add, ALU.add),
                     reads=[tn, tn2], writes=[hn])
                for jj in range(8):
                    b.op("tensor",
                         lambda t, hb=hb, jj=jj: t.transpose(
                             pbt[:, jj * 128:(jj + 1) * 128],
                             hb[:, jj * 128:(jj + 1) * 128], id128[:]),
                         reads=[hn, "id128"], writes=["pbt"])
                b.op("vector",
                     lambda v, qb=qb, w=w: v.tensor_copy(
                         hqTs[w][:, qb * 1024:(qb + 1) * 1024],
                         pbt[:]),
                     reads=["pbt"], writes=["hqT%d" % w])

            def hquant_ysc(w):
                b.op("vector",
                     lambda v, w=w: v.tensor_mul(
                         col([YSC0, YSC1][w]), col(MHC),
                         col([MX1C0, MX1C1][w])),
                     reads=["cols"], writes=["cols"])
                b.op("vector",
                     lambda v, w=w: v.tensor_mul(
                         col([YSC0, YSC1][w]), col([YSC0, YSC1][w]),
                         col(ACOMB)),
                     reads=["cols"], writes=["cols"])

            for w in range(TPC):
                hquant_cols(w)
                for qb in range(4):
                    hquant_qb(w, qb)
                hquant_ysc(w)

            y2s = [[avp[:, 0:512], avp[:, 512:1024]], [pp0[:], pp1[:]]]
            y2ns = [["avp", "avp"], ["pp0", "pp1"]]
            for jb2 in range(16):
                wd_t, wd_n = wdt[jb2 % 5], "wdt%d" % (jb2 % 5)
                b.dma(["gpsimd", "sync", "gpsimd", "sync",
                       "gpsimd"][jb2 % 5], wd_t[:],
                      s_wdq[:, jb2 * 2048:(jb2 + 1) * 2048],
                      reads=["wdq_s"], writes=[wd_n])
                for w in range(TPC):
                    for jc in range(2):
                        jg = jb2 * 2 + jc
                        for half in range(2):
                            b.op("tensor",
                                 lambda t, w=w, jg=jg, jc=jc, half=half,
                                 wd_t=wd_t: t.matmul(
                                     y2s[w][half],
                                     hqTs[w][:, jg * 128:(jg + 1) * 128],
                                     wd_t[:, jc * 1024 + half * 512:
                                          jc * 1024 + half * 512 + 512],
                                     start=(jg == 0), stop=(jg == 31)),
                                 reads=["hqT%d" % w, wd_n],
                                 writes=[y2ns[w][half]])
            for w in range(TPC):
                rw = rA[w]
                rn = "rA%d" % w
                for half in range(2):
                    b.op("vector",
                         lambda v, w=w, half=half, rw=rw: v.scalar_tensor_tensor(
                             rw[:, half * 512:(half + 1) * 512], y2s[w][half],
                             col([YSC0, YSC1][w]),
                             x1ch[:, w * 1024 + half * 512:
                                  w * 1024 + half * 512 + 512],
                             ALU.mult, ALU.add),
                         reads=[y2ns[w][half], "x1ch", "cols"], writes=[rn])
                b.op("vector", lambda v, rw=rw: v.bn_stats(bns[:, 0:6],
                                                           rw[:, 0:512]),
                     reads=[rn], writes=["bns"])
                b.op("vector", lambda v, rw=rw: v.bn_stats(bns[:, 6:12],
                                                           rw[:, 512:1024]),
                     reads=[rn], writes=["bns"])
                b.op("vector", lambda v: v.bn_aggr(bnag[:], bns[:]),
                     reads=["bns"], writes=["bnag"])
                b.op("scalar", lambda s: s.activation(col(LCOL), bnag[:, 1:2],
                                                      AF.Ln, bias=cEPS,
                                                      scale=1.0),
                     reads=["bnag", "consts"], writes=["cols"])
                b.op("scalar", lambda s: s.activation(col(RSTD), col(LCOL),
                                                      AF.Exp, bias=0.0,
                                                      scale=-0.5),
                     reads=["cols"], writes=["cols"])
                b.op("vector", lambda v: v.tensor_mul(col(TMPC), bnag[:, 0:1],
                                                      col(RSTD)),
                     reads=["bnag", "cols"], writes=["cols"])
                b.op("vector", lambda v: v.tensor_scalar(col(NEGMUR),
                                                         col(TMPC),
                                                         -1.0, None, ALU.mult),
                     reads=["cols"], writes=["cols"])
                b.op("scalar",
                     lambda s, rw=rw: s.activation(
                         tmp1[:], rw[:], AF.Identity,
                         bias=col(NEGMUR), scale=col(RSTD)),
                     reads=[rn, "cols"], writes=["tmp1"])
                tglob = ch * TPC + w
                b.dma("sync", d_y[tglob * 128:(tglob + 1) * 128, :], tmp1[:],
                      reads=["tmp1"])

        b.dma("sync", consts[:], d_consts, writes=["consts"])
        b.dma("sync", id128[:], d_id128, writes=["id128"])
        b.dma("sync", id2[:], d_id2, writes=["id2"])
        b.dma("sync", mU[:], d_mU, writes=["mU"])
        b.dma("sync", mV[:], d_mV, writes=["mV"])
        b.dma("sync", ones_row[:], d_ones, writes=["ones_row"])
        b.dma("sync", bout_row[:], d_bout, writes=["bout_row"])
        b.dma("sync", bq_row[:], d_bq_row, writes=["bq_row"])
        b.dma("sync", bup_row[:], d_bup_row, writes=["bup_row"])
        b.dma("sync", bdn_cols[:], d_bdn, writes=["bdn"])

        b.op("vector", lambda v: v.memset(col(SCR), 1.0), writes=["cols"])
        b.op("vector", lambda v: v.tensor_scalar(col(IND), col(SCR), C2, -C2,
                                                 ALU.add, ALU.add),
             reads=["cols"], writes=["cols"])
        b.op("vector", lambda v: v.tensor_scalar(col(INDC), col(IND), -1.0, 1.0,
                                                 ALU.mult, ALU.add),
             reads=["cols"], writes=["cols"])
        b.op("vector", lambda v: v.memset(onecol[:], 1.0), writes=["onecol"])
        b.op("vector", lambda v: v.memset(ones1[:], 1.0), writes=["ones1"])

        b.dma("sync", xstg[:].rearrange("p (ic t) -> p ic t", t=TCH),
              d_xT.rearrange("(ic p) t -> p ic t", p=128)[:, :, 0:TCH],
              writes=["xstg"])
        b.dma("sync", xtmt[:], d_xtm[0:128, :], writes=["xtmt"])
        b.dma("sync", xtmt2[:], d_xtm[128:256, :],
              writes=["xtmt2"])

        def load_cast(qq, dst_sb, dst_name, dram, n_iter, width, stgs):
            rr = dram.rearrange("(i p) o -> p i o", p=128)
            for i in range(n_iter):
                stg, sn = stgs[i % len(stgs)]
                b.dma(qq, stg[:, 0:width], rr[:, i, :], writes=[sn])
                b.op("vector",
                     lambda v, stg=stg, i=i: v.tensor_copy(
                         dst_sb[:, i * width:(i + 1) * width],
                         stg[:, 0:width]),
                     reads=[sn], writes=[dst_name])

        stg_h01 = [(hch[:, 0:2048], "hch0"), (hch[:, 2048:4096], "hch1")]
        stg_x1h = [(x1ch[:, 0:1024], "x1ch"), (x1ch[:, 1024:2048], "x1ch")]
        stg_rA = [(rA[0][:], "rA0"), (rA[1][:], "rA1")]
        load_cast("sync", wq_sb, "wq_sb", d_Wq, 8, 1024, stg_h01)
        load_cast("sync", wdn_sb, "wdn_sb", d_Wdn, 8, 256, stg_h01)
        load_cast("scalar", wout_sb, "wout_sb", d_Wout, 8, 1024, stg_rA)
        rru2 = d_Wup.rearrange("(i p) o -> p i o", p=128)
        for i2 in range(2):
            for hh in range(2):
                stg, sn = stg_rA[hh]
                b.dma("scalar", stg[:, 0:1024],
                      rru2[:, i2, hh * 1024:(hh + 1) * 1024], writes=[sn])
                b.op("vector",
                     lambda v, stg=stg, i2=i2, hh=hh: v.tensor_copy(
                         wup_sb[:, i2 * 2048 + hh * 1024:
                                i2 * 2048 + (hh + 1) * 1024], stg[:, 0:1024]),
                     reads=[sn], writes=["wup_sb"])

        for ac in (MX, SCR, MXR):
            b.op("vector", lambda v, ac=ac: v.memset(col(ac), 0.0),
                 writes=["cols"])

        def p1_unit(qq, src, stg, sns, tc, ac):
            b.dma(qq, stg, src, writes=sns)
            b.op("vector",
                 lambda v: v.tensor_reduce(col(tc), stg, AX, ALU.add,
                                           apply_absolute_value=True),
                 reads=sns, writes=["cols"])
            b.op("vector",
                 lambda v: v.tensor_add(col(ac), col(ac), col(tc)),
                 reads=["cols"], writes=["cols"])

        NEL = float(D * DI)
        rrg1 = d_wgT.rearrange("(l p g) c -> p l (g c)", p=128, g=2)
        rrg = d_wgT.rearrange("(rb p) c -> p rb c", p=128)
        rru = d_wuT.rearrange("(rb p) c -> p rb c", p=128)
        rrd = d_wdT.rearrange("(rb p) c -> p rb c", p=128)
        wg1 = [(rrg1[:, i // 2, (i % 2) * 4096:(i % 2) * 4096 + 4096],
                hch[:, 0:4096] if i % 2 == 0 else hch[:, 4096:8192],
                ["hch0", "hch1"] if i % 2 == 0 else ["hch2", "hch3"])
               for i in range(8)]
        wd1 = [(rrd[:, i, :],
                x1ch[:, 0:1024] if i % 2 == 0 else x1ch[:, 1024:2048],
                ["x1ch"]) for i in range(32)]
        wu1 = [(rru[:, i // 4, (i % 4) * 1024:(i % 4) * 1024 + 1024],
                tmp1[:], ["tmp1"]) for i in range(32)]
        for i in range(32):
            p1_unit("gpsimd", *wd1[i], MXC, MX)
            p1_unit("sync", *wu1[i], SXI, MXR)
            if i < 8:
                p1_unit("scalar", *wg1[i], TMPC, SCR)
            if n_chunks > 0 and phase >= 1:
                if i == 7:
                    emit_proj(0, skip_load=True)
                elif i == 15 and phase >= 2:
                    emit_attnA(0, 0, skip_xtmt=True)
                elif i == 23 and phase >= 2:
                    emit_attnA(0, 1, skip_xtmt=True)
                elif i == 27 and n_chunks > 1:
                    emit_proj(1, load_q="scalar")

        def qw_bcast(ac, acol, wc):
            b.op("tensor", lambda t: t.matmul(avd[0:1, 0:1], cONE, col(ac),
                                              start=True, stop=True),
                 reads=["consts", "cols"], writes=["avd"])
            b.op("vector",
                 lambda v: v.tensor_scalar(col(ac)[0:1, :], avd[0:1, 0:1],
                                           1.0 / NEL, 1e-5, ALU.mult,
                                           ALU.max),
                 reads=["avd"], writes=["cols"])
            b.op("tensor", lambda t: t.matmul(avd[:, 1:2], ones1[0:1, :],
                                              col(ac)[0:1, :], start=True,
                                              stop=True),
                 reads=["ones1", "cols"], writes=["avd"])
            b.op("vector",
                 lambda v: v.tensor_copy(consts[:, acol:acol + 1],
                                         avd[:, 1:2]),
                 reads=["avd"], writes=["consts"])
            b.op("vector",
                 lambda v: v.reciprocal(col(wc), consts[:, acol:acol + 1]),
                 reads=["consts"], writes=["cols"])

        qw_bcast(SCR, AG, WSC)
        qw_bcast(MXR, AU, SH127)
        qw_bcast(MX, AD, S127)

        rr5g = s_wgu.rearrange("p (jb ic k) -> p jb ic k", ic=8, k=1024)
        rr5d = s_wdq.rearrange("p (jb jc o) -> p jb jc o", jc=2, o=1024)

        def p2_unit(qq, src, stg, sns, ub, un, bnb, bnn, tern, ternn,
                    wc, oap, tap, sname):
            b.dma(qq, stg, src, writes=sns)
            b.op("scalar",
                 lambda s: s.activation(ub, stg, AF.Identity, bias=0.0,
                                        scale=col(wc)),
                 reads=sns + ["cols"], writes=[un])
            b.op("vector",
                 lambda v: v.tensor_scalar(bnb, ub, -0.5, None, ALU.is_le),
                 reads=[un], writes=[bnn])
            b.op("vector",
                 lambda v: v.scalar_tensor_tensor(tern, ub, 0.5, bnb,
                                                  ALU.is_ge, ALU.subtract),
                 reads=[un, bnn], writes=[ternn])
            b.dma(qq, oap, tap, reads=[ternn], writes=[sname])

        gu2 = []
        for cb_ in range(2):
            for rb_ in range(8):
                for rrw, koff, wc in ((rrg, 0, WSC), (rru, 512, SH127)):
                    gu2.append((rrw[:, rb_, cb_ * 2048:(cb_ + 1) * 2048],
                                rb_, cb_, koff, wc))
        for i in range(32):
            stgd, snd = stg_x1h[i % 2]
            td8 = hqTs[0][:, 0:512].bitcast(F8)
            p2_unit("gpsimd", rrd[:, i, :], stgd, [snd],
                    hch[:, 4096:5120], "hch2", hch[:, 6144:7168], "hch3",
                    td8, "hqT0", S127,
                    rr5d[:, i // 2, i % 2, :], td8, "wdq_s")
            src, rb_, cb_, koff, wc = gu2[i]
            stgg, sng = stg_h01[i % 2]
            tern = hqTs[1][:, 0:1024].bitcast(F8)
            p2_unit(["scalar", "sync"][i % 2], src, stgg, [sng],
                    hch[:, 4096:6144], "hch2", hch[:, 6144:8192], "hch3",
                    tern, "hqT1", wc,
                    rr5g[:, cb_ * 4:(cb_ + 1) * 4, rb_, koff:koff + 512],
                    tern.rearrange("p (jb j) -> p jb j", j=512),
                    "wgu_sA" if cb_ == 0 else "wgu_sB")
        b.op("vector", lambda v: v.tensor_scalar(col(AG127),
                                                 consts[:, AG:AG + 1],
                                                 1.0 / 127.0, None, ALU.mult),
             reads=["consts"], writes=["cols"])
        b.op("vector", lambda v: v.tensor_mul(col(ACOMB), consts[:, AU:AU + 1],
                                              consts[:, AD:AD + 1]),
             reads=["consts"], writes=["cols"])
        b.op("vector", lambda v: v.tensor_scalar(col(ACOMB), col(ACOMB),
                                                 1.0 / (127.0 * 127.0), None,
                                                 ALU.mult),
             reads=["cols"], writes=["cols"])

        if n_chunks > 0 and phase >= 2:
            emit_tileB(0, 0)
            emit_tileB_xpose(0)
            emit_tileB(0, 1)
            if phase < 3:
                emit_tileB_xpose(1)

        for ch in range(n_chunks):
            if ch > 0:
                if phase >= 2:
                    emit_attnA(ch, 0)
                    emit_attnA(ch, 1)
            if ch + 1 < n_chunks and phase >= 1 and ch > 0:
                emit_proj(ch + 1)
            if ch > 0 and phase >= 2:
                emit_tileB(ch, 0)
                emit_tileB_xpose(0)
                emit_tileB(ch, 1)
            if phase >= 3:
                emit_ffn(ch, after_first_w0=lambda: emit_tileB_xpose(1))
            elif phase >= 2 and ch > 0:
                emit_tileB_xpose(1)

        b.final_wait("sync", "yout")
        b.emit(block, ctx, nc, sems_op)
    return nc


_CACHE = {}


def _host_prep(inputs):
    x = np.asarray(inputs["L0_attn_x"], dtype=np.float32)
    Wq = np.ascontiguousarray(np.asarray(inputs["L0_attn_Wq"], np.float32))
    bq = np.asarray(inputs["L0_attn_bq"], np.float32)
    Wdn = np.ascontiguousarray(np.asarray(inputs["L0_attn_Wdown"], np.float32))
    bdn = np.asarray(inputs["L0_attn_bdown"], np.float32)
    Wup = np.ascontiguousarray(np.asarray(inputs["L0_attn_Wup"], np.float32))
    bup = np.asarray(inputs["L0_attn_bup"], np.float32)
    Wout = np.ascontiguousarray(np.asarray(inputs["L0_attn_Wout"], np.float32))
    bout = np.asarray(inputs["L0_attn_bout"], np.float32)
    wgT = np.ascontiguousarray(np.asarray(inputs["L0_ffn_w_gate"], np.float32).T)
    wuT = np.ascontiguousarray(np.asarray(inputs["L0_ffn_w_up"], np.float32).T)
    wdT = np.ascontiguousarray(np.asarray(inputs["L0_ffn_w_down"], np.float32).T)

    bdn_cols = np.ascontiguousarray(bdn.reshape(2, 128).T)
    bout_row = np.ascontiguousarray(bout.reshape(1, D)
                                    .astype(ml_dtypes.bfloat16))
    bq_row = np.ascontiguousarray(bq.reshape(1, D).astype(ml_dtypes.bfloat16))
    bup_row = np.ascontiguousarray(bup.reshape(1, DKV)
                                   .astype(ml_dtypes.bfloat16))

    consts = np.zeros((128, 8), np.float32)
    consts[:, 0] = C2
    consts[:, 1] = C1
    consts[:, 2] = 1.0
    consts[:, 3] = 1e-5

    m_idx = np.arange(128)
    rm = m_idx // 16
    rn64 = (np.arange(64)) // 8
    NEG = float(np.float32(ml_dtypes.bfloat16(-30000.0)))
    mU = np.zeros((128, 128), np.float32)
    mV = np.zeros((128, 512), np.float32)
    for base in (0, 64):
        mU[base, :] = 1.0
        mV[base, :] = NEG
        for r in range(8):
            mU[base + 1 + r, :] = (rm == r).astype(np.float32)
            mV[base + 1 + r, :] = np.tile((rn64 == r) * (-NEG), 8)
    mU = mU.astype(ml_dtypes.bfloat16)
    mV = mV.astype(ml_dtypes.bfloat16)
    id128 = np.eye(128, dtype=ml_dtypes.bfloat16)
    id2 = np.zeros((128, 64), dtype=ml_dtypes.bfloat16)
    id2[0:64, :] = np.eye(64)
    id2[64:128, :] = np.eye(64)
    ones_row = np.ones((1, 256), ml_dtypes.bfloat16)

    shared = {
        "Wq": Wq, "Wdn": Wdn, "Wup": Wup, "Wout": Wout,
        "bdn_cols": bdn_cols, "bout_row": bout_row,
        "bq_row": bq_row, "bup_row": bup_row,
        "wgT": wgT, "wuT": wuT, "wdT": wdT,
        "consts": consts, "mU": mU, "mV": mV,
        "id128": id128, "id2": id2,
        "ones_row": ones_row,
    }
    in_maps = []
    for bidx in range(NCORE):
        m = dict(shared)
        m["xT"] = np.ascontiguousarray(x[bidx].T)
        m["xtm"] = np.ascontiguousarray(x[bidx])
        in_maps.append(m)
    return in_maps


def kernel(**inputs):
    if "nc" not in _CACHE:
        _CACHE["nc"] = build_nc()
    nc = _CACHE["nc"]
    in_maps = _host_prep(inputs)
    res = run_bass_kernel_spmd(nc, in_maps, core_ids=list(range(NCORE)))
    out = np.stack([np.asarray(r["y"], dtype=np.float32) for r in res.results],
                   axis=0)
    return out


# version 24 (final)
# speedup vs baseline: 1.0618x; 1.0182x over previous
import numpy as np
import ml_dtypes
import concourse.bass as bass
import concourse.mybir as mybir
from concourse.bass_utils import run_bass_kernel_spmd
from contextlib import ExitStack

F32 = mybir.dt.float32
BF16 = mybir.dt.bfloat16
F8 = mybir.dt.float8e4
AF = mybir.ActivationFunctionType
ALU = mybir.AluOpType
AX = mybir.AxisListType.X

T, D, NH, DH, DLAT, DKV, DI = 4096, 1024, 16, 64, 256, 2048, 4096
NCORE = 8
TCH = 256
NCHUNK = T // TCH
TPC = TCH // 128
JB = 512
NJB = DI // JB
C1 = 12582912.0
C2 = 6755399441055744.0

ENG = ("tensor", "vector", "scalar", "gpsimd", "sync")


class B:

    def __init__(self):
        self.items = {e: [] for e in ENG}
        self.nop = {e: 0 for e in ENG}
        self.ndma = {}
        self.writer = {}
        self.readers = {}

    def _need(self, waits, eng, entry, is_dma):
        kind, e2, i = entry
        if (kind == "op" and e2 == eng and eng == "tensor" and not is_dma):
            return
        key = (kind, e2)
        v = (i + 1) if kind == "op" else 16 * (i + 1)
        if waits.get(key, 0) < v:
            waits[key] = v

    def _deps(self, eng, reads, writes, is_dma):
        waits = {}
        for b_ in reads:
            w = self.writer.get(b_)
            if w is not None:
                self._need(waits, eng, w, is_dma)
        for b_ in writes:
            w = self.writer.get(b_)
            if w is not None:
                self._need(waits, eng, w, is_dma)
            for r in self.readers.get(b_, ()):
                self._need(waits, eng, r, is_dma)
        return waits

    def _record(self, entry, reads, writes):
        for b_ in reads:
            self.readers.setdefault(b_, []).append(entry)
        for b_ in writes:
            self.writer[b_] = entry
            self.readers[b_] = []

    def op(self, eng, emit, reads=(), writes=()):
        waits = self._deps(eng, reads, writes, False)
        i = self.nop[eng]
        self.items[eng].append((waits, emit, "op"))
        self.nop[eng] = i + 1
        self._record(("op", eng, i), reads, writes)

    def dma(self, q, out_ap, in_ap, reads=(), writes=(), stream=None):
        stream = stream or (writes[0] if writes else "yout")
        waits = self._deps(q, reads, writes, True)
        i = self.ndma.get(stream, 0)
        self.items[q].append((waits, (out_ap, in_ap, stream), "dma"))
        self.ndma[stream] = i + 1
        self._record(("dma", stream, i), reads, writes)

    def final_wait(self, eng, stream):
        n = self.ndma.get(stream, 0)
        self.items[eng].append(({("dma", stream): 16 * n}, None, "wait_only"))

    def emit(self, block, ctx, nc, sems_op):
        sems_dma = {}
        for s in self.ndma:
            sems_dma[s] = ctx.enter_context(nc.semaphore("sdm_%s" % s))
        handles = {"vector": block.vector, "scalar": block.scalar,
                   "tensor": block.tensor, "gpsimd": block.gpsimd,
                   "sync": block.sync}
        for e in ENG:
            items = self.items[e]
            if not items:
                continue

            def make(eng_name, eng_items):
                def _(eng):
                    hw = {}
                    for waits, payload, kind in eng_items:
                        for key in sorted(waits):
                            v = waits[key]
                            if hw.get(key, 0) >= v:
                                continue
                            k2, e2 = key
                            sem = sems_op[e2] if k2 == "op" else sems_dma[e2]
                            eng.wait_ge(sem, v)
                            hw[key] = v
                        if kind == "op":
                            payload(eng).then_inc(sems_op[eng_name], 1)
                        elif kind == "dma":
                            out_ap, in_ap, stream = payload
                            eng.dma_start(out=out_ap, in_=in_ap).then_inc(
                                sems_dma[stream], 16)
                return _
            handles[e](make(e, items))


def build_nc(n_chunks=NCHUNK, phase=3):
    nc = bass.Bass("TRN2", target_bir_lowering=False, debug=False)

    d_xT = nc.dram_tensor("xT", [D, T], F32, kind="ExternalInput").ap()
    d_xtm = nc.dram_tensor("xtm", [T, D], F32, kind="ExternalInput").ap()
    d_Wq = nc.dram_tensor("Wq", [D, D], F32, kind="ExternalInput").ap()
    d_Wdn = nc.dram_tensor("Wdn", [D, DLAT], F32, kind="ExternalInput").ap()
    d_Wup = nc.dram_tensor("Wup", [DLAT, DKV], F32, kind="ExternalInput").ap()
    d_Wout = nc.dram_tensor("Wout", [D, D], F32, kind="ExternalInput").ap()
    d_bdn = nc.dram_tensor("bdn_cols", [128, 2], F32, kind="ExternalInput").ap()
    d_bout = nc.dram_tensor("bout_row", [1, D], BF16, kind="ExternalInput").ap()
    d_bq_row = nc.dram_tensor("bq_row", [1, D], BF16, kind="ExternalInput").ap()
    d_bup_row = nc.dram_tensor("bup_row", [1, DKV], BF16,
                               kind="ExternalInput").ap()
    d_wgT = nc.dram_tensor("wgT", [D, DI], F32, kind="ExternalInput").ap()
    d_wuT = nc.dram_tensor("wuT", [D, DI], F32, kind="ExternalInput").ap()
    d_wdT = nc.dram_tensor("wdT", [DI, D], F32, kind="ExternalInput").ap()
    d_consts = nc.dram_tensor("consts", [128, 8], F32, kind="ExternalInput").ap()
    d_mU = nc.dram_tensor("mU", [128, 128], BF16, kind="ExternalInput").ap()
    d_mV = nc.dram_tensor("mV", [128, 512], BF16, kind="ExternalInput").ap()
    d_id128 = nc.dram_tensor("id128", [128, 128], BF16, kind="ExternalInput").ap()
    d_id2 = nc.dram_tensor("id2", [128, 64], BF16, kind="ExternalInput").ap()
    d_ones = nc.dram_tensor("ones_row", [1, 256], BF16,
                            kind="ExternalInput").ap()
    d_y = nc.dram_tensor("y", [T, D], F32, kind="ExternalOutput").ap()
    s_wgu = nc.dram_tensor("wgu_s", [128, 65536], F8, kind="Internal").ap()
    s_wdq = nc.dram_tensor("wdq_s", [128, 32768], F8, kind="Internal").ap()

    with ExitStack() as ctx:
        def sb(name, shape, dtype=F32):
            return ctx.enter_context(nc.sbuf_tensor(name, shape, dtype))

        def psum(name, shape, dtype=F32):
            return ctx.enter_context(nc.psum_tensor(name, shape, dtype))

        consts = sb("consts_sb", [128, 8])
        id128 = sb("id128t", [128, 128], BF16)
        id2 = sb("id2t", [128, 64], BF16)
        mU = sb("mUt", [128, 128], BF16)
        mV = sb("mVt", [128, 512], BF16)
        ones_row = sb("ones_rowt", [1, 256], BF16)
        ones1 = sb("ones1t", [1, 128])
        bout_row = sb("bout_rowt", [1, D], BF16)
        bq_row = sb("bq_rowt", [1, D], BF16)
        bup_row = sb("bup_rowt", [1, DKV], BF16)
        bdn_cols = sb("bdn_colst", [128, 2])
        wq_sb = sb("wq_sb", [128, 8192], BF16)
        wdn_sb = sb("wdn_sb", [128, 2048], BF16)
        wup_sb = sb("wup_sb", [128, 4096], BF16)
        wout_sb = sb("wout_sb", [128, 8192], BF16)
        cols = sb("colst", [128, 48])
        rdt = sb("rdt", [128, 16])
        xstg = sb("xstgt", [128, 2048])
        xtb = sb("xtb", [128, 2048], BF16)
        kvlT = sb("kvlT", [128, 512], BF16)
        qT = sb("qTt", [128, TCH * 8], BF16)
        kvT = sb("kvTt", [128, TCH * 16], BF16)
        kdupT = sb("kdupTt", [128, TCH * 16], BF16)
        eM = sb("eMt", [128, 2048], BF16)
        vaug = sb("vaugt", [128, 1024], BF16)
        onecol = sb("onecolt", [128, 1], BF16)
        aoP = eM[:, 1024:2048]
        aoTs = eM[:, 0:1024]
        xq = eM[:, 0:1024]
        xtmt = sb("xtmtt", [128, 1024])
        xtmt2 = sb("xtmt2t", [128, 1024])
        rA = [sb("r1a", [128, 1024]), sb("r1b", [128, 1024])]
        tmp1 = sb("tmp1t", [128, 1024])
        x1ch = sb("x1cht", [128, 2048])
        xqTs = [sb("xqTs0", [128, 1024], BF16), sb("xqTs1", [128, 1024], BF16)]
        sgate = sb("sgatet", [128, JB])
        hch = sb("hcht", [128, 2 * DI])
        hqb2 = [sb("hqb0", [128, 1024], BF16), sb("hqb1", [128, 1024], BF16)]
        hqTs = [sb("hqTst0", [128, DI], BF16), sb("hqTst1", [128, DI], BF16)]
        wgu = [sb("wgu%d" % i, [128, 4096], F8) for i in range(4)]
        wdt = [sb("wdt%d" % i, [128, 2048], F8) for i in range(5)]
        bns = sb("bnst", [128, 12])
        bnag = sb("bnagt", [128, 2])

        cC2 = consts[:, 0:1]
        cC1 = consts[:, 1:2]
        cONE = consts[:, 2:3]
        cEPS = consts[:, 3:4]
        AG, AU, AD = 5, 6, 7

        def col(i):
            return cols[:, i:i + 1]
        (IND, INDC, WSC, AG127, ACOMB, MU, VAR_, LCOL, RSTD, NEGMUR, TMPC,
         MX, MXC, SX, S127, SXI, MXR, MX1C0, MX1C1, SG0, SG1,
         MH, MHC, SH127, SHI, YSC0, YSC1, SCR) = range(28)
        HP0 = 32

        psc0 = psum("psc0", [128, 512])
        psc1 = psum("psc1", [128, 512])
        pbt = psum("pbt", [128, 1024], BF16)
        avp = psum("avp", [128, 1024])
        avd = psum("avd", [128, 512])
        pp0 = psum("pp0", [128, 512])
        pp1 = psum("pp1", [128, 512])

        sems_op = {e: ctx.enter_context(nc.semaphore("so_" + e)) for e in ENG}
        block = ctx.enter_context(nc.Block())
        b = B()

        def emit_proj(ch, skip_load=False, load_q="sync"):
            t0 = ch * TCH
            if not skip_load:
                b.dma(load_q, xstg[:].rearrange("p (ic t) -> p ic t", t=TCH),
                      d_xT.rearrange("(ic p) t -> p ic t", p=128)[:, :,
                                                                 t0:t0 + TCH],
                      writes=["xstg"])
            b.op("vector", lambda v: v.tensor_copy(xtb[:], xstg[:]),
                 reads=["xstg"], writes=["xtb"])

            for lc in range(2):
                ps, pn = ([pp0, "pp0"], [pp1, "pp1"])[lc]
                for ic in range(8):
                    b.op("tensor",
                         lambda t, ps=ps, ic=ic, lc=lc: t.matmul(
                             ps[:, 0:TCH],
                             wdn_sb[:, ic * 256 + lc * 128:
                                    ic * 256 + lc * 128 + 128],
                             xtb[:, ic * TCH:(ic + 1) * TCH],
                             start=(ic == 0), stop=(ic == 7)),
                         reads=["wdn_sb", "xtb"], writes=[pn])
                b.op("scalar",
                     lambda s, ps=ps, lc=lc: s.activation(
                         kvlT[:, lc * TCH:(lc + 1) * TCH], ps[:, 0:TCH],
                         AF.Identity, bias=bdn_cols[:, lc:lc + 1], scale=1.0),
                     reads=[pn, "bdn"], writes=["kvlT"])

            for quad in range(2):
                for oc4 in range(4):
                    oc = quad * 4 + oc4
                    for ic in range(8):
                        b.op("tensor",
                             lambda t, ic=ic, oc=oc, oc4=oc4: t.matmul(
                                 avp[:, oc4 * 256: oc4 * 256 + TCH],
                                 wq_sb[:, ic * 1024 + oc * 128:
                                       ic * 1024 + oc * 128 + 128],
                                 xtb[:, ic * TCH:(ic + 1) * TCH],
                                 start=(ic == 0), stop=False),
                             reads=["wq_sb", "xtb"], writes=["avp"])
                    b.op("tensor",
                         lambda t, oc=oc, oc4=oc4: t.matmul(
                             avp[:, oc4 * 256: oc4 * 256 + TCH],
                             bq_row[0:1, oc * 128:(oc + 1) * 128],
                             ones_row[0:1, 0:TCH],
                             start=False, stop=True),
                         reads=["bq_row", "ones_row"], writes=["avp"])
                qdst = qT[:].rearrange("p (t c) -> p t c", c=8)[
                    :, :, quad * 4: quad * 4 + 4]
                qsrc = avp[:].rearrange("p (c t) -> p c t", t=256) \
                    .transpose([0, 2, 1])
                b.op("scalar",
                     lambda s, qdst=qdst, qsrc=qsrc: s.activation(
                         qdst, qsrc, AF.Identity, bias=0.0, scale=1.0),
                     reads=["avp"], writes=["qT"])

            for hq4 in range(4):
                for hh in range(4):
                    h = hq4 * 4 + hh
                    psA = [pp0, pp1][hh // 2]
                    pnA = ["pp0", "pp1"][hh // 2]
                    psB = [psc0, psc1][hh // 2]
                    pnB = ["psc0", "psc1"][hh // 2]
                    co = (hh % 2) * 256
                    for lc in range(2):
                        b.op("tensor",
                             lambda t, h=h, lc=lc, psA=psA, co=co: t.matmul(
                                 psA[:, co:co + TCH],
                                 wup_sb[:, lc * 2048 + h * 128:
                                        lc * 2048 + h * 128 + 128],
                                 kvlT[:, lc * TCH:(lc + 1) * TCH],
                                 start=(lc == 0), stop=False),
                             reads=["wup_sb", "kvlT"], writes=[pnA])
                    b.op("tensor",
                         lambda t, h=h, psA=psA, co=co: t.matmul(
                             psA[:, co:co + TCH],
                             bup_row[0:1, h * 128:(h + 1) * 128],
                             ones_row[0:1, 0:TCH],
                             start=False, stop=True),
                         reads=["bup_row", "ones_row"], writes=[pnA])
                for bk in range(2):
                    psA = [pp0, pp1][bk]
                    pnA = ["pp0", "pp1"][bk]
                    g0 = hq4 * 4 + bk * 2
                    kvdst = kvT[:].rearrange("p (t g) -> p t g", g=16)[
                        :, :, g0:g0 + 2]
                    src_a = psA[:].rearrange("p (g t) -> p g t", t=256) \
                        .transpose([0, 2, 1])
                    b.op("scalar",
                         lambda s, kvdst=kvdst, src_a=src_a: s.activation(
                             kvdst, src_a, AF.Identity, bias=0.0, scale=1.0),
                         reads=[pnA], writes=["kvT"])
            b.dma("sync", kdupT[64:128, :], kvT[0:64, :],
                  reads=["kvT"], writes=["kdupT"])

        def emit_attnA_qk(ch, w, part=None):
            tok0 = w * 128
            for b8 in range(2):
                if part is not None and part != b8:
                    continue
                for i8 in range(8):
                    s_ = b8 * 8 + i8
                    S16 = (tok0 + 8 * s_) * 16
                    S8 = (tok0 + 8 * s_) * 8
                    b.op("tensor",
                         lambda t, i8=i8, S16=S16, S8=S8: t.matmul(
                             psc0[:, i8 * 64:(i8 + 1) * 64],
                             kvT[0:64, S16:S16 + 128],
                             qT[0:64, S8:S8 + 64],
                             start=(i8 == 0), stop=False),
                         reads=["kvT", "qT"], writes=["psc0"])
                    b.op("tensor",
                         lambda t, i8=i8, S16=S16, S8=S8: t.matmul(
                             psc1[:, i8 * 64:(i8 + 1) * 64],
                             kdupT[64:128, S16:S16 + 128],
                             qT[64:128, S8:S8 + 64],
                             start=(i8 == 0), stop=False),
                         reads=["kdupT", "qT"], writes=["psc1"])
                b.op("tensor",
                     lambda t: t.matmul(psc0[:], mU[0:64, :], mV[0:64, :],
                                        start=False, stop=True),
                     reads=["mU", "mV"], writes=["psc0"])
                b.op("tensor",
                     lambda t: t.matmul(psc1[:], mU[64:128, :],
                                        mV[64:128, :],
                                        start=False, stop=True),
                     reads=["mU", "mV"], writes=["psc1"])
                eMb = eM[:, b8 * 1024:(b8 + 1) * 1024].rearrange(
                    "p (s c) -> p s c", c=128)
                b.op("scalar",
                     lambda s, eMb=eMb: s.activation(
                         eMb[:, :, 0:64], psc0[:], AF.Exp, bias=0.0,
                         scale=0.125),
                     reads=["psc0"], writes=["eM"])
                b.op("scalar",
                     lambda s, eMb=eMb: s.activation(
                         eMb[:, :, 64:128], psc1[:], AF.Exp, bias=0.0,
                         scale=0.125),
                     reads=["psc1"], writes=["eM"])

        def emit_attnA(ch, w, skip_xtmt=False, skip_qk=False, xtmt_q="sync"):
            tglob = ch * TPC + w
            tok0 = w * 128
            xtb_, xtn_ = [(xtmt, "xtmt"), (xtmt2, "xtmt2")][tglob % 2]
            if not skip_xtmt:
                b.dma(xtmt_q, xtb_[:],
                      d_xtm[tglob * 128:(tglob + 1) * 128, :],
                      writes=[xtn_])
            if not skip_qk:
                emit_attnA_qk(ch, w)

            for s_ in range(16):
                S16 = (tok0 + 8 * s_) * 16
                b.op("tensor",
                     lambda t, s_=s_, S16=S16: t.transpose(
                         pbt[:, s_ * 64:(s_ + 1) * 64],
                         kvT[64:128, S16:S16 + 128],
                         id2[64:128, 0:64]),
                     reads=["kvT", "id2"], writes=["pbt"])
            b.op("vector", lambda v: v.tensor_copy(vaug[:], pbt[:]),
                 reads=["pbt"], writes=["vaug"])

            for s_ in range(16):
                b.op("tensor",
                     lambda t, s_=s_: t.matmul(
                         avp[:, s_ * 64:(s_ + 1) * 64],
                         eM[:, s_ * 128:(s_ + 1) * 128],
                         vaug[:, s_ * 64:(s_ + 1) * 64],
                         start=True, stop=True),
                     reads=["eM", "vaug"], writes=["avp"])
                b.op("tensor",
                     lambda t, s_=s_: t.matmul(
                         avd[:, s_:s_ + 1],
                         eM[:, s_ * 128:(s_ + 1) * 128],
                         onecol[:],
                         start=True, stop=True),
                     reads=["eM", "onecol"], writes=["avd"])
            b.op("vector", lambda v: v.reciprocal(rdt[:], avd[:, 0:16]),
                 reads=["avd"], writes=["rdt"])
            b.op("vector", lambda v: v.tensor_mul(
                aoP.rearrange("p (s j) -> p s j", j=64),
                avp[:].rearrange("p (s j) -> p s j", j=64),
                rdt[:].unsqueeze(2).broadcast_to([128, 16, 64])),
                 reads=["avp", "rdt", "eM"], writes=["eM"])

            for p_ in (1, 0):
                for s_ in range(16):
                    b.op("tensor",
                         lambda t, p_=p_, s_=s_: t.transpose(
                             pbt[p_ * 64:(p_ + 1) * 64,
                                 s_ * 64:(s_ + 1) * 64],
                             aoP[p_ * 64:(p_ + 1) * 64,
                                 s_ * 64:(s_ + 1) * 64],
                             id2[p_ * 64:(p_ + 1) * 64, 0:64]),
                         reads=["eM", "id2"], writes=["pbt"])
                b.op("vector",
                     lambda v, p_=p_: v.tensor_copy(
                         aoTs[p_ * 64:(p_ + 1) * 64, :],
                         pbt[p_ * 64:(p_ + 1) * 64, :]),
                     reads=["pbt"], writes=["eM"])

            aoTr = aoTs.rearrange("p (t c) -> p t c", c=8)
            for cs in range(8):
                for half in range(2):
                    ps, pn = ([pp0, "pp0"], [pp1, "pp1"])[half]
                    b.op("tensor",
                         lambda t, ps=ps, cs=cs, half=half: t.matmul(
                             ps[:],
                             aoTr[:, :, cs],
                             wout_sb[:, cs * 1024 + half * 512:
                                     cs * 1024 + half * 512 + 512],
                             start=(cs == 0), stop=False),
                         reads=["eM", "wout_sb"], writes=[pn])
            for half in range(2):
                ps, pn = ([pp0, "pp0"], [pp1, "pp1"])[half]
                b.op("tensor",
                     lambda t, ps=ps, half=half: t.matmul(
                         ps[:], ones_row[0:1, 0:128],
                         bout_row[0:1, half * 512:(half + 1) * 512],
                         start=False, stop=True),
                     reads=["ones_row", "bout_row"], writes=[pn])
            for half in range(2):
                ps, pn = ([pp0, "pp0"], [pp1, "pp1"])[half]
                b.op("vector",
                     lambda v, ps=ps, half=half, w=w: v.tensor_add(
                         rA[w][:, half * 512:(half + 1) * 512], ps[:],
                         xtb_[:, half * 512:(half + 1) * 512]),
                     reads=[pn, xtn_], writes=["rA%d" % w])

        def emit_tileB_xpose(w):
            for ic in range(8):
                b.op("tensor",
                     lambda t, ic=ic: t.transpose(
                         pbt[:, ic * 128:(ic + 1) * 128],
                         xq[:, ic * 128:(ic + 1) * 128], id128[:]),
                     reads=["eM", "id128"], writes=["pbt"])
            b.op("vector",
                 lambda v, w=w: v.tensor_copy(xqTs[w][:], pbt[:]),
                 reads=["pbt"], writes=["xqT%d" % w])

        def emit_tileB(ch, w):
            rw = rA[w]
            rn = "rA%d" % w
            b.op("vector", lambda v: v.bn_stats(bns[:, 0:6], rw[:, 0:512]),
                 reads=[rn], writes=["bns"])
            b.op("vector", lambda v: v.bn_stats(bns[:, 6:12], rw[:, 512:1024]),
                 reads=[rn], writes=["bns"])
            b.op("vector", lambda v: v.bn_aggr(bnag[:], bns[:]),
                 reads=["bns"], writes=["bnag"])
            b.op("scalar", lambda s: s.activation(col(LCOL), bnag[:, 1:2],
                                                  AF.Ln, bias=cEPS, scale=1.0),
                 reads=["bnag", "consts"], writes=["cols"])
            b.op("scalar", lambda s: s.activation(col(RSTD), col(LCOL),
                                                  AF.Exp, bias=0.0, scale=-0.5),
                 reads=["cols"], writes=["cols"])
            b.op("vector", lambda v: v.tensor_mul(col(TMPC), bnag[:, 0:1],
                                                  col(RSTD)),
                 reads=["bnag", "cols"], writes=["cols"])
            b.op("vector", lambda v: v.tensor_scalar(col(NEGMUR), col(TMPC),
                                                     -1.0, None, ALU.mult),
                 reads=["cols"], writes=["cols"])
            b.op("scalar",
                 lambda s, w=w: s.activation(
                     x1ch[:, w * 1024:(w + 1) * 1024], rw[:], AF.Identity,
                     bias=col(NEGMUR), scale=col(RSTD)),
                 reads=[rn, "cols"], writes=["x1ch"])
            b.op("vector", lambda v, rw=rw: v.tensor_scalar(
                rw[:], rw[:], bnag[:, 0:1], None, ALU.subtract),
                 reads=[rn, "bnag"], writes=[rn])
            b.op("vector", lambda v, rw=rw: v.tensor_reduce(
                col(MX), rw[:], AX, ALU.max, apply_absolute_value=True),
                 reads=[rn], writes=["cols"])
            b.op("vector", lambda v: v.tensor_scalar(col(MXC), col(MX), 1e-5,
                                                     None, ALU.max),
                 reads=["cols"], writes=["cols"])
            b.op("vector", lambda v: v.reciprocal(col(SX), col(MXC)),
                 reads=["cols"], writes=["cols"])
            b.op("vector", lambda v: v.tensor_scalar(col(S127), col(SX), 127.0,
                                                     None, ALU.mult),
                 reads=["cols"], writes=["cols"])
            b.op("vector", lambda v: v.tensor_mul(col(SXI), col(S127),
                                                  col(INDC)),
                 reads=["cols"], writes=["cols"])
            b.op("scalar", lambda s, rw=rw: s.activation(
                tmp1[:], rw[:], AF.Identity, bias=cC2, scale=col(S127)),
                 reads=[rn, "cols", "consts"], writes=["tmp1"])
            b.op("scalar", lambda s, rw=rw: s.activation(
                xtmt[:], rw[:], AF.Identity, bias=cC1, scale=col(SXI)),
                 reads=[rn, "cols", "consts"], writes=["xtmt"])
            b.op("vector", lambda v: v.tensor_scalar(
                xtmt[:], xtmt[:], -C1, None, ALU.add),
                 reads=["xtmt"], writes=["xtmt"])
            b.op("vector", lambda v: v.scalar_tensor_tensor(
                xq, tmp1[:], -C2, xtmt[:], ALU.add, ALU.add),
                 reads=["tmp1", "xtmt"], writes=["eM"])
            b.op("vector", lambda v: v.tensor_mul(col(MXR), col(MX),
                                                  col(RSTD)),
                 reads=["cols"], writes=["cols"])
            b.op("vector",
                 lambda v, w=w: v.tensor_scalar(
                     col([MX1C0, MX1C1][w]), col(MXR), 1e-5, None, ALU.max),
                 reads=["cols"], writes=["cols"])
            b.op("vector",
                 lambda v, w=w: v.tensor_mul(
                     col([SG0, SG1][w]), col([MX1C0, MX1C1][w]), col(AG127)),
                 reads=["cols"], writes=["cols"])

        gbanks = [(pp0[:], "pp0"), (avp[:, 0:512], "avp"), (psc0[:], "psc0")]
        ubanks = [(pp1[:], "pp1"), (avp[:, 512:1024], "avp"), (psc1[:], "psc1")]

        def emit_ffn(ch, after_first_w0=None, qk_next=None):
            for jb in range(NJB):
                for icH in range(2):
                    u4 = (jb * 2 + icH) % 4
                    b.dma(["sync", "scalar"][icH], wgu[u4][:],
                          s_wgu[:, jb * 8192 + icH * 4096:
                                jb * 8192 + (icH + 1) * 4096],
                          reads=["wgu_sA" if jb < 4 else "wgu_sB"],
                          writes=["wgu%d" % u4])
                for w in range(TPC):
                    k = jb * 2 + w
                    gb, gn = gbanks[k % 3]
                    ub, un = ubanks[k % 3]
                    for ic in range(8):
                        icH, icl = ic // 4, ic % 4
                        u4 = (jb * 2 + icH) % 4
                        b.op("tensor",
                             lambda t, w=w, ic=ic, u4=u4, icl=icl,
                             gb=gb: t.matmul(
                                 gb,
                                 xqTs[w][:, ic * 128:(ic + 1) * 128],
                                 wgu[u4][:, icl * 1024:icl * 1024 + 512],
                                 start=(ic == 0), stop=(ic == 7)),
                             reads=["xqT%d" % w, "wgu%d" % u4],
                             writes=[gn])
                        b.op("tensor",
                             lambda t, w=w, ic=ic, u4=u4, icl=icl,
                             ub=ub: t.matmul(
                                 ub,
                                 xqTs[w][:, ic * 128:(ic + 1) * 128],
                                 wgu[u4][:, icl * 1024 + 512:
                                          (icl + 1) * 1024],
                                 start=(ic == 0), stop=(ic == 7)),
                             reads=["xqT%d" % w, "wgu%d" % u4],
                             writes=[un])
                    b.op("scalar",
                         lambda s, w=w, gb=gb: s.activation(
                             sgate[:], gb, AF.Silu, bias=0.0,
                             scale=col([SG0, SG1][w])),
                         reads=[gn, "cols"], writes=["sgate"])
                    hreg = "hch%d" % ((w * DI + jb * JB) // 2048)
                    b.op("vector",
                         lambda v, w=w, jb=jb, ub=ub: v.tensor_mul(
                             hch[:, w * DI + jb * JB: w * DI + (jb + 1) * JB],
                             sgate[:], ub),
                         reads=["sgate", un], writes=[hreg])
                    b.op("vector",
                         lambda v, w=w, jb=jb: v.tensor_reduce(
                             col(HP0 + w * 8 + jb),
                             hch[:, w * DI + jb * JB: w * DI + (jb + 1) * JB],
                             AX, ALU.max, apply_absolute_value=True),
                         reads=[hreg], writes=["cols"])
                    if jb == 0 and w == 0 and after_first_w0 is not None:
                        after_first_w0()

            def hquant_cols(w):
                b.op("vector",
                     lambda v, w=w: v.tensor_reduce(
                         col(MH), cols[:, HP0 + w * 8:HP0 + w * 8 + 8], AX,
                         ALU.max),
                     reads=["cols"], writes=["cols"])
                b.op("vector", lambda v: v.tensor_scalar(col(MHC), col(MH),
                                                         1e-5, None, ALU.max),
                     reads=["cols"], writes=["cols"])
                b.op("vector", lambda v: v.reciprocal(col(SCR), col(MHC)),
                     reads=["cols"], writes=["cols"])
                b.op("vector", lambda v: v.tensor_scalar(col(SH127), col(SCR),
                                                         127.0, None,
                                                         ALU.mult),
                     reads=["cols"], writes=["cols"])
                b.op("vector", lambda v: v.tensor_mul(col(SHI), col(SH127),
                                                      col(INDC)),
                     reads=["cols"], writes=["cols"])

            def hquant_qb(w, qb):
                hr = ["hch%d" % (w * 2), "hch%d" % (w * 2 + 1)]
                hsl = hch[:, w * DI + qb * 1024: w * DI + (qb + 1) * 1024]
                hb, hn = hqb2[qb % 2], "hqb%d" % (qb % 2)
                tt, tn = [(tmp1, "tmp1"), (xtmt, "xtmt")][qb % 2]
                tm2, tn2 = [(rA[0], "rA0"), (rA[1], "rA1")][qb % 2]
                b.op("scalar",
                     lambda s, hsl=hsl, tt=tt: s.activation(
                         tt[:], hsl, AF.Identity, bias=cC2,
                         scale=col(SH127)),
                     reads=hr + ["cols", "consts"], writes=[tn])
                b.op("scalar",
                     lambda s, hsl=hsl, tm2=tm2: s.activation(
                         tm2[:], hsl, AF.Identity, bias=cC1,
                         scale=col(SHI)),
                     reads=hr + ["cols", "consts"], writes=[tn2])
                b.op("vector", lambda v, tm2=tm2: v.tensor_scalar(
                    tm2[:], tm2[:], -C1, None, ALU.add),
                     reads=[tn2], writes=[tn2])
                b.op("vector",
                     lambda v, hb=hb, tt=tt,
                     tm2=tm2: v.scalar_tensor_tensor(
                         hb[:], tt[:], -C2,
                         tm2[:], ALU.add, ALU.add),
                     reads=[tn, tn2], writes=[hn])
                for jj in range(8):
                    b.op("tensor",
                         lambda t, hb=hb, jj=jj: t.transpose(
                             pbt[:, jj * 128:(jj + 1) * 128],
                             hb[:, jj * 128:(jj + 1) * 128], id128[:]),
                         reads=[hn, "id128"], writes=["pbt"])
                b.op("vector",
                     lambda v, qb=qb, w=w: v.tensor_copy(
                         hqTs[w][:, qb * 1024:(qb + 1) * 1024],
                         pbt[:]),
                     reads=["pbt"], writes=["hqT%d" % w])

            def hquant_ysc(w):
                b.op("vector",
                     lambda v, w=w: v.tensor_mul(
                         col([YSC0, YSC1][w]), col(MHC),
                         col([MX1C0, MX1C1][w])),
                     reads=["cols"], writes=["cols"])
                b.op("vector",
                     lambda v, w=w: v.tensor_mul(
                         col([YSC0, YSC1][w]), col([YSC0, YSC1][w]),
                         col(ACOMB)),
                     reads=["cols"], writes=["cols"])

            for w in range(TPC):
                hquant_cols(w)
                for qb in range(4):
                    hquant_qb(w, qb)
                hquant_ysc(w)

            y2s = [[avp[:, 0:512], avp[:, 512:1024]], [pp0[:], pp1[:]]]
            y2ns = [["avp", "avp"], ["pp0", "pp1"]]
            for jb2 in range(16):
                wd_t, wd_n = wdt[jb2 % 5], "wdt%d" % (jb2 % 5)
                b.dma(["gpsimd", "sync", "gpsimd", "sync",
                       "gpsimd"][jb2 % 5], wd_t[:],
                      s_wdq[:, jb2 * 2048:(jb2 + 1) * 2048],
                      reads=["wdq_s"], writes=[wd_n])
                for w in range(TPC):
                    for jc in range(2):
                        jg = jb2 * 2 + jc
                        for half in range(2):
                            b.op("tensor",
                                 lambda t, w=w, jg=jg, jc=jc, half=half,
                                 wd_t=wd_t: t.matmul(
                                     y2s[w][half],
                                     hqTs[w][:, jg * 128:(jg + 1) * 128],
                                     wd_t[:, jc * 1024 + half * 512:
                                          jc * 1024 + half * 512 + 512],
                                     start=(jg == 0), stop=(jg == 31)),
                                 reads=["hqT%d" % w, wd_n],
                                 writes=[y2ns[w][half]])
            for w in range(TPC):
                rw = rA[w]
                rn = "rA%d" % w
                for half in range(2):
                    b.op("vector",
                         lambda v, w=w, half=half, rw=rw: v.scalar_tensor_tensor(
                             rw[:, half * 512:(half + 1) * 512], y2s[w][half],
                             col([YSC0, YSC1][w]),
                             x1ch[:, w * 1024 + half * 512:
                                  w * 1024 + half * 512 + 512],
                             ALU.mult, ALU.add),
                         reads=[y2ns[w][half], "x1ch", "cols"], writes=[rn])
                b.op("vector", lambda v, rw=rw: v.bn_stats(bns[:, 0:6],
                                                           rw[:, 0:512]),
                     reads=[rn], writes=["bns"])
                b.op("vector", lambda v, rw=rw: v.bn_stats(bns[:, 6:12],
                                                           rw[:, 512:1024]),
                     reads=[rn], writes=["bns"])
                b.op("vector", lambda v: v.bn_aggr(bnag[:], bns[:]),
                     reads=["bns"], writes=["bnag"])
                b.op("scalar", lambda s: s.activation(col(LCOL), bnag[:, 1:2],
                                                      AF.Ln, bias=cEPS,
                                                      scale=1.0),
                     reads=["bnag", "consts"], writes=["cols"])
                b.op("scalar", lambda s: s.activation(col(RSTD), col(LCOL),
                                                      AF.Exp, bias=0.0,
                                                      scale=-0.5),
                     reads=["cols"], writes=["cols"])
                b.op("vector", lambda v: v.tensor_mul(col(TMPC), bnag[:, 0:1],
                                                      col(RSTD)),
                     reads=["bnag", "cols"], writes=["cols"])
                b.op("vector", lambda v: v.tensor_scalar(col(NEGMUR),
                                                         col(TMPC),
                                                         -1.0, None, ALU.mult),
                     reads=["cols"], writes=["cols"])
                b.op("scalar",
                     lambda s, rw=rw: s.activation(
                         tmp1[:], rw[:], AF.Identity,
                         bias=col(NEGMUR), scale=col(RSTD)),
                     reads=[rn, "cols"], writes=["tmp1"])
                tglob = ch * TPC + w
                b.dma("sync", d_y[tglob * 128:(tglob + 1) * 128, :], tmp1[:],
                      reads=["tmp1"])

        b.dma("sync", consts[:], d_consts, writes=["consts"])
        b.dma("sync", id128[:], d_id128, writes=["id128"])
        b.dma("sync", id2[:], d_id2, writes=["id2"])
        b.dma("sync", mU[:], d_mU, writes=["mU"])
        b.dma("sync", mV[:], d_mV, writes=["mV"])
        b.dma("sync", ones_row[:], d_ones, writes=["ones_row"])
        b.dma("sync", bout_row[:], d_bout, writes=["bout_row"])
        b.dma("sync", bq_row[:], d_bq_row, writes=["bq_row"])
        b.dma("sync", bup_row[:], d_bup_row, writes=["bup_row"])
        b.dma("sync", bdn_cols[:], d_bdn, writes=["bdn"])

        b.op("vector", lambda v: v.memset(col(SCR), 1.0), writes=["cols"])
        b.op("vector", lambda v: v.tensor_scalar(col(IND), col(SCR), C2, -C2,
                                                 ALU.add, ALU.add),
             reads=["cols"], writes=["cols"])
        b.op("vector", lambda v: v.tensor_scalar(col(INDC), col(IND), -1.0, 1.0,
                                                 ALU.mult, ALU.add),
             reads=["cols"], writes=["cols"])
        b.op("vector", lambda v: v.memset(onecol[:], 1.0), writes=["onecol"])
        b.op("vector", lambda v: v.memset(ones1[:], 1.0), writes=["ones1"])

        b.dma("sync", xstg[:].rearrange("p (ic t) -> p ic t", t=TCH),
              d_xT.rearrange("(ic p) t -> p ic t", p=128)[:, :, 0:TCH],
              writes=["xstg"])
        b.dma("sync", xtmt[:], d_xtm[0:128, :], writes=["xtmt"])
        b.dma("sync", xtmt2[:], d_xtm[128:256, :],
              writes=["xtmt2"])

        def load_cast(qq, dst_sb, dst_name, dram, n_iter, width, stgs):
            rr = dram.rearrange("(i p) o -> p i o", p=128)
            for i in range(n_iter):
                stg, sn = stgs[i % len(stgs)]
                b.dma(qq, stg[:, 0:width], rr[:, i, :], writes=[sn])
                b.op("vector",
                     lambda v, stg=stg, i=i: v.tensor_copy(
                         dst_sb[:, i * width:(i + 1) * width],
                         stg[:, 0:width]),
                     reads=[sn], writes=[dst_name])

        stg_h01 = [(hch[:, 0:2048], "hch0"), (hch[:, 2048:4096], "hch1")]
        stg_x1h = [(x1ch[:, 0:1024], "x1ch"), (x1ch[:, 1024:2048], "x1ch")]
        stg_rA = [(rA[0][:], "rA0"), (rA[1][:], "rA1")]
        load_cast("sync", wq_sb, "wq_sb", d_Wq, 8, 1024, stg_h01)
        load_cast("sync", wdn_sb, "wdn_sb", d_Wdn, 8, 256, stg_h01)
        load_cast("scalar", wout_sb, "wout_sb", d_Wout, 8, 1024, stg_rA)
        rru2 = d_Wup.rearrange("(i p) o -> p i o", p=128)
        for i2 in range(2):
            for hh in range(2):
                stg, sn = stg_rA[hh]
                b.dma("scalar", stg[:, 0:1024],
                      rru2[:, i2, hh * 1024:(hh + 1) * 1024], writes=[sn])
                b.op("vector",
                     lambda v, stg=stg, i2=i2, hh=hh: v.tensor_copy(
                         wup_sb[:, i2 * 2048 + hh * 1024:
                                i2 * 2048 + (hh + 1) * 1024], stg[:, 0:1024]),
                     reads=[sn], writes=["wup_sb"])

        for ac in (MX, SCR, MXR):
            b.op("vector", lambda v, ac=ac: v.memset(col(ac), 0.0),
                 writes=["cols"])

        def p1_unit(qq, src, stg, sns, tc, ac):
            b.dma(qq, stg, src, writes=sns)
            b.op("vector",
                 lambda v: v.tensor_reduce(col(tc), stg, AX, ALU.add,
                                           apply_absolute_value=True),
                 reads=sns, writes=["cols"])
            b.op("vector",
                 lambda v: v.tensor_add(col(ac), col(ac), col(tc)),
                 reads=["cols"], writes=["cols"])

        NEL = float(D * DI)
        rrg1 = d_wgT.rearrange("(l p g) c -> p l (g c)", p=128, g=2)
        rrg = d_wgT.rearrange("(rb p) c -> p rb c", p=128)
        rru = d_wuT.rearrange("(rb p) c -> p rb c", p=128)
        rrd = d_wdT.rearrange("(rb p) c -> p rb c", p=128)
        wg1 = [(rrg1[:, i // 2, (i % 2) * 4096:(i % 2) * 4096 + 4096],
                hch[:, 0:4096] if i % 2 == 0 else hch[:, 4096:8192],
                ["hch0", "hch1"] if i % 2 == 0 else ["hch2", "hch3"])
               for i in range(8)]
        wd1 = [(rrd[:, i, :],
                x1ch[:, 0:1024] if i % 2 == 0 else x1ch[:, 1024:2048],
                ["x1ch"]) for i in range(32)]
        wu1 = [(rru[:, i // 4, (i % 4) * 1024:(i % 4) * 1024 + 1024],
                tmp1[:], ["tmp1"]) for i in range(32)]
        for i in range(32):
            p1_unit("gpsimd", *wd1[i], MXC, MX)
            p1_unit("sync", *wu1[i], SXI, MXR)
            if i < 8:
                p1_unit("scalar", *wg1[i], TMPC, SCR)
            if n_chunks > 0 and phase >= 1:
                if i == 7:
                    emit_proj(0, skip_load=True)
                elif i == 15 and phase >= 2:
                    emit_attnA(0, 0, skip_xtmt=True)
                elif i == 23 and phase >= 2:
                    emit_attnA(0, 1, skip_xtmt=True)
                elif i == 27 and n_chunks > 1:
                    emit_proj(1, load_q="scalar")

        def qw_bcast(ac, acol, wc):
            b.op("tensor", lambda t: t.matmul(avd[0:1, 0:1], cONE, col(ac),
                                              start=True, stop=True),
                 reads=["consts", "cols"], writes=["avd"])
            b.op("vector",
                 lambda v: v.tensor_scalar(col(ac)[0:1, :], avd[0:1, 0:1],
                                           1.0 / NEL, 1e-5, ALU.mult,
                                           ALU.max),
                 reads=["avd"], writes=["cols"])
            b.op("tensor", lambda t: t.matmul(avd[:, 1:2], ones1[0:1, :],
                                              col(ac)[0:1, :], start=True,
                                              stop=True),
                 reads=["ones1", "cols"], writes=["avd"])
            b.op("vector",
                 lambda v: v.tensor_copy(consts[:, acol:acol + 1],
                                         avd[:, 1:2]),
                 reads=["avd"], writes=["consts"])
            b.op("vector",
                 lambda v: v.reciprocal(col(wc), consts[:, acol:acol + 1]),
                 reads=["consts"], writes=["cols"])

        qw_bcast(SCR, AG, WSC)
        qw_bcast(MXR, AU, SH127)
        qw_bcast(MX, AD, S127)

        rr5g = s_wgu.rearrange("p (jb ic k) -> p jb ic k", ic=8, k=1024)
        rr5d = s_wdq.rearrange("p (jb jc o) -> p jb jc o", jc=2, o=1024)

        def p2_unit(qq, src, stg, sns, ub, un, bnb, bnn, tern, ternn,
                    wc, oap, tap, sname):
            b.dma(qq, stg, src, writes=sns)
            b.op("scalar",
                 lambda s: s.activation(ub, stg, AF.Identity, bias=0.0,
                                        scale=col(wc)),
                 reads=sns + ["cols"], writes=[un])
            b.op("vector",
                 lambda v: v.tensor_scalar(bnb, ub, -0.5, None, ALU.is_le),
                 reads=[un], writes=[bnn])
            b.op("vector",
                 lambda v: v.scalar_tensor_tensor(tern, ub, 0.5, bnb,
                                                  ALU.is_ge, ALU.subtract),
                 reads=[un, bnn], writes=[ternn])
            b.dma(qq, oap, tap, reads=[ternn], writes=[sname])

        gu2 = []
        for cb_ in range(2):
            for rb_ in range(8):
                for rrw, koff, wc in ((rrg, 0, WSC), (rru, 512, SH127)):
                    gu2.append((rrw[:, rb_, cb_ * 2048:(cb_ + 1) * 2048],
                                rb_, cb_, koff, wc))
        for i in range(32):
            stgd, snd = stg_x1h[i % 2]
            td8 = hqTs[0][:, 0:512].bitcast(F8)
            p2_unit("gpsimd", rrd[:, i, :], stgd, [snd],
                    hch[:, 4096:5120], "hch2", hch[:, 6144:7168], "hch3",
                    td8, "hqT0", S127,
                    rr5d[:, i // 2, i % 2, :], td8, "wdq_s")
            src, rb_, cb_, koff, wc = gu2[i]
            stgg, sng = stg_h01[i % 2]
            tern = hqTs[1][:, 0:1024].bitcast(F8)
            p2_unit(["scalar", "sync"][i % 2], src, stgg, [sng],
                    hch[:, 4096:6144], "hch2", hch[:, 6144:8192], "hch3",
                    tern, "hqT1", wc,
                    rr5g[:, cb_ * 4:(cb_ + 1) * 4, rb_, koff:koff + 512],
                    tern.rearrange("p (jb j) -> p jb j", j=512),
                    "wgu_sA" if cb_ == 0 else "wgu_sB")
        b.op("vector", lambda v: v.tensor_scalar(col(AG127),
                                                 consts[:, AG:AG + 1],
                                                 1.0 / 127.0, None, ALU.mult),
             reads=["consts"], writes=["cols"])
        b.op("vector", lambda v: v.tensor_mul(col(ACOMB), consts[:, AU:AU + 1],
                                              consts[:, AD:AD + 1]),
             reads=["consts"], writes=["cols"])
        b.op("vector", lambda v: v.tensor_scalar(col(ACOMB), col(ACOMB),
                                                 1.0 / (127.0 * 127.0), None,
                                                 ALU.mult),
             reads=["cols"], writes=["cols"])

        if n_chunks > 0 and phase >= 2:
            emit_tileB(0, 0)
            emit_tileB_xpose(0)
            emit_tileB(0, 1)
            if phase < 3:
                emit_tileB_xpose(1)

        for ch in range(n_chunks):
            if ch > 0:
                if phase >= 2:
                    emit_attnA(ch, 0)
                    emit_attnA(ch, 1)
            if ch + 1 < n_chunks and phase >= 1 and ch > 0:
                emit_proj(ch + 1)
            if ch > 0 and phase >= 2:
                emit_tileB(ch, 0)
                emit_tileB_xpose(0)
                emit_tileB(ch, 1)
            if phase >= 3:
                emit_ffn(ch, after_first_w0=lambda: emit_tileB_xpose(1))
            elif phase >= 2 and ch > 0:
                emit_tileB_xpose(1)

        b.final_wait("sync", "yout")
        b.emit(block, ctx, nc, sems_op)
    return nc


_CACHE = {}


def _host_prep(inputs):
    x = np.asarray(inputs["L0_attn_x"], dtype=np.float32)
    Wq = np.ascontiguousarray(np.asarray(inputs["L0_attn_Wq"], np.float32))
    bq = np.asarray(inputs["L0_attn_bq"], np.float32)
    Wdn = np.ascontiguousarray(np.asarray(inputs["L0_attn_Wdown"], np.float32))
    bdn = np.asarray(inputs["L0_attn_bdown"], np.float32)
    Wup = np.ascontiguousarray(np.asarray(inputs["L0_attn_Wup"], np.float32))
    bup = np.asarray(inputs["L0_attn_bup"], np.float32)
    Wout = np.ascontiguousarray(np.asarray(inputs["L0_attn_Wout"], np.float32))
    bout = np.asarray(inputs["L0_attn_bout"], np.float32)
    wgT = np.ascontiguousarray(np.asarray(inputs["L0_ffn_w_gate"], np.float32).T)
    wuT = np.ascontiguousarray(np.asarray(inputs["L0_ffn_w_up"], np.float32).T)
    wdT = np.ascontiguousarray(np.asarray(inputs["L0_ffn_w_down"], np.float32).T)

    bdn_cols = np.ascontiguousarray(bdn.reshape(2, 128).T)
    bout_row = np.ascontiguousarray(bout.reshape(1, D)
                                    .astype(ml_dtypes.bfloat16))
    bq_row = np.ascontiguousarray(bq.reshape(1, D).astype(ml_dtypes.bfloat16))
    bup_row = np.ascontiguousarray(bup.reshape(1, DKV)
                                   .astype(ml_dtypes.bfloat16))

    consts = np.zeros((128, 8), np.float32)
    consts[:, 0] = C2
    consts[:, 1] = C1
    consts[:, 2] = 1.0
    consts[:, 3] = 1e-5

    m_idx = np.arange(128)
    rm = m_idx // 16
    rn64 = (np.arange(64)) // 8
    NEG = float(np.float32(ml_dtypes.bfloat16(-30000.0)))
    mU = np.zeros((128, 128), np.float32)
    mV = np.zeros((128, 512), np.float32)
    for base in (0, 64):
        mU[base, :] = 1.0
        mV[base, :] = NEG
        for r in range(8):
            mU[base + 1 + r, :] = (rm == r).astype(np.float32)
            mV[base + 1 + r, :] = np.tile((rn64 == r) * (-NEG), 8)
    mU = mU.astype(ml_dtypes.bfloat16)
    mV = mV.astype(ml_dtypes.bfloat16)
    id128 = np.eye(128, dtype=ml_dtypes.bfloat16)
    id2 = np.zeros((128, 64), dtype=ml_dtypes.bfloat16)
    id2[0:64, :] = np.eye(64)
    id2[64:128, :] = np.eye(64)
    ones_row = np.ones((1, 256), ml_dtypes.bfloat16)

    shared = {
        "Wq": Wq, "Wdn": Wdn, "Wup": Wup, "Wout": Wout,
        "bdn_cols": bdn_cols, "bout_row": bout_row,
        "bq_row": bq_row, "bup_row": bup_row,
        "wgT": wgT, "wuT": wuT, "wdT": wdT,
        "consts": consts, "mU": mU, "mV": mV,
        "id128": id128, "id2": id2,
        "ones_row": ones_row,
    }
    in_maps = []
    for bidx in range(NCORE):
        m = dict(shared)
        m["xT"] = np.ascontiguousarray(x[bidx].T)
        m["xtm"] = np.ascontiguousarray(x[bidx])
        in_maps.append(m)
    return in_maps


def kernel(**inputs):
    if "nc" not in _CACHE:
        _CACHE["nc"] = build_nc()
    nc = _CACHE["nc"]
    in_maps = _host_prep(inputs)
    res = run_bass_kernel_spmd(nc, in_maps, core_ids=list(range(NCORE)))
    out = np.stack([np.asarray(r["y"], dtype=np.float32) for r in res.results],
                   axis=0)
    return out
